# version 01
# baseline (speedup 1.0000x reference)
from contextlib import ExitStack
import numpy as np
import ml_dtypes
import concourse.bass as bass
import concourse.mybir as mybir
from concourse.bass_utils import run_bass_kernel_spmd

F32 = mybir.dt.float32
BF16 = mybir.dt.bfloat16
AF = mybir.ActivationFunctionType
ALU = mybir.AluOpType
AX = mybir.AxisListType
BF = ml_dtypes.bfloat16

C = 8
T = 8192
TL = T // C
D = 4096
NH = 32
HD = 128
QKV = 3 * D
LN_EPS = 1e-5



class SelfSem:
    def __init__(self, sem):
        self.sem = sem
        self.n = 0

    def mark(self, inst):
        inst.then_inc(self.sem, 1)
        self.n += 1
        return inst

    def wait(self, eng):
        if self.n:
            eng.wait_ge(self.sem, self.n)


def build_l1():
    nc = bass.Bass("TRN2", target_bir_lowering=False)
    x_in = nc.dram_tensor("x", [TL, D], F32, kind="ExternalInput").ap()
    w_in = nc.dram_tensor("wqkv", [D, QKV], F32, kind="ExternalInput").ap()
    b_in = nc.dram_tensor("qkvb_t", [128, 96], F32, kind="ExternalInput").ap()
    id_in = nc.dram_tensor("ident", [128, 128], BF16, kind="ExternalInput").ap()
    out = nc.dram_tensor("qkvT", [QKV, TL], BF16, kind="ExternalOutput").ap()

    NTT = TL // 128
    NDB = D // 128
    NCBG = QKV // 512
    NUNITS = NCBG * 2 * 4

    with ExitStack() as ctx:
        sb = lambda n, sh, dt: ctx.enter_context(nc.sbuf_tensor(n, sh, dt))
        ps = lambda n, sh, dt: ctx.enter_context(nc.psum_tensor(n, sh, dt))
        sem = lambda n: ctx.enter_context(nc.semaphore(n))
        xt = sb("xt", [128, 2, D], F32)
        sq = sb("sq", [128, D], F32)
        stt = sb("st", [128, 2, 8], F32)
        xln = sb("xln", [128, 2, D], BF16)
        xlnT = sb("xlnT", [128, NDB, TL], BF16)
        idb = sb("idb", [128, 128], BF16)
        bq = sb("bq", [128, 96], F32)
        wst = sb("wst", [128, 2, NDB, 512], BF16)
        stg = sb("stg", [128, 2, 4, 512], BF16)
        cst = sb("cst", [128, 1], F32)
        ptr = [ps(f"ptr{i}", [128, 1024], BF16) for i in range(2)]
        pm = [ps(f"pm{i}", [128, 512], F32) for i in range(6)]
        ldc = sem("ldc")
        ldp = [sem("ld0"), sem("ld1")]
        wlp = [sem("wl0"), sem("wl1")]
        odp = [sem("od0"), sem("od1")]
        cpS = sem("cpS"); cpV1 = sem("cpV1")
        ssS = SelfSem(sem("ssS")); ssV = SelfSem(sem("ssV"))
        cpV = sem("cpV"); trS = sem("trS"); evT = sem("evT"); mmS = sem("mmS")
        evS = sem("evS")
        block = ctx.enter_context(nc.Block())

        @block.sync
        def _(s):
            s.dma_start(out=idb[:], in_=id_in).then_inc(ldc, 16)
            s.dma_start(out=bq[:], in_=b_in).then_inc(ldc, 16)
            for tt in range(NTT):
                if tt >= 2:
                    s.wait_ge(cpV, tt - 1)
                s.dma_start(out=xt[:, tt % 2, :], in_=x_in[tt * 128:(tt + 1) * 128, :]
                            ).then_inc(ldp[tt % 2], 16)
            s.wait_ge(odp[0], 16 * 4 * (NUNITS // 8))
            s.wait_ge(odp[1], 16 * 4 * (NUNITS // 8))

        @block.scalar
        def _(s):
            for tt in range(NTT):
                s.wait_ge(ldp[tt % 2], 16 * (tt // 2 + 1))
                if tt >= 2:
                    s.wait_ge(cpV, tt - 1)
                b = stt[:, tt % 2, :]
                ssS.mark(s.activation(sq[:], xt[:, tt % 2, :], AF.Square,
                                      accum_out=b[:, 1:2]))
                ssS.wait(s)
                s.wait_ge(cpV1, tt + 1)
                ssS.mark(s.mul(b[:, 2:3], b[:, 0:1], 1.0 / D))
                ssS.mark(s.mul(b[:, 3:4], b[:, 1:2], 1.0 / D))
                ssS.wait(s)
                ssS.mark(s.activation(b[:, 4:5], b[:, 2:3], AF.Square))
                ssS.wait(s)
                ssS.mark(s.mul(b[:, 4:5], b[:, 4:5], -1.0))
                ssS.wait(s)
                ssS.mark(s.add(b[:, 4:5], b[:, 3:4], b[:, 4:5]))
                ssS.wait(s)
                ssS.mark(s.activation(b[:, 5:6], b[:, 4:5], AF.Ln, bias=cst[:, 0:1], scale=1.0))
                ssS.wait(s)
                s.activation(b[:, 6:7], b[:, 5:6], AF.Exp, bias=0.0, scale=-0.5
                             ).then_inc(cpS, 1)
                if tt >= 1:
                    for k in range(4):
                        pk = (tt - 1) * 4 + k
                        s.wait_ge(trS, pk + 1)
                        s.copy(xlnT[:, k * 8:(k + 1) * 8,
                                    (tt - 1) * 128: tt * 128],
                               ptr[pk % 2][:].rearrange("p (a b) -> p a b", a=8)
                               ).then_inc(evT, 1)
            for k in range(4):
                pk = (NTT - 1) * 4 + k
                s.wait_ge(trS, pk + 1)
                s.copy(xlnT[:, k * 8:(k + 1) * 8, (NTT - 1) * 128: NTT * 128],
                       ptr[pk % 2][:].rearrange("p (a b) -> p a b", a=8)
                       ).then_inc(evT, 1)
            s.wait_ge(ldc, 32)
            for u in range(NUNITS):
                cbg, r = divmod(u, 8)
                tch, cb = divmod(r, 4)
                s.wait_ge(mmS, u + 1)
                if u >= 2 * 4:
                    s.wait_ge(odp[(u // 4) % 2], 64 * ((u // 4) // 2))
                s.activation(stg[:, (u // 4) % 2, cb, :], pm[u % 6][:], AF.Identity,
                             bias=bq[:, cbg * 4 + cb:cbg * 4 + cb + 1], scale=1.0
                             ).then_inc(evS, 1)

        @block.vector
        def _(v):
            v.memset(cst[:], LN_EPS)
            for tt in range(NTT):
                v.wait_ge(ldp[tt % 2], 16 * (tt // 2 + 1))
                b = stt[:, tt % 2, :]
                v.reduce_sum(b[:, 0:1], xt[:, tt % 2, :], AX.X).then_inc(cpV1, 1)
                v.wait_ge(cpS, tt + 1)
                if tt >= 2:
                    v.wait_ge(trS, (tt - 1) * 4)
                ssV.mark(v.tensor_tensor(b[:, 7:8], b[:, 2:3], b[:, 6:7], ALU.mult))
                ssV.wait(v)
                v.tensor_scalar(xln[:, tt % 2, :], xt[:, tt % 2, :],
                                b[:, 6:7], b[:, 7:8], ALU.mult, ALU.subtract
                                ).then_inc(cpV, 1)

        @block.tensor
        def _(t):
            t.wait_ge(ldc, 32)
            for tt in range(NTT):
                t.wait_ge(cpV, tt + 1)
                for k in range(4):
                    pk = tt * 4 + k
                    if pk >= 2:
                        t.wait_ge(evT, pk - 1)
                    for j in range(8):
                        db = k * 8 + j
                        mm = t.transpose(
                            ptr[pk % 2][:, j * 128:(j + 1) * 128],
                            xln[:, tt % 2, db * 128:(db + 1) * 128], idb[:])
                    mm.then_inc(trS, 1)
            t.wait_ge(evT, NTT * 4)
            for cbg in range(NCBG):
                t.wait_ge(wlp[cbg % 2], 64 * (cbg // 2 + 1))
                for tch in range(2):
                    for cb in range(4):
                        u = cbg * 8 + tch * 4 + cb
                        if u >= 6:
                            t.wait_ge(evS, u - 5)
                        for db in range(NDB):
                            mm = t.matmul(
                                pm[u % 6][:],
                                wst[:, cbg % 2, db, cb * 128:(cb + 1) * 128],
                                xlnT[:, db, tch * 512:(tch + 1) * 512],
                                start=(db == 0), stop=(db == NDB - 1))
                        mm.then_inc(mmS, 1)

        @block.gpsimd
        def _(g):
            def outdmas(cbg2):
                for r in range(8):
                    u = cbg2 * 8 + r
                    tch, cb = divmod(r, 4)
                    g.wait_ge(evS, u + 1)
                    g.dma_start(
                        out=out[cbg2 * 512 + cb * 128: cbg2 * 512 + (cb + 1) * 128,
                                tch * 512:(tch + 1) * 512],
                        in_=stg[:, (u // 4) % 2, cb, :]).then_inc(odp[(u // 4) % 2], 16)
            for cbg in range(NCBG):
                if cbg >= 2:
                    g.wait_ge(mmS, (cbg - 1) * 8)
                src = w_in[:, cbg * 512:(cbg + 1) * 512].rearrange(
                    "(db p) c -> p db c", p=128)
                for q in range(4):
                    g.dma_start(out=wst[:, cbg % 2, q * 8:(q + 1) * 8, :],
                                in_=src[:, q * 8:(q + 1) * 8, :]).then_inc(wlp[cbg % 2], 16)
                if cbg >= 1:
                    outdmas(cbg - 1)
            outdmas(NCBG - 1)
    return nc


def host_l1_inputs(inputs):
    x = np.ascontiguousarray(inputs["L0_attn_x"].reshape(T, D))
    w = inputs["L0_attn_attn_qkvw"]
    bt = np.ascontiguousarray(inputs["L0_attn_attn_qkvb"].reshape(96, 128).T)
    ident = np.eye(128, dtype=BF)
    return [
        {"x": np.ascontiguousarray(x[c * TL:(c + 1) * TL]),
         "wqkv": w, "qkvb_t": bt, "ident": ident}
        for c in range(C)
    ]


def build_l2():
    nc = bass.Bass("TRN2", target_bir_lowering=False)
    qT_in = nc.dram_tensor("qT", [512, T], BF16, kind="ExternalInput").ap()
    kT_in = nc.dram_tensor("kT", [512, T], BF16, kind="ExternalInput").ap()
    vh_in = nc.dram_tensor("vh", [4, T, HD], BF16, kind="ExternalInput").ap()
    idf_in = nc.dram_tensor("identF", [128, 128], F32, kind="ExternalInput").ap()
    out = nc.dram_tensor("ctxq", [4, T, HD], BF16, kind="ExternalOutput").ap()

    SCALE = float(1.0 / np.sqrt(HD))
    NQC = T // 512
    NKC = T // 128
    NG = NKC // 2

    with ExitStack() as ctx:
        sb = lambda n, sh, dt: ctx.enter_context(nc.sbuf_tensor(n, sh, dt))
        ps = lambda n, sh, dt: ctx.enter_context(nc.psum_tensor(n, sh, dt))
        sem = lambda n: ctx.enter_context(nc.semaphore(n))
        kt = sb("kt", [128, 2, T], BF16)
        qt = sb("qt", [128, 2, T], BF16)
        vt = sb("vt", [128, 2, NKC, HD], BF16)
        idf = sb("idf", [128, 128], F32)
        onr = sb("onr", [1, 128], BF16)
        onc = sb("onc", [128, 1], BF16)
        pbt = sb("pbt", [128, 2, 1024], BF16)
        mraw = sb("mraw", [1, 2], F32)
        mbf = sb("mbf", [1, 1], BF16)
        bias = sb("bias", [128, 1], F32)
        ctmp = sb("ctmp", [128, 512], F32)
        rs = sb("rs", [1, 512], F32)
        rcs = sb("rcs", [128, 4], F32)
        stg = sb("stg", [128, 2, 4, 128], BF16)
        psc = ps("psc", [128, 1024], F32)
        pcx = ps("pcx", [128, 512], F32)
        psm = ps("psm", [1, 512], F32)
        pbb = ps("pbb", [128, 1], F32)
        ptr = ps("ptr", [128, 512], F32)
        prc = ps("prc", [128, 4], F32)
        ldc = sem("ldc")
        ldh = [sem("ldh0"), sem("ldh1")]
        odp = [sem("od0"), sem("od1")]
        mmsc = sem("mmsc"); expS = sem("expS"); pvS = sem("pvS")
        smpT = sem("smpT"); smpV = sem("smpV"); bcP = sem("bcP"); bcV = sem("bcV")
        evC = sem("evC"); trC = sem("trC"); nrmV = sem("nrmV"); cmpH = sem("cmpH")
        ssV = SelfSem(sem("ssV"))
        block = ctx.enter_context(nc.Block())

        NQ_UNITS = 4 * NQC

        @block.sync
        def _(s):
            s.dma_start(out=idf[:], in_=idf_in).then_inc(ldc, 16)
            for j in range(4):
                if j >= 2:
                    s.wait_ge(cmpH, j - 1)
                s.dma_start(out=kt[:, j % 2, :], in_=kT_in[j * 128:(j + 1) * 128, :]
                            ).then_inc(ldh[j % 2], 16)
                s.dma_start(out=qt[:, j % 2, :], in_=qT_in[j * 128:(j + 1) * 128, :]
                            ).then_inc(ldh[j % 2], 16)
                s.dma_start(out=vt[:, j % 2, :, :],
                            in_=vh_in[j].rearrange("(kc p) c -> p kc c", p=128)
                            ).then_inc(ldh[j % 2], 16)
            s.wait_ge(odp[0], 16 * (NQ_UNITS // 2))
            s.wait_ge(odp[1], 16 * (NQ_UNITS // 2))

        @block.tensor
        def _(t):
            t.wait_ge(ldc, 16)
            nsc = 0
            npv = 0
            for j in range(4):
                t.wait_ge(ldh[j % 2], 48 * (j // 2 + 1))
                t.matmul(psc[:, 0:512], kt[:, j % 2, 0:128], qt[:, j % 2, 0:512],
                         start=True, stop=True).then_inc(smpT, 1)
                t.wait_ge(smpV, j + 1)
                t.matmul(pbb[:], onr[:], mbf[:], start=True, stop=True
                         ).then_inc(bcP, 1)
                t.wait_ge(bcV, j + 1)
                for qc in range(NQC):
                    u = j * NQC + qc
                    if u >= 1:
                        t.wait_ge(evC, u)
                        t.wait_ge(nrmV, 2 * u - 1)
                    for g in range(NG + 1):
                        if g < NG:
                            if not (j == 0 and qc == 0 and g == 0):
                                t.wait_ge(expS, nsc)
                            for i in range(2):
                                kc = g * 2 + i
                                mm = t.matmul(psc[:, i * 512:(i + 1) * 512],
                                              kt[:, j % 2, kc * 128:(kc + 1) * 128],
                                              qt[:, j % 2, qc * 512:(qc + 1) * 512],
                                              start=True, stop=True)
                            mm.then_inc(mmsc, 1)
                            nsc += 1
                        if g >= 1:
                            gg = g - 1
                            t.wait_ge(expS, npv + 1)
                            for i in range(2):
                                kc = gg * 2 + i
                                t.matmul(pcx[:], vt[:, j % 2, kc, :],
                                         pbt[:, gg % 2, i * 512:(i + 1) * 512],
                                         start=(kc == 0), stop=(kc == NKC - 1),
                                         skip_group_check=True)
                                mm = t.matmul(psm[:], onc[:],
                                              pbt[:, gg % 2, i * 512:(i + 1) * 512],
                                              start=(kc == 0), stop=(kc == NKC - 1),
                                              skip_group_check=True)
                            mm.then_inc(pvS, 1)
                            npv += 1
                    t.wait_ge(evC, u + 1)
                    if u >= 1:
                        t.wait_ge(nrmV, 2 * u)
                    for i in range(4):
                        mm = t.transpose(ptr[:, i * 128:(i + 1) * 128],
                                         ctmp[:, i * 128:(i + 1) * 128], idf[:])
                        if qc == NQC - 1 and i == 0:
                            mm.then_inc(cmpH, 1)
                    t.wait_ge(nrmV, 2 * u + 1)
                    for i in range(4):
                        mm = t.transpose(prc[:, i:i + 1], rs[:, i * 128:(i + 1) * 128],
                                         idf[0:1, 0:1])
                    mm.then_inc(trC, 1)

        @block.scalar
        def _(s):
            for j in range(4):
                for qc in range(NQC):
                    u = j * NQC + qc
                    for g in range(NG):
                        s.wait_ge(mmsc, u * NG + g + 1)
                        if u * NG + g >= 2:
                            s.wait_ge(pvS, u * NG + g - 1)
                        s.activation(pbt[:, g % 2, :], psc[:], AF.Exp,
                                     bias=bias[:], scale=SCALE).then_inc(expS, 1)
                    s.wait_ge(pvS, (u + 1) * NG)
                    if u >= 1:
                        s.wait_ge(trC, u)
                    s.copy(ctmp[:], pcx[:]).then_inc(evC, 1)

        @block.vector
        def _(v):
            v.memset(onr[:], 1.0)
            v.memset(onc[:], 1.0)
            for j in range(4):
                v.wait_ge(smpT, j + 1)
                ssV.mark(v.reduce_max(mraw[:, 0:1], psc[0:1, 0:512], AX.X))
                ssV.wait(v)
                ssV.mark(v.tensor_copy(mbf[:], mraw[:, 0:1]))
                ssV.wait(v)
                v.sem_inc(smpV, 1)
                v.wait_ge(bcP, j + 1)
                v.tensor_scalar(bias[:], pbb[:], -SCALE, None, ALU.mult
                                ).then_inc(bcV, 1)
                for qc in range(NQC):
                    u = j * NQC + qc
                    v.wait_ge(pvS, (u + 1) * NG)
                    if u >= 1:
                        v.wait_ge(trC, u)
                    ssV.mark(v.reciprocal(rs[:], psm[:]))
                    ssV.wait(v)
                    v.sem_inc(nrmV, 1)
                    v.wait_ge(trC, u + 1)
                    ssV.mark(v.tensor_copy(rcs[:, 0:4], prc[:]))
                    ssV.wait(v)
                    if u >= 2:
                        v.wait_ge(odp[u % 2], 16 * (u // 2))
                    for i in range(4):
                        vv = v.tensor_scalar(stg[:, u % 2, i, :],
                                             ptr[:, i * 128:(i + 1) * 128],
                                             rcs[:, i:i + 1], None, ALU.mult)
                    vv.then_inc(nrmV, 1)

        @block.gpsimd
        def _(g):
            for j in range(4):
                for qc in range(NQC):
                    u = j * NQC + qc
                    g.wait_ge(nrmV, 2 * (u + 1))
                    g.dma_start(
                        out=out[j, qc * 512:(qc + 1) * 512, :].rearrange(
                            "(sp p) c -> p sp c", p=128),
                        in_=stg[:, u % 2, :, :]).then_inc(odp[u % 2], 16)
    return nc


def host_l2_inputs(l1_outs):
    qkvT = np.concatenate(l1_outs, axis=1)
    qT, kT, vT = qkvT[0:D], qkvT[D:2 * D], qkvT[2 * D:3 * D]
    identF = np.eye(128, dtype=np.float32)
    maps = []
    for c in range(C):
        rows = slice(c * 512, (c + 1) * 512)
        vh = np.stack([np.ascontiguousarray(vT[c * 512 + j * 128: c * 512 + (j + 1) * 128, :].T)
                       for j in range(4)])
        maps.append({"qT": np.ascontiguousarray(qT[rows]),
                     "kT": np.ascontiguousarray(kT[rows]),
                     "vh": vh, "identF": identF})
    return maps


def build_l3a():
    nc = bass.Bass("TRN2", target_bir_lowering=False)
    ctxT_in = nc.dram_tensor("ctxT", [D, TL], BF16, kind="ExternalInput").ap()
    ow_in = nc.dram_tensor("ow", [D, D], F32, kind="ExternalInput").ap()
    ob_in = nc.dram_tensor("ob_t", [128, 32], F32, kind="ExternalInput").ap()
    x_in = nc.dram_tensor("x", [TL, D], F32, kind="ExternalInput").ap()
    idf_in = nc.dram_tensor("identF", [128, 128], F32, kind="ExternalInput").ap()
    x1_out = nc.dram_tensor("x1", [TL, D], F32, kind="ExternalOutput").ap()
    ln1_out = nc.dram_tensor("xln1", [TL, D], F32, kind="ExternalOutput").ap()

    NDB = D // 128
    NCBG = D // 512

    with ExitStack() as ctx:
        sb = lambda n, sh, dt: ctx.enter_context(nc.sbuf_tensor(n, sh, dt))
        ps = lambda n, sh, dt: ctx.enter_context(nc.psum_tensor(n, sh, dt))
        sem = lambda n: ctx.enter_context(nc.semaphore(n))
        x1b = sb("x1b", [128, 4, D], F32)
        ctxb = sb("ctxb", [128, NDB, 512], BF16)
        wst = sb("wst", [128, 2, NDB, 512], BF16)
        aotmp = sb("aotmp", [128, 512], F32)
        obt = sb("obt", [128, 32], F32)
        idf = sb("idf", [128, 128], F32)
        sq = sb("sq", [128, D], F32)
        stt = sb("stt", [128, 2, 8], F32)
        cst = sb("cst", [128, 1], F32)
        ln1t = sb("ln1t", [128, 1, D], F32)
        pmm = [ps(f"pmm{i}", [128, 512], F32) for i in range(4)]
        ptr = [ps(f"ptr{i}", [128, 512], F32) for i in range(2)]
        ldc = sem("ldc")
        ldx = [sem("ldx0"), sem("ldx1")]
        ldcx = [sem("ldcx0"), sem("ldcx1")]
        wlp = [sem("wl0"), sem("wl1")]
        odx = [sem("odx0"), sem("odx1")]
        odl = [sem("odl0"), sem("odl1")]
        mmS = sem("mmS"); evA = sem("evA"); trS = sem("trS"); adV = sem("adV")
        cpS = sem("cpS"); cpV1 = sem("cpV1"); cpV = sem("cpV")
        ssS = SelfSem(sem("ssS")); ssV = SelfSem(sem("ssV"))
        block = ctx.enter_context(nc.Block())


        @block.sync
        def _(s):
            s.dma_start(out=idf[:], in_=idf_in).then_inc(ldc, 16)
            s.dma_start(out=obt[:], in_=ob_in).then_inc(ldc, 16)
            for h in range(2):
                if h >= 1:
                    s.wait_ge(cpV, 4 * h)
                    s.wait_ge(odx[h - 1], 64)
                for tt in range(4):
                    s.dma_start(out=x1b[:, tt, :],
                                in_=x_in[(h * 4 + tt) * 128:(h * 4 + tt + 1) * 128, :]
                                ).then_inc(ldx[h], 16)
                s.dma_start(out=ctxb[:],
                            in_=ctxT_in[:, h * 512:(h + 1) * 512].rearrange(
                                "(db p) t -> p db t", p=128)).then_inc(ldcx[h], 16)
            s.wait_ge(odx[0], 64); s.wait_ge(odx[1], 64)
            s.wait_ge(odl[0], 128)

        @block.gpsimd
        def _(g):
            for h in range(2):
                for cbg in range(NCBG):
                    k = h * NCBG + cbg
                    if k >= 2:
                        g.wait_ge(mmS, (k - 1) * 4)
                    src = ow_in[:, cbg * 512:(cbg + 1) * 512].rearrange(
                        "(db p) c -> p db c", p=128)
                    for q in range(4):
                        g.dma_start(out=wst[:, k % 2, q * 8:(q + 1) * 8, :],
                                    in_=src[:, q * 8:(q + 1) * 8, :]
                                    ).then_inc(wlp[k % 2], 16)
                g.wait_ge(adV, 128 * (h + 1))
                for tt in range(4):
                    gt = h * 4 + tt
                    g.dma_start(out=x1_out[gt * 128:(gt + 1) * 128, :],
                                in_=x1b[:, tt, :]).then_inc(odx[h], 16)
                    g.wait_ge(cpV, gt + 1)
                    g.dma_start(out=ln1_out[gt * 128:(gt + 1) * 128, :],
                                in_=ln1t[:, 0, :]).then_inc(odl[0], 16)

        @block.tensor
        def _(t):
            t.wait_ge(ldc, 32)
            for h in range(2):
                t.wait_ge(ldcx[h], 16)
                for cbg in range(NCBG):
                    k = h * NCBG + cbg
                    t.wait_ge(wlp[k % 2], 64 * (k // 2 + 1))
                    for cb in range(4):
                        u2 = cbg * 4 + cb
                        u = h * 32 + u2
                        if u >= 4:
                            t.wait_ge(evA, u - 3)
                        for db in range(NDB):
                            mm = t.matmul(pmm[u % 4][:],
                                          wst[:, k % 2, db, cb * 128:(cb + 1) * 128],
                                          ctxb[:, db, :],
                                          start=(db == 0), stop=(db == NDB - 1))
                        mm.then_inc(mmS, 1)
                        t.wait_ge(evA, u + 1)
                        if u >= 2:
                            t.wait_ge(adV, 4 * (u - 1))
                        for i in range(4):
                            mm = t.transpose(ptr[u % 2][:, i * 128:(i + 1) * 128],
                                             aotmp[:, i * 128:(i + 1) * 128], idf[:])
                        mm.then_inc(trS, 1)

        @block.scalar
        def _(s):
            for h in range(2):
                for u2 in range(32):
                    u = h * 32 + u2
                    cbg, cb = divmod(u2, 4)
                    s.wait_ge(mmS, u + 1)
                    if u >= 1:
                        s.wait_ge(trS, u)
                    s.activation(aotmp[:], pmm[u % 4][:], AF.Identity,
                                 bias=obt[:, cbg * 4 + cb:cbg * 4 + cb + 1],
                                 scale=1.0).then_inc(evA, 1)
                for tt in range(4):
                    gt = h * 4 + tt
                    s.wait_ge(adV, 128 * (h + 1))
                    s.wait_ge(cpV1, gt + 1)
                    b = stt[:, gt % 2, :]
                    ssS.mark(s.activation(sq[:], x1b[:, tt, :], AF.Square,
                                          accum_out=b[:, 1:2]))
                    ssS.wait(s)
                    ssS.mark(s.mul(b[:, 2:3], b[:, 0:1], 1.0 / D))
                    ssS.mark(s.mul(b[:, 3:4], b[:, 1:2], 1.0 / D))
                    ssS.wait(s)
                    ssS.mark(s.activation(b[:, 4:5], b[:, 2:3], AF.Square))
                    ssS.wait(s)
                    ssS.mark(s.mul(b[:, 4:5], b[:, 4:5], -1.0))
                    ssS.wait(s)
                    ssS.mark(s.add(b[:, 4:5], b[:, 3:4], b[:, 4:5]))
                    ssS.wait(s)
                    ssS.mark(s.activation(b[:, 5:6], b[:, 4:5], AF.Ln,
                                          bias=cst[:, 0:1], scale=1.0))
                    ssS.wait(s)
                    s.activation(b[:, 6:7], b[:, 5:6], AF.Exp, bias=0.0, scale=-0.5
                                 ).then_inc(cpS, 1)

        @block.vector
        def _(v):
            v.memset(cst[:], LN_EPS)
            for h in range(2):
                for u2 in range(32):
                    u = h * 32 + u2
                    cbg, cb = divmod(u2, 4)
                    v.wait_ge(trS, u + 1)
                    v.wait_ge(ldx[h], 64)
                    oc = cbg * 512 + cb * 128
                    for i in range(4):
                        vv = v.tensor_tensor(x1b[:, i, oc:oc + 128],
                                             x1b[:, i, oc:oc + 128],
                                             ptr[u % 2][:, i * 128:(i + 1) * 128],
                                             ALU.add)
                    vv.then_inc(adV, 4)
                for tt in range(4):
                    gt = h * 4 + tt
                    v.wait_ge(adV, 128 * (h + 1))
                    b = stt[:, gt % 2, :]
                    ssV.mark(v.reduce_sum(b[:, 0:1], x1b[:, tt, :], AX.X))
                    ssV.wait(v)
                    v.sem_inc(cpV1, 1)
                    v.wait_ge(cpS, gt + 1)
                    if gt >= 1:
                        v.wait_ge(odl[0], 16 * gt)
                    ssV.mark(v.tensor_tensor(b[:, 7:8], b[:, 2:3], b[:, 6:7], ALU.mult))
                    ssV.wait(v)
                    v.tensor_scalar(ln1t[:, 0, :], x1b[:, tt, :],
                                    b[:, 6:7], b[:, 7:8], ALU.mult, ALU.subtract
                                    ).then_inc(cpV, 1)

    return nc


def host_l3a_inputs(inputs, l2_ctx):
    x = np.ascontiguousarray(inputs["L0_attn_x"].reshape(T, D))
    ow = inputs["L0_attn_attn_ow"]
    obt = np.ascontiguousarray(inputs["L0_attn_attn_ob"].reshape(32, 128).T)
    identF = np.eye(128, dtype=np.float32)
    ctxT = np.ascontiguousarray(l2_ctx.T)
    return [
        {"ctxT": np.ascontiguousarray(ctxT[:, c * TL:(c + 1) * TL]),
         "ow": ow, "ob_t": obt,
         "x": np.ascontiguousarray(x[c * TL:(c + 1) * TL]),
         "identF": identF}
        for c in range(C)
    ]

SINK_ITERS = 50
SINK_TAU = 0.1
RMS_EPS0 = 1.1920929e-07


def build_l3b():
    nc = bass.Bass("TRN2", target_bir_lowering=False)
    x1_in = nc.dram_tensor("x1", [TL, D], F32, kind="ExternalInput").ap()
    ln1_in = nc.dram_tensor("xln1", [TL, D], F32, kind="ExternalInput").ap()
    wres_in = nc.dram_tensor("wres_t", [D, 24], F32, kind="ExternalInput").ap()
    b24_in = nc.dram_tensor("base24", [24, 1], F32, kind="ExternalInput").ap()
    gate_in = nc.dram_tensor("gate", [1, 1], F32, kind="ExternalInput").ap()
    wbr_in = nc.dram_tensor("wbr", [1024, 1024], F32, kind="ExternalInput").ap()
    idb_in = nc.dram_tensor("ident", [128, 128], BF16, kind="ExternalInput").ap()
    idf_in = nc.dram_tensor("identF", [128, 128], F32, kind="ExternalInput").ap()
    xb_out = nc.dram_tensor("xb", [TL, D], F32, kind="ExternalOutput").ap()
    ln2T_out = nc.dram_tensor("xln2T", [D, TL], BF16, kind="ExternalOutput").ap()

    NDB = D // 128
    NTT = 8
    G = NTT

    sink_steps = []

    def VS(eng):
        def deco(fn):
            sink_steps.append((eng, fn))
            return fn
        return deco

    with ExitStack() as ctx:
        sb = lambda n, sh, dt: ctx.enter_context(nc.sbuf_tensor(n, sh, dt))
        ps = lambda n, sh, dt: ctx.enter_context(nc.psum_tensor(n, sh, dt))
        sem = lambda n: ctx.enter_context(nc.semaphore(n))
        xnT = sb("xnT", [128, NDB, TL], BF16)
        xl1 = sb("xl1", [128, D], F32)
        x1t = sb("x1t", [128, D], F32)
        xbt = sb("xbt", [128, D], F32)
        sq = sb("sq", [128, D], F32)
        tokbf = sb("tokbf", [128, D], BF16)
        stt = sb("stt", [128, 2, 8], F32)
        cst = sb("cst", [128, 2], F32)
        wresb = sb("wresb", [128, NDB, 24], BF16)
        b24 = sb("b24", [24, 1], F32)
        gsb = sb("gsb", [128, 2], F32)
        lsb = sb("lsb", [24, TL], F32)
        ltok = sb("ltok", [128, G, 24], F32)
        Zp = sb("Zp", [128, G, 4, 4], F32)
        t2b = sb("t2b", [128, G, 4, 4], F32)
        escr = sb("escr", [128, G, 4, 4], F32)
        ssb = sb("ssb", [128, G, 4], F32)
        lub = sb("lub", [128, G, 4], F32)
        lvb = sb("lvb", [128, G, 4], F32)
        Mb = sb("Mb", [128, G], F32)
        rcb = sb("rcb", [128, G], F32)
        Hres = sb("Hres", [128, G, 4, 4], F32)
        Hpre = sb("Hpre", [128, G, 4], F32)
        Hpost = sb("Hpost", [128, G, 4], F32)
        sftm = sb("sftm", [128, G, 4], F32)
        xpre = sb("xpre", [128, 2, 1024], F32)
        wbrb = sb("wbrb", [128, 8, 1024], BF16)
        xpreT = sb("xpreT", [128, 8, TL], BF16)
        ysb = sb("ysb", [128, 8, 512], F32)
        idb = sb("idb", [128, 128], BF16)
        idf = sb("idf", [128, 128], F32)
        ptrx = [ps(f"ptrx{i}", [128, 1024], BF16) for i in range(2)]
        plog = ps("plog", [24, 512], F32)
        ptr24 = ps("ptr24", [128, 24], F32)
        pwb = ps("pwb", [128, 512], F32)
        ptry = ps("ptry", [128, 1024], F32)
        ldc = sem("ldc"); ldg = sem("ldg")
        lxd = sem("lxd"); l1d = sem("l1d")
        odb = sem("odb"); odz = sem("odz")
        xf = sem("xf"); x1f = sem("x1f")
        cpS = sem("cpS"); cpV1 = sem("cpV1"); cpV2 = sem("cpV2"); cstR = sem("cstR")
        trX = sem("trX"); evX = sem("evX")
        mmL = sem("mmL"); evL = sem("evL"); trL = sem("trL"); ltV = sem("ltV")
        snkS = sem("snkS"); snkV = sem("snkV")
        xpV = sem("xpV"); trP = sem("trP"); evP = sem("evP")
        mmW = sem("mmW"); evW = sem("evW"); trY = sem("trY"); yuV = sem("yuV")
        cpB = sem("cpB"); trZ = sem("trZ"); evZ = sem("evZ")
        ssS = SelfSem(sem("ssS")); ssV = SelfSem(sem("ssV"))
        block = ctx.enter_context(nc.Block())

        bc44_j = lambda a: a.rearrange("p g j -> p g () j").broadcast_to((128, G, 4, 4))
        bc44_i = lambda a: a.rearrange("p g i -> p g i ()").broadcast_to((128, G, 4, 4))
        bc4 = lambda a: a.rearrange("p g -> p g ()").broadcast_to((128, G, 4))

        @VS('V')
        def _(v):
            ssV.mark(v.reduce_max(Mb[:], ltok[:, :, 0:16], AX.X))
            ssV.wait(v)
            return v.tensor_tensor(t2b[:], ltok[:, :, 0:16].rearrange(
                "p g (i j) -> p g i j", i=4),
                Mb[:].rearrange("p g -> p g ()").rearrange("p g o -> p g o ()").broadcast_to((128, G, 4, 4)),
                ALU.subtract)

        @VS('V')
        def _(v):
            return v.tensor_scalar(Zp[:], t2b[:], 1.0 / SINK_TAU, None, ALU.mult)

        for _it in range(SINK_ITERS):
            if _it == 0:
                @VS('S')
                def _(s):
                    return s.activation(escr[:], Zp[:], AF.Exp)
            else:
                @VS('V')
                def _(v):
                    return v.tensor_tensor(t2b[:], Zp[:], bc44_j(lvb[:]), ALU.subtract)

                @VS('S')
                def _(s):
                    return s.activation(escr[:], t2b[:], AF.Exp)

            @VS('V')
            def _(v):
                return v.reduce_sum(ssb[:], escr[:], AX.X)

            @VS('S')
            def _(s):
                return s.activation(lub[:], ssb[:], AF.Ln)

            @VS('V')
            def _(v):
                return v.tensor_tensor(t2b[:], Zp[:], bc44_i(lub[:]), ALU.subtract)

            @VS('S')
            def _(s):
                return s.activation(escr[:], t2b[:], AF.Exp)

            @VS('V')
            def _(v):
                return v.reduce_sum(ssb[:], escr[:].rearrange("p g i j -> p g j i"),
                                    AX.X)

            @VS('S')
            def _(s):
                return s.activation(lvb[:], ssb[:], AF.Ln)

        @VS('V')
        def _(v):
            return v.tensor_tensor(t2b[:], t2b[:], bc44_j(lvb[:]), ALU.subtract)

        @VS('S')
        def _(s):
            return s.activation(Hres[:], t2b[:], AF.Exp)

        @VS('V')
        def _(v):
            ssV.mark(v.tensor_scalar(Hres[:], Hres[:], gsb[:, 0:1], None, ALU.mult))
            ssV.wait(v)
            d16 = Hres[:].rearrange("p g i j -> p g (i j)")
            return v.tensor_scalar(d16[:, :, 0:16:5], d16[:, :, 0:16:5],
                                   gsb[:, 1:2], None, ALU.add)

        for _lo, _H in ((16, Hpre), (20, Hpost)):
            def _mk(lo, H):
                @VS('V')
                def _(v):
                    ssV.mark(v.reduce_max(Mb[:], ltok[:, :, lo:lo + 4], AX.X))
                    ssV.wait(v)
                    return v.tensor_tensor(sftm[:], ltok[:, :, lo:lo + 4],
                                           bc4(Mb[:]), ALU.subtract)

                @VS('S')
                def _(s):
                    return s.activation(escr[:, :, 0, :], sftm[:], AF.Exp)

                @VS('V')
                def _(v):
                    ssV.mark(v.reduce_sum(Mb[:], escr[:, :, 0, :], AX.X))
                    ssV.wait(v)
                    ssV.mark(v.reciprocal(rcb[:], Mb[:]))
                    ssV.wait(v)
                    return v.tensor_tensor(H[:], escr[:, :, 0, :], bc4(rcb[:]),
                                           ALU.mult)
            _mk(_lo, _H)

        n_sinkV = sum(1 for e, _ in sink_steps if e == 'V')
        n_sinkS = sum(1 for e, _ in sink_steps if e == 'S')

        @block.sync
        def _(s):
            s.dma_start(out=idb[:], in_=idb_in).then_inc(ldc, 16)
            s.dma_start(out=idf[:], in_=idf_in).then_inc(ldc, 16)
            s.dma_start(out=b24[:], in_=b24_in).then_inc(ldc, 16)
            s.dma_start(out=gsb[:, 0:1], in_=gate_in.broadcast_to((128, 1))
                        ).then_inc(ldc, 16)
            for i in range(16):
                gt = i % 8
                if i >= 1:
                    s.wait_ge(xf, i)
                s.dma_start(out=xl1[:], in_=ln1_in[gt * 128:(gt + 1) * 128, :]
                            ).then_inc(lxd, 16)
            for gt in range(8):
                if gt >= 1:
                    s.wait_ge(x1f, gt)
                s.dma_start(out=x1t[:], in_=x1_in[gt * 128:(gt + 1) * 128, :]
                            ).then_inc(l1d, 16)
                s.wait_ge(xf, 16 + gt)
                s.dma_start(out=xl1[:], in_=ln1_in[gt * 128:(gt + 1) * 128, :]
                            ).then_inc(lxd, 16)
            s.wait_ge(odb, NTT * 16)
            s.wait_ge(odz, 64)

        @block.gpsimd
        def _(g):
            g.dma_start(out=wresb[:], in_=wres_in.rearrange("(db p) e -> p db e", p=128)
                        ).then_inc(ldg, 16)
            g.dma_start(out=wbrb[:], in_=wbr_in.rearrange("(db p) c -> p db c", p=128)
                        ).then_inc(ldg, 16)
            for gt in range(NTT):
                g.wait_ge(cpB, gt + 1)
                g.dma_start(out=xb_out[gt * 128:(gt + 1) * 128, :], in_=xbt[:]
                            ).then_inc(odb, 16)
            g.wait_ge(evZ, 32)
            ln2T_r = ln2T_out.rearrange("(db p) t -> p db t", p=128)
            for q in range(4):
                g.dma_start(out=ln2T_r[:, q * 8:(q + 1) * 8, :],
                            in_=xnT[:, q * 8:(q + 1) * 8, :]).then_inc(odz, 16)

        @block.tensor
        def _(t):
            t.wait_ge(ldc, 64)
            for gt in range(NTT):
                t.wait_ge(xf, gt + 1)
                for k in range(4):
                    pk = gt * 4 + k
                    if pk >= 2:
                        t.wait_ge(evX, pk - 1)
                    for j in range(8):
                        db = k * 8 + j
                        mm = t.transpose(ptrx[pk % 2][:, j * 128:(j + 1) * 128],
                                         tokbf[:, db * 128:(db + 1) * 128], idb[:])
                    mm.then_inc(trX, 1)
            t.wait_ge(evX, 32)
            t.wait_ge(ldg, 32)
            for tch in range(2):
                if tch == 1:
                    t.wait_ge(evL, 1)
                for db in range(NDB):
                    mm = t.matmul(plog[:], wresb[:, db, :],
                                  xnT[:, db, tch * 512:(tch + 1) * 512],
                                  start=(db == 0), stop=(db == NDB - 1))
                mm.then_inc(mmL, 1)
            for tt in range(NTT):
                t.wait_ge(evL, 2)
                if tt >= 1:
                    t.wait_ge(ltV, tt)
                mm = t.transpose(ptr24[:], lsb[0:24, tt * 128:(tt + 1) * 128],
                                 idf[0:24, 0:24])
                mm.then_inc(trL, 1)
            for gt in range(NTT):
                t.wait_ge(xpV, gt + 1)
                if gt >= 2:
                    t.wait_ge(evP, gt - 1)
                for j in range(8):
                    mm = t.transpose(ptrx[gt % 2][:, j * 128:(j + 1) * 128],
                                     tokbf[:, j * 128:(j + 1) * 128], idb[:])
                mm.then_inc(trP, 1)
            t.wait_ge(ldg, 32)
            for tch in range(2):
                t.wait_ge(evP, 8)
                for ob in range(8):
                    u = tch * 8 + ob
                    if u >= 1:
                        t.wait_ge(evW, u)
                    for db in range(8):
                        mm = t.matmul(pwb[:], wbrb[:, db, ob * 128:(ob + 1) * 128],
                                      xpreT[:, db, tch * 512:(tch + 1) * 512],
                                      start=(db == 0), stop=(db == 7))
                    mm.then_inc(mmW, 1)
                for tl in range(4):
                    gt = tch * 4 + tl
                    if gt >= 1:
                        zt = gt - 1
                        t.wait_ge(cpV2, zt + 1)
                        for k in range(4):
                            pk = zt * 4 + k
                            t.wait_ge(evZ, max(0, pk - 1))
                            for j in range(8):
                                db = k * 8 + j
                                mm = t.transpose(
                                    ptrx[pk % 2][:, j * 128:(j + 1) * 128],
                                    tokbf[:, db * 128:(db + 1) * 128], idb[:])
                            mm.then_inc(trZ, 1)
                    t.wait_ge(evW, (tch + 1) * 8)
                    if gt >= 1:
                        t.wait_ge(yuV, gt)
                    for j in range(8):
                        mm = t.transpose(ptry[:, j * 128:(j + 1) * 128],
                                         ysb[:, j, tl * 128:(tl + 1) * 128], idf[:])
                    mm.then_inc(trY, 1)
            zt = NTT - 1
            t.wait_ge(cpV2, zt + 1)
            for k in range(4):
                pk = zt * 4 + k
                t.wait_ge(evZ, max(0, pk - 1))
                for j in range(8):
                    db = k * 8 + j
                    mm = t.transpose(ptrx[pk % 2][:, j * 128:(j + 1) * 128],
                                     tokbf[:, db * 128:(db + 1) * 128], idb[:])
                mm.then_inc(trZ, 1)

        @block.scalar
        def _(s):
            s.wait_ge(cstR, 1)
            for gt in range(NTT):
                s.wait_ge(lxd, 16 * (gt + 1))
                b = stt[:, gt % 2, :]
                ssS.mark(s.activation(sq[:], xl1[:], AF.Square, accum_out=b[:, 1:2]))
                ssS.wait(s)
                ssS.mark(s.mul(b[:, 3:4], b[:, 1:2], 1.0 / D))
                ssS.wait(s)
                ssS.mark(s.activation(b[:, 5:6], b[:, 3:4], AF.Ln,
                                      bias=cst[:, 1:2], scale=1.0))
                ssS.wait(s)
                s.activation(b[:, 6:7], b[:, 5:6], AF.Exp, bias=0.0, scale=-0.5
                             ).then_inc(cpS, 1)
                for k in range(4):
                    pk = gt * 4 + k
                    s.wait_ge(trX, pk + 1)
                    s.copy(xnT[:, k * 8:(k + 1) * 8, gt * 128:(gt + 1) * 128],
                           ptrx[pk % 2][:].rearrange("p (a b) -> p a b", a=8)
                           ).then_inc(evX, 1)
            s.wait_ge(ldc, 64)
            ssS.mark(s.activation(gsb[:, 0:1], gsb[:, 0:1], AF.Sigmoid))
            ssS.wait(s)
            s.sem_inc(snkS, 1)
            for tch in range(2):
                s.wait_ge(mmL, tch + 1)
                s.activation(lsb[:, tch * 512:(tch + 1) * 512], plog[:], AF.Identity,
                             bias=b24[:], scale=1.0).then_inc(evL, 1)
            sv = 0
            for eng, fn in sink_steps:
                if eng == 'V':
                    sv += 1
                else:
                    s.wait_ge(snkV, sv)
                    inst = fn(s)
                    ssS.mark(inst)
                    ssS.wait(s)
                    s.sem_inc(snkS, 1)
            for gt in range(NTT):
                s.wait_ge(trP, gt + 1)
                s.copy(xpreT[:, :, gt * 128:(gt + 1) * 128],
                       ptrx[gt % 2][:].rearrange("p (a b) -> p a b", a=8)
                       ).then_inc(evP, 1)
            for gt in range(-1, NTT):
                if gt in (-1, 3):
                    tch = 0 if gt == -1 else 1
                    for ob in range(8):
                        u = tch * 8 + ob
                        s.wait_ge(mmW, u + 1)
                        s.copy(ysb[:, ob, :], pwb[:]).then_inc(evW, 1)
                if gt < 0:
                    continue
                s.wait_ge(cpB, gt + 1)
                b = stt[:, gt % 2, :]
                ssS.mark(s.activation(sq[:], xbt[:], AF.Square, accum_out=b[:, 1:2]))
                ssS.wait(s)
                s.wait_ge(cpV1, gt + 1)
                ssS.mark(s.mul(b[:, 2:3], b[:, 0:1], 1.0 / D))
                ssS.mark(s.mul(b[:, 3:4], b[:, 1:2], 1.0 / D))
                ssS.wait(s)
                ssS.mark(s.activation(b[:, 4:5], b[:, 2:3], AF.Square))
                ssS.wait(s)
                ssS.mark(s.mul(b[:, 4:5], b[:, 4:5], -1.0))
                ssS.wait(s)
                ssS.mark(s.add(b[:, 4:5], b[:, 3:4], b[:, 4:5]))
                ssS.wait(s)
                ssS.mark(s.activation(b[:, 5:6], b[:, 4:5], AF.Ln,
                                      bias=cst[:, 0:1], scale=1.0))
                ssS.wait(s)
                s.activation(b[:, 6:7], b[:, 5:6], AF.Exp, bias=0.0, scale=-0.5
                             ).then_inc(cpS, 1)
                for k in range(4):
                    s.wait_ge(trZ, gt * 4 + k + 1)
                    s.copy(xnT[:, k * 8:(k + 1) * 8, gt * 128:(gt + 1) * 128],
                           ptrx[(gt * 4 + k) % 2][:].rearrange(
                               "p (a b) -> p a b", a=8)).then_inc(evZ, 1)

        @block.vector
        def _(v):
            ssV.mark(v.memset(cst[:, 0:1], LN_EPS))
            ssV.mark(v.memset(cst[:, 1:2], RMS_EPS0))
            ssV.wait(v)
            v.sem_inc(cstR, 1)
            for gt in range(NTT):
                v.wait_ge(cpS, gt + 1)
                b = stt[:, gt % 2, :]
                if gt >= 1:
                    v.wait_ge(trX, gt * 4)
                v.tensor_scalar(tokbf[:], xl1[:], b[:, 6:7], None, ALU.mult
                                ).then_inc(xf, 1)
            v.wait_ge(snkS, 1)
            ssV.mark(v.tensor_scalar(gsb[:, 1:2], gsb[:, 0:1], -1.0, 1.0,
                                     ALU.mult, ALU.add))
            ssV.wait(v)
            for tt in range(NTT):
                v.wait_ge(trL, tt + 1)
                ssV.mark(v.tensor_copy(ltok[:, tt, :], ptr24[:]))
                ssV.wait(v)
                v.sem_inc(ltV, 1)
            ss_count = 1
            for eng, fn in sink_steps:
                if eng == 'S':
                    ss_count += 1
                else:
                    v.wait_ge(snkS, ss_count)
                    inst = fn(v)
                    ssV.mark(inst)
                    ssV.wait(v)
                    v.sem_inc(snkV, 1)
            for gt in range(NTT):
                v.wait_ge(lxd, 16 * (NTT + gt + 1))
                h = lambda n: Hpre[:, gt, n:n + 1]
                ssV.mark(v.tensor_scalar(xpre[:, 0, :], xl1[:, 0:1024], h(0),
                                         None, ALU.mult))
                ssV.wait(v)
                ssV.mark(v.scalar_tensor_tensor(xpre[:, 1, :], xl1[:, 1024:2048],
                                                h(1), xpre[:, 0, :], ALU.mult, ALU.add))
                ssV.wait(v)
                ssV.mark(v.scalar_tensor_tensor(xpre[:, 0, :], xl1[:, 2048:3072],
                                                h(2), xpre[:, 1, :], ALU.mult, ALU.add))
                ssV.wait(v)
                ssV.mark(v.scalar_tensor_tensor(xpre[:, 1, :], xl1[:, 3072:4096],
                                                h(3), xpre[:, 0, :], ALU.mult, ALU.add))
                ssV.wait(v)
                v.sem_inc(xf, 1)
                if gt == 0:
                    v.wait_ge(trX, 32)
                else:
                    v.wait_ge(trP, gt)
                v.tensor_copy(tokbf[:, 0:1024], xpre[:, 1, :]).then_inc(xpV, 1)
            for tch in range(2):
                for tl in range(4):
                    gt = tch * 4 + tl
                    v.wait_ge(lxd, 16 * (2 * NTT + gt + 1))
                    v.wait_ge(trY, gt + 1)
                    if gt >= 1:
                        v.wait_ge(odb, 16 * gt)
                    for i in range(4):
                        dst = xbt[:, i * 1024:(i + 1) * 1024]
                        hs = lambda jj: Hres[:, gt, i, jj:jj + 1]
                        ssV.mark(v.tensor_scalar(dst, xl1[:, 0:1024], hs(0),
                                                 None, ALU.mult))
                        ssV.wait(v)
                        for jj in range(1, 4):
                            ssV.mark(v.scalar_tensor_tensor(
                                dst, xl1[:, jj * 1024:(jj + 1) * 1024], hs(jj),
                                dst, ALU.mult, ALU.add))
                            ssV.wait(v)
                    v.sem_inc(xf, 1)
                    for i in range(4):
                        dst = xbt[:, i * 1024:(i + 1) * 1024]
                        ssV.mark(v.scalar_tensor_tensor(
                            dst, ptry[:], Hpost[:, gt, i:i + 1], dst,
                            ALU.mult, ALU.add))
                        ssV.wait(v)
                    v.sem_inc(yuV, 1)
                    v.wait_ge(l1d, 16 * (gt + 1))
                    ssV.mark(v.tensor_tensor(xbt[:], xbt[:], x1t[:], ALU.add))
                    ssV.wait(v)
                    v.sem_inc(x1f, 1)
                    v.sem_inc(cpB, 1)
                    b = stt[:, gt % 2, :]
                    ssV.mark(v.reduce_sum(b[:, 0:1], xbt[:], AX.X))
                    ssV.wait(v)
                    v.sem_inc(cpV1, 1)
                    v.wait_ge(cpS, NTT + gt + 1)
                    if gt == 0:
                        v.wait_ge(trP, 8)
                    else:
                        v.wait_ge(trZ, gt * 4)
                    ssV.mark(v.tensor_tensor(b[:, 7:8], b[:, 2:3], b[:, 6:7],
                                             ALU.mult))
                    ssV.wait(v)
                    v.tensor_scalar(tokbf[:], xbt[:], b[:, 6:7], b[:, 7:8],
                                    ALU.mult, ALU.subtract).then_inc(cpV2, 1)
    return nc


def host_l3b_inputs(inputs, x1_full, ln1_full):
    wres = np.concatenate([inputs["L0_ffn_W_res"], inputs["L0_ffn_W_pre"],
                           inputs["L0_ffn_W_post"]], 0)
    wres_t = np.ascontiguousarray(wres.T)
    b24 = np.concatenate([inputs["L0_ffn_H_res_base"].reshape(16),
                          inputs["L0_ffn_H_pre_base"],
                          inputs["L0_ffn_H_post_base"]]).reshape(24, 1)
    b24 = np.ascontiguousarray(b24.astype(np.float32))
    gate = inputs["L0_ffn_gate"].reshape(1, 1)
    wbr = inputs["L0_ffn_W_branch"]
    ident = np.eye(128, dtype=BF)
    identF = np.eye(128, dtype=np.float32)
    return [
        {"x1": np.ascontiguousarray(x1_full[c * TL:(c + 1) * TL]),
         "xln1": np.ascontiguousarray(ln1_full[c * TL:(c + 1) * TL]),
         "wres_t": wres_t, "base24": b24, "gate": gate, "wbr": wbr,
         "ident": ident, "identF": identF}
        for c in range(C)
    ]


DIN = 8192
THL = TL + 3


def build_l4a():
    nc = bass.Bass("TRN2", target_bir_lowering=False)
    xlh_in = nc.dram_tensor("xlh", [D, THL], BF16, kind="ExternalInput").ap()
    w_in = nc.dram_tensor("inw", [D, 2 * DIN], F32, kind="ExternalInput").ap()
    xp_out = nc.dram_tensor("xpart", [DIN, THL], F32, kind="ExternalOutput").ap()
    sz_out = nc.dram_tensor("sz", [DIN, TL], F32, kind="ExternalOutput").ap()

    NDB = D // 128
    NCBG = 2 * DIN // 512

    units = []
    for cbg in range(NCBG):
        is_z = cbg >= 16
        for cb in range(4):
            chunks = ((3, 512), (515, 512)) if is_z else ((0, 512), (512, 512), (1024, 3))
            for (lo, n) in chunks:
                units.append((cbg, cb, lo, n, is_z))
    NU = len(units)

    with ExitStack() as ctx:
        sb = lambda n, sh, dt: ctx.enter_context(nc.sbuf_tensor(n, sh, dt))
        ps = lambda n, sh, dt: ctx.enter_context(nc.psum_tensor(n, sh, dt))
        sem = lambda n: ctx.enter_context(nc.semaphore(n))
        xlh = sb("xlh_s", [128, NDB, THL], BF16)
        wst = sb("wst", [128, 2, NDB, 512], BF16)
        stg = sb("stg", [128, 4, 512], F32)
        pin = [ps(f"pin{i}", [128, 512], F32) for i in range(4)]
        ldx = sem("ldx")
        wlp = [sem("wl0"), sem("wl1")]
        odp = [sem("od0"), sem("od1"), sem("od2"), sem("od3")]
        mmS = sem("mmS"); evS = sem("evS")
        block = ctx.enter_context(nc.Block())

        od_cnt = [0, 0, 0, 0]
        od_sched = []
        for i in range(NU):
            par = i % 4
            od_sched.append((par, od_cnt[par]))
            od_cnt[par] += 1

        @block.sync
        def _(s):
            for q in range(4):
                s.dma_start(out=xlh[:, q * 8:(q + 1) * 8, :],
                            in_=xlh_in.rearrange("(db p) t -> p db t", p=128)
                            [:, q * 8:(q + 1) * 8, :]).then_inc(ldx, 16)
            for par in range(4):
                s.wait_ge(odp[par], 16 * od_cnt[par])

        @block.gpsimd
        def _(g):
            for cbg in range(NCBG):
                if cbg >= 2:
                    done = sum(1 for u in units if u[0] < cbg - 1)
                    g.wait_ge(mmS, done)
                src = w_in[:, cbg * 512:(cbg + 1) * 512].rearrange(
                    "(db p) c -> p db c", p=128)
                for q in range(4):
                    g.dma_start(out=wst[:, cbg % 2, q * 8:(q + 1) * 8, :],
                                in_=src[:, q * 8:(q + 1) * 8, :]
                                ).then_inc(wlp[cbg % 2], 16)
                if cbg >= 1:
                    for i, (c2, cb, lo, n, is_z) in enumerate(units):
                        if c2 != cbg - 1:
                            continue
                        par, cnt = od_sched[i]
                        g.wait_ge(evS, i + 1)
                        row0 = (c2 - 16 if is_z else c2) * 512 + cb * 128
                        dst = (sz_out[row0:row0 + 128, lo - 3:lo - 3 + n] if is_z
                               else xp_out[row0:row0 + 128, lo:lo + n])
                        g.dma_start(out=dst, in_=stg[:, par, 0:n]
                                    ).then_inc(odp[par], 16)
            for i, (c2, cb, lo, n, is_z) in enumerate(units):
                if c2 != NCBG - 1:
                    continue
                par, cnt = od_sched[i]
                g.wait_ge(evS, i + 1)
                row0 = (c2 - 16 if is_z else c2) * 512 + cb * 128
                dst = (sz_out[row0:row0 + 128, lo - 3:lo - 3 + n] if is_z
                       else xp_out[row0:row0 + 128, lo:lo + n])
                g.dma_start(out=dst, in_=stg[:, par, 0:n]).then_inc(odp[par], 16)

        @block.tensor
        def _(t):
            t.wait_ge(ldx, 64)
            for i, (cbg, cb, lo, n, is_z) in enumerate(units):
                t.wait_ge(wlp[cbg % 2], 64 * (cbg // 2 + 1))
                if i >= 4:
                    t.wait_ge(evS, i - 3)
                for db in range(NDB):
                    mm = t.matmul(pin[i % 4][:, 0:n],
                                  wst[:, cbg % 2, db, cb * 128:(cb + 1) * 128],
                                  xlh[:, db, lo:lo + n],
                                  start=(db == 0), stop=(db == NDB - 1))
                mm.then_inc(mmS, 1)

        @block.scalar
        def _(s):
            for i, (cbg, cb, lo, n, is_z) in enumerate(units):
                par, cnt = od_sched[i]
                s.wait_ge(mmS, i + 1)
                if cnt >= 1:
                    s.wait_ge(odp[par], 16 * cnt)
                s.activation(stg[:, par, 0:n], pin[i % 4][:, 0:n],
                             AF.Silu if is_z else AF.Identity).then_inc(evS, 1)
    return nc


RMS_EPS1 = 1e-6


def build_l4b():
    nc = bass.Bass("TRN2", target_bir_lowering=False)
    xp_in = nc.dram_tensor("xpart", [DIN, THL], F32, kind="ExternalInput").ap()
    cw_in = nc.dram_tensor("convw_t", [128, 64, 4], F32, kind="ExternalInput").ap()
    cb_in = nc.dram_tensor("convb_t", [128, 64], F32, kind="ExternalInput").ap()
    xw_in = nc.dram_tensor("xprojw_t", [DIN, 288], F32, kind="ExternalInput").ap()
    lw_in = nc.dram_tensor("lnw_row", [1, 288], F32, kind="ExternalInput").ap()
    dw_in = nc.dram_tensor("dtprojw_t", [256, DIN], F32, kind="ExternalInput").ap()
    db_in = nc.dram_tensor("dtbias_t", [128, 64], F32, kind="ExternalInput").ap()
    idb_in = nc.dram_tensor("ident", [128, 128], BF16, kind="ExternalInput").ap()
    idf_in = nc.dram_tensor("identF", [128, 128], F32, kind="ExternalInput").ap()
    xc_out = nc.dram_tensor("xc", [DIN, TL], F32, kind="ExternalOutput").ap()
    dt_out = nc.dram_tensor("dt", [DIN, TL], F32, kind="ExternalOutput").ap()
    bc_out = nc.dram_tensor("BC", [128, 8, 32], F32, kind="ExternalOutput").ap()

    NDB2 = DIN // 128

    with ExitStack() as ctx:
        sb = lambda n, sh, dt: ctx.enter_context(nc.sbuf_tensor(n, sh, dt))
        ps = lambda n, sh, dt: ctx.enter_context(nc.psum_tensor(n, sh, dt))
        sem = lambda n: ctx.enter_context(nc.semaphore(n))
        xpc = sb("xpc", [128, 2, 515], F32)
        cc = sb("cc", [128, 2, 512], F32)
        xcc = sb("xcc", [128, 2, 512], F32)
        xcb = sb("xcb", [128, 2, 512], BF16)
        xprw = sb("xprw", [128, NDB2, 288], BF16)
        cwt = sb("cwt", [128, 64, 4], F32)
        cbt = sb("cbt", [128, 64], F32)
        ssp = sb("ssp", [96, 3, TL], F32)
        stok = sb("stok", [128, 8, 288], F32)
        sq2 = sb("sq2", [128, 8, 288], F32)
        rno = sb("rno", [128, 3, 8], F32)
        sss = sb("sss", [128, 3, 8], F32)
        lnwb = sb("lnwb", [128, 288], F32)
        dtrb = sb("dtrb", [128, 8, 256], BF16)
        dtrT = sb("dtrT", [128, 2, TL], BF16)
        dpw = sb("dpw", [128, 2, DIN], BF16)
        dbt = sb("dbt", [128, 64], F32)
        dtst = sb("dtst", [128, 2, 512], F32)
        etmp = sb("etmp", [128, 512], F32)
        cst = sb("cst", [128, 2], F32)
        idb = sb("idb", [128, 128], BF16)
        idf = sb("idf", [128, 128], F32)
        ps_ssm = [ps(f"pssm{i}", [96, 512], F32) for i in range(3)]
        ptr96 = ps("ptr96", [128, 96], F32)
        ptrd = ps("ptrd", [128, 1024], BF16)
        pdt = [ps(f"pdt{i}", [128, 512], F32) for i in range(2)]
        ldc = sem("ldc"); ldg = sem("ldg")
        ldp = [sem("ldp0"), sem("ldp1")]
        odc = [sem("odc0"), sem("odc1")]
        odd = [sem("odd0"), sem("odd1")]
        odbc = sem("odbc")
        cvV = sem("cvV"); slS = sem("slS"); cbV = sem("cbV"); mmX = sem("mmX")
        evM = sem("evM"); tr9 = sem("tr9"); ev9 = sem("ev9"); rmsV = sem("rmsV")
        nrmS = sem("nrmS"); dcV = sem("dcV"); trD = sem("trD"); evD = sem("evD")
        mmD = sem("mmD"); evE = sem("evE")
        ssS = SelfSem(sem("ssS")); ssV = SelfSem(sem("ssV"))
        block = ctx.enter_context(nc.Block())

        @block.sync
        def _(s):
            s.dma_start(out=idb[:], in_=idb_in).then_inc(ldc, 16)
            s.dma_start(out=idf[:], in_=idf_in).then_inc(ldc, 16)
            s.dma_start(out=cwt[:], in_=cw_in).then_inc(ldc, 16)
            s.dma_start(out=cbt[:], in_=cb_in).then_inc(ldc, 16)
            s.dma_start(out=dbt[:], in_=db_in).then_inc(ldc, 16)
            s.dma_start(out=lnwb[:], in_=lw_in.broadcast_to((128, 288))
                        ).then_inc(ldc, 16)
            for tch in range(2):
                for db in range(NDB2):
                    i = tch * NDB2 + db
                    if i >= 2:
                        s.wait_ge(cvV, i - 1)
                    s.dma_start(out=xpc[:, i % 2, :],
                                in_=xp_in[db * 128:(db + 1) * 128,
                                          tch * 512:tch * 512 + 515]
                                ).then_inc(ldp[i % 2], 16)
            s.wait_ge(odc[0], 16 * NDB2); s.wait_ge(odc[1], 16 * NDB2)
            s.wait_ge(odd[0], 16 * NDB2); s.wait_ge(odd[1], 16 * NDB2)
            s.wait_ge(odbc, 16)

        @block.gpsimd
        def _(g):
            g.dma_start(out=xprw[:], in_=xw_in.rearrange("(db p) e -> p db e", p=128)
                        ).then_inc(ldg, 16)
            g.dma_start(out=dpw[:], in_=dw_in.rearrange("(rb p) d -> p rb d", p=128)
                        ).then_inc(ldg, 16)
            for tch in range(2):
                for db in range(NDB2):
                    i = tch * NDB2 + db
                    g.wait_ge(slS, i + 1)
                    g.dma_start(out=xc_out[db * 128:(db + 1) * 128,
                                           tch * 512:(tch + 1) * 512],
                                in_=xcc[:, i % 2, :]).then_inc(odc[i % 2], 16)
            g.wait_ge(rmsV, 2)
            g.dma_start(out=bc_out, in_=stok[:, :, 256:288]).then_inc(odbc, 16)
            for db in range(NDB2):
                for tch in range(2):
                    i = db * 2 + tch
                    g.wait_ge(evE, i + 1)
                    g.dma_start(out=dt_out[db * 128:(db + 1) * 128,
                                           tch * 512:(tch + 1) * 512],
                                in_=dtst[:, i % 2, :]).then_inc(odd[i % 2], 16)

        @block.vector
        def _(v):
            ssV.mark(v.memset(cst[:, 0:1], RMS_EPS1))
            ssV.mark(v.memset(cst[:, 1:2], 1.0))
            ssV.wait(v)
            v.sem_inc(cbV, 1)
            v.wait_ge(ldc, 96)
            for tch in range(2):
                for db in range(NDB2):
                    i = tch * NDB2 + db
                    v.wait_ge(ldp[i % 2], 16 * (i // 2 + 1))
                    w = lambda k: cwt[:, db, k:k + 1]
                    xs = xpc[:, i % 2, :]
                    ssV.mark(v.tensor_scalar(cc[:, 0, :], xs[:, 0:512], w(0),
                                             cbt[:, db:db + 1], ALU.mult, ALU.add))
                    ssV.wait(v)
                    ssV.mark(v.scalar_tensor_tensor(cc[:, 1, :], xs[:, 1:513], w(1),
                                                    cc[:, 0, :], ALU.mult, ALU.add))
                    ssV.wait(v)
                    ssV.mark(v.scalar_tensor_tensor(cc[:, 0, :], xs[:, 2:514], w(2),
                                                    cc[:, 1, :], ALU.mult, ALU.add))
                    ssV.wait(v)
                    ssV.mark(v.scalar_tensor_tensor(cc[:, 1, :], xs[:, 3:515], w(3),
                                                    cc[:, 0, :], ALU.mult, ALU.add))
                    ssV.wait(v)
                    v.sem_inc(cvV, 1)
                    v.wait_ge(slS, i + 1)
                    if i >= 2:
                        v.wait_ge(mmX, 3 * (i - 1))
                    ssV.mark(v.tensor_copy(xcb[:, i % 2, :], xcc[:, i % 2, :]))
                    ssV.wait(v)
                    v.sem_inc(cbV, 1)
            for c in range(3):
                for tt in range(8):
                    j = c * 8 + tt
                    v.wait_ge(tr9, j + 1)
                    ssV.mark(v.tensor_copy(stok[:, tt, c * 96:(c + 1) * 96], ptr96[:]))
                    ssV.wait(v)
                    v.sem_inc(ev9, 1)
            v.wait_ge(nrmS, 1)
            ssV.mark(v.reduce_sum(sss[:, 0, :], sq2[:, :, 0:256], AX.X))
            ssV.mark(v.reduce_sum(sss[:, 1, :], sq2[:, :, 256:272], AX.X))
            ssV.mark(v.reduce_sum(sss[:, 2, :], sq2[:, :, 272:288], AX.X))
            ssV.wait(v)
            ssV.mark(v.tensor_scalar(sss[:, 0, :], sss[:, 0, :], 1.0 / 256, None, ALU.mult))
            ssV.mark(v.tensor_scalar(sss[:, 1, :], sss[:, 1, :], 1.0 / 16, None, ALU.mult))
            ssV.mark(v.tensor_scalar(sss[:, 2, :], sss[:, 2, :], 1.0 / 16, None, ALU.mult))
            ssV.wait(v)
            v.sem_inc(rmsV, 1)
            v.wait_ge(nrmS, 2)
            for tt in range(8):
                ssV.mark(v.tensor_scalar(stok[:, tt, 0:256], stok[:, tt, 0:256],
                                         rno[:, 0, tt:tt + 1], None, ALU.mult))
                ssV.mark(v.tensor_scalar(stok[:, tt, 256:272], stok[:, tt, 256:272],
                                         rno[:, 1, tt:tt + 1], None, ALU.mult))
                ssV.mark(v.tensor_scalar(stok[:, tt, 272:288], stok[:, tt, 272:288],
                                         rno[:, 2, tt:tt + 1], None, ALU.mult))
                ssV.wait(v)
                ssV.mark(v.tensor_tensor(stok[:, tt, :], stok[:, tt, :], lnwb[:],
                                         ALU.mult))
                ssV.wait(v)
            v.sem_inc(rmsV, 1)
            ssV.mark(v.tensor_copy(dtrb[:], stok[:, :, 0:256]))
            ssV.wait(v)
            v.sem_inc(dcV, 1)
            for rc in range(2):
                v.wait_ge(trD, rc + 1)
                ssV.mark(v.tensor_copy(dtrT[:, rc, :], ptrd[:]))
                ssV.wait(v)
                v.sem_inc(evD, 1)

        @block.scalar
        def _(s):
            s.wait_ge(ldc, 96)
            for tch in range(2):
                for db in range(NDB2):
                    i = tch * NDB2 + db
                    s.wait_ge(cvV, i + 1)
                    if i >= 2:
                        s.wait_ge(odc[i % 2], 16 * (i // 2))
                    s.activation(xcc[:, i % 2, :], cc[:, 1, :], AF.Silu
                                 ).then_inc(slS, 1)
                s.wait_ge(mmX, 3 * NDB2 * (tch + 1))
                for c in range(3):
                    s.copy(ssp[:, c, tch * 512:(tch + 1) * 512], ps_ssm[c][:]
                           ).then_inc(evM, 1)
            s.wait_ge(ev9, 24)
            ssS.mark(s.activation(sq2[:], stok[:], AF.Square))
            ssS.wait(s)
            s.sem_inc(nrmS, 1)
            s.wait_ge(rmsV, 1)
            for seg in range(3):
                ssS.mark(s.activation(rno[:, seg, :], sss[:, seg, :], AF.Ln,
                                      bias=cst[:, 0:1], scale=1.0))
                ssS.wait(s)
                ssS.mark(s.activation(rno[:, seg, :], rno[:, seg, :], AF.Exp,
                                      bias=0.0, scale=-0.5))
                ssS.wait(s)
            s.sem_inc(nrmS, 1)
            for db in range(NDB2):
                for tch in range(2):
                    i = db * 2 + tch
                    s.wait_ge(mmD, i + 1)
                    if i >= 1:
                        s.wait_ge(evE, i)
                    if i >= 2:
                        s.wait_ge(odd[i % 2], 16 * (i // 2))
                    ssS.mark(s.activation(etmp[:], pdt[i % 2][:], AF.Exp,
                                          bias=dbt[:, db:db + 1], scale=1.0))
                    ssS.wait(s)
                    s.activation(dtst[:, i % 2, :], etmp[:], AF.Ln,
                                 bias=cst[:, 1:2], scale=1.0).then_inc(evE, 1)

        @block.tensor
        def _(t):
            t.wait_ge(ldc, 96)
            t.wait_ge(ldg, 32)
            for tch in range(2):
                for db in range(NDB2):
                    i = tch * NDB2 + db
                    t.wait_ge(cbV, i + 2)
                    if tch == 1 and db == 0:
                        t.wait_ge(evM, 3)
                    for c in range(3):
                        mm = t.matmul(ps_ssm[c][:], xprw[:, db, c * 96:(c + 1) * 96],
                                      xcb[:, i % 2, :],
                                      start=(db == 0), stop=(db == NDB2 - 1))
                    mm.then_inc(mmX, 3)
            t.wait_ge(evM, 6)
            for c in range(3):
                for tt in range(8):
                    j = c * 8 + tt
                    if j >= 1:
                        t.wait_ge(ev9, j)
                    mm = t.transpose(ptr96[:], ssp[0:96, c, tt * 128:(tt + 1) * 128],
                                     idf[0:96, 0:96])
                    mm.then_inc(tr9, 1)
            t.wait_ge(dcV, 1)
            for rc in range(2):
                if rc >= 1:
                    t.wait_ge(evD, rc)
                for tt in range(8):
                    mm = t.transpose(ptrd[:, tt * 128:(tt + 1) * 128],
                                     dtrb[:, tt, rc * 128:(rc + 1) * 128], idb[:])
                mm.then_inc(trD, 1)
            t.wait_ge(evD, 2)
            for db in range(NDB2):
                for tch in range(2):
                    i = db * 2 + tch
                    if i >= 2:
                        t.wait_ge(evE, i - 1)
                    for rb in range(2):
                        mm = t.matmul(pdt[i % 2][:],
                                      dpw[:, rb, db * 128:(db + 1) * 128],
                                      dtrT[:, rb, tch * 512:(tch + 1) * 512],
                                      start=(rb == 0), stop=(rb == 1))
                    mm.then_inc(mmD, 1)
    return nc


def build_l5():
    nc = bass.Bass("TRN2", target_bir_lowering=False)
    xc_in = nc.dram_tensor("xc_d", [TL, T], F32, kind="ExternalInput").ap()
    dt_in = nc.dram_tensor("dt_d", [TL, T], F32, kind="ExternalInput").ap()
    sz_in = nc.dram_tensor("sz_d", [TL, T], F32, kind="ExternalInput").ap()
    bt_in = nc.dram_tensor("B_T", [16, T], F32, kind="ExternalInput").ap()
    ct_in = nc.dram_tensor("C_T", [16, T], F32, kind="ExternalInput").ap()
    al_in = nc.dram_tensor("Alog_t", [128, 8, 16], F32, kind="ExternalInput").ap()
    dp_in = nc.dram_tensor("D_t", [128, 8], F32, kind="ExternalInput").ap()
    idb_in = nc.dram_tensor("ident", [128, 128], BF16, kind="ExternalInput").ap()
    yg_out = nc.dram_tensor("yg", [TL, T], BF16, kind="ExternalOutput").ap()

    TC = 1024
    NCH = T // TC
    NN = 16

    with ExitStack() as ctx:
        sb = lambda n, sh, dt: ctx.enter_context(nc.sbuf_tensor(n, sh, dt))
        ps = lambda n, sh, dt: ctx.enter_context(nc.psum_tensor(n, sh, dt))
        sem = lambda n: ctx.enter_context(nc.semaphore(n))
        dtc = sb("dtc", [128, 8, TC], F32)
        xcc = sb("xcc", [128, 8, TC], F32)
        Brep = sb("Brep", [128, NN, TC], BF16)
        Crep = sb("Crep", [128, NN, TC], BF16)
        ub = sb("ub", [128, 8, TC], BF16)
        dAb = sb("dAb", [128, 2, TC], F32)
        dxb = sb("dxb", [128, 2, TC], BF16)
        hq = sb("hq", [128, 2, TC], F32)
        hb = sb("hb", [128, 2, TC], BF16)
        chb = sb("chb", [128, 2, TC], BF16)
        carr = sb("carr", [128, 128, 2], F32)
        szc = sb("szc", [128, 2, TC], F32)
        t1b = sb("t1b", [128, TC], F32)
        ygst = sb("ygst", [128, 2, TC], BF16)
        acol = sb("acol", [128, 8, 16], F32)
        dpt = sb("dpt", [128, 8], F32)
        idb = sb("idb", [128, 128], BF16)
        py = [ps(f"py{i}", [128, TC], F32) for i in range(2)]
        ldc = sem("ldc"); ldt = sem("ldt"); ldb = sem("ldb")
        lsz = [sem("lsz0"), sem("lsz1")]
        ody = [sem("ody0"), sem("ody1")]
        aS = sem("aS"); uD = sem("uD"); chV = sem("chV"); myU = sem("myU")
        gtV = sem("gtV"); upV = sem("upV"); acS = sem("acS")
        ssS = SelfSem(sem("ssS")); ssV = SelfSem(sem("ssV"))
        block = ctx.enter_context(nc.Block())

        dt_r = dt_in.rearrange("(db p) t -> p db t", p=128)
        xc_r = xc_in.rearrange("(db p) t -> p db t", p=128)
        sz_r = sz_in.rearrange("(db p) t -> p db t", p=128)
        yg_r = yg_out.rearrange("(db p) t -> p db t", p=128)

        @block.sync
        def _(s):
            s.dma_start(out=idb[:], in_=idb_in).then_inc(ldc, 16)
            s.dma_start(out=acol[:], in_=al_in).then_inc(ldc, 16)
            s.dma_start(out=dpt[:], in_=dp_in).then_inc(ldc, 16)
            for tc in range(NCH):
                if tc >= 1:
                    s.wait_ge(aS, 128 * tc)
                    s.wait_ge(gtV, 8 * tc)
                s.dma_start(out=dtc[:], in_=dt_r[:, :, tc * TC:(tc + 1) * TC]
                            ).then_inc(ldt, 16)
                s.dma_start(out=xcc[:], in_=xc_r[:, :, tc * TC:(tc + 1) * TC]
                            ).then_inc(ldt, 16)
                for db in range(8):
                    gidx = tc * 8 + db
                    if gidx >= 2:
                        s.wait_ge(gtV, gidx - 1)
                    s.dma_start(out=szc[:, gidx % 2, :],
                                in_=sz_r[:, db, tc * TC:(tc + 1) * TC]
                                ).then_inc(lsz[gidx % 2], 16)
            s.wait_ge(ody[0], 16 * 32)
            s.wait_ge(ody[1], 16 * 32)

        @block.gpsimd
        def _(g):
            for tc in range(NCH):
                if tc >= 1:
                    g.wait_ge(uD, 128 * tc)
                for n in range(NN):
                    g.dma_start(out=Brep[:, n, :],
                                in_=bt_in[n:n + 1, tc * TC:(tc + 1) * TC]
                                .broadcast_to((128, TC))).then_inc(ldb, 16)
                    g.dma_start(out=Crep[:, n, :],
                                in_=ct_in[n:n + 1, tc * TC:(tc + 1) * TC]
                                .broadcast_to((128, TC))).then_inc(ldb, 16)
                for db in range(8):
                    gidx = tc * 8 + db
                    g.wait_ge(gtV, gidx + 1)
                    g.dma_start(out=yg_r[:, db, tc * TC:(tc + 1) * TC],
                                in_=ygst[:, gidx % 2, :]).then_inc(ody[gidx % 2], 16)

        @block.scalar
        def _(s):
            s.wait_ge(ldc, 48)
            ssS.mark(s.activation(acol[:], acol[:], AF.Exp))
            ssS.wait(s)
            ssS.mark(s.mul(acol[:], acol[:], -1.0))
            ssS.wait(s)
            s.sem_inc(acS, 1)
            for tc in range(NCH):
                s.wait_ge(ldt, 32 * (tc + 1))
                for db in range(8):
                    for n in range(NN):
                        u = tc * 128 + db * 16 + n
                        if u >= 2:
                            s.wait_ge(uD, u - 1)
                        s.activation(dAb[:, u % 2, :], dtc[:, db, :], AF.Exp,
                                     bias=0.0, scale=acol[:, db, n:n + 1]
                                     ).then_inc(aS, 1)

        @block.vector
        def _(v):
            v.wait_ge(ldc, 48)
            for tc in range(NCH):
                cp = tc % 2
                v.wait_ge(ldt, 32 * (tc + 1))
                for db in range(8):
                    ssV.mark(v.tensor_tensor(ub[:, db, :], dtc[:, db, :],
                                             xcc[:, db, :], ALU.mult))
                    ssV.wait(v)
                v.wait_ge(ldb, 512 * (tc + 1))
                for db in range(8):
                    for n in range(NN):
                        u = tc * 128 + db * 16 + n
                        cidx = db * 16 + n
                        v.wait_ge(aS, u + 1)
                        ssV.mark(v.tensor_tensor(dxb[:, u % 2, :], ub[:, db, :],
                                                 Brep[:, n, :], ALU.mult))
                        ssV.wait(v)
                        if tc == 0:
                            init = 0.0
                        else:
                            pp = (tc - 1) % 2
                            init = carr[:, cidx:cidx + 1, pp:pp + 1]
                        ssV.mark(v.tensor_tensor_scan(hq[:, u % 2, :],
                                                      dAb[:, u % 2, :],
                                                      dxb[:, u % 2, :], init,
                                                      ALU.mult, ALU.add))
                        ssV.wait(v)
                        ssV.mark(v.tensor_copy(
                            carr[:, cidx:cidx + 1, (tc % 2):(tc % 2) + 1],
                            hq[:, u % 2, TC - 1:TC]))
                        ssV.mark(v.tensor_copy(hb[:, u % 2, :], hq[:, u % 2, :]))
                        ssV.wait(v)
                        if u >= 2:
                            v.wait_ge(myU, u - 1)
                        ssV.mark(v.tensor_tensor(chb[:, u % 2, :], hb[:, u % 2, :],
                                                 Crep[:, n, :], ALU.mult))
                        ssV.wait(v)
                        v.sem_inc(chV, 1)
                        v.sem_inc(uD, 1)
                    gidx = tc * 8 + db
                    v.wait_ge(myU, tc * 128 + (db + 1) * 16)
                    v.wait_ge(lsz[gidx % 2], 16 * (gidx // 2 + 1))
                    if gidx >= 2:
                        v.wait_ge(ody[gidx % 2], 16 * (gidx // 2))
                    ssV.mark(v.scalar_tensor_tensor(t1b[:], xcc[:, db, :],
                                                    dpt[:, db:db + 1], py[db % 2][:],
                                                    ALU.mult, ALU.add))
                    ssV.wait(v)
                    ssV.mark(v.tensor_tensor(ygst[:, gidx % 2, :], t1b[:],
                                             szc[:, gidx % 2, :], ALU.mult))
                    ssV.wait(v)
                    v.sem_inc(gtV, 1)

        @block.tensor
        def _(t):
            t.wait_ge(ldc, 48)
            for tc in range(NCH):
                for db in range(8):
                    gidx = tc * 8 + db
                    if gidx >= 2:
                        t.wait_ge(gtV, gidx - 1)
                    for n in range(NN):
                        u = tc * 128 + db * 16 + n
                        t.wait_ge(chV, u + 1)
                        for c in range(2):
                            mm = t.matmul(py[db % 2][:, c * 512:(c + 1) * 512],
                                          idb[:], chb[:, u % 2, c * 512:(c + 1) * 512],
                                          start=(n == 0), stop=(n == NN - 1),
                                          skip_group_check=True)
                        mm.then_inc(myU, 1)
    return nc


def build_l6():
    nc = bass.Bass("TRN2", target_bir_lowering=False)
    yg_in = nc.dram_tensor("ygT", [DIN, TL], BF16, kind="ExternalInput").ap()
    ow_in = nc.dram_tensor("opw_t", [DIN, D], F32, kind="ExternalInput").ap()
    xb_in = nc.dram_tensor("xb", [TL, D], F32, kind="ExternalInput").ap()
    w1_in = nc.dram_tensor("w1t", [D, D], F32, kind="ExternalInput").ap()
    b1_in = nc.dram_tensor("b1_t", [128, 32], F32, kind="ExternalInput").ap()
    at_in = nc.dram_tensor("At", [D, 256], F32, kind="ExternalInput").ap()
    bf_in = nc.dram_tensor("Bft", [256, D], F32, kind="ExternalInput").ap()
    co_in = nc.dram_tensor("coeffs", [16, 1], F32, kind="ExternalInput").ap()
    sc_in = nc.dram_tensor("scales", [16, 1], F32, kind="ExternalInput").ap()
    idf_in = nc.dram_tensor("identF", [128, 128], F32, kind="ExternalInput").ap()
    idb_in = nc.dram_tensor("ident", [128, 128], BF16, kind="ExternalInput").ap()
    x3_out = nc.dram_tensor("x3", [TL, D], F32, kind="ExternalOutput").ap()

    NDB = 32
    NDB2 = 64
    NQ = 4
    w16d = nc.dram_tensor("w16d", [16, 1], F32)

    with ExitStack() as ctx:
        sb = lambda n, sh, dt: ctx.enter_context(nc.sbuf_tensor(n, sh, dt))
        ps = lambda n, sh, dt: ctx.enter_context(nc.psum_tensor(n, sh, dt))
        sem = lambda n: ctx.enter_context(nc.semaphore(n))
        ygb = sb("ygb", [128, NDB2, 256], BF16)
        wst = sb("wst", [128, 2, NDB2, 128], BF16)
        x2q = sb("x2q", [128, 2, D], F32)
        aotmp = sb("aotmp", [128, 256], F32)
        sq = sb("sq", [128, D], F32)
        stt = sb("stt", [128, 2, 8], F32)
        cst = sb("cst", [128, 1], F32)
        tokbf = sb("tokbf", [128, D], BF16)
        xl3T = sb("xl3T", [128, NDB, 256], BF16)
        w1s = sb("w1s", [128, 2, NDB, 128], BF16)
        atb = sb("atb", [128, NDB, 256], BF16)
        bftb = sb("bftb", [128, 2, D], BF16)
        b1t = sb("b1t", [128, 32], F32)
        w16 = sb("w16", [16, 2], F32)
        wcol = sb("wcol", [128, 2], F32)
        xab = sb("xab", [128, 2, 256], BF16)
        xaf = sb("xaf", [128, 2, 256], F32)
        f1tmp = sb("f1tmp", [128, 256], F32)
        idb = sb("idb", [128, 128], BF16)
        idf = sb("idf", [128, 128], F32)
        pmo = [ps(f"pmo{i}", [128, 256], F32) for i in range(2)]
        ptr = ps("ptr", [128, 256], F32)
        pb2 = ps("pb2", [128, 256], BF16)
        pm1 = [ps(f"pm1{i}", [128, 256], F32) for i in range(2)]
        pxa = [ps(f"pxa{i}", [128, 256], F32) for i in range(2)]
        ldc = sem("ldc"); ldg6 = sem("ldg6"); ldy = sem("ldy"); ldxq = sem("ldxq")
        wlo = [sem("wlo0"), sem("wlo1")]
        wl1 = [sem("wl10"), sem("wl11")]
        odx = [sem("odx0"), sem("odx1")]
        mmO = sem("mmO"); evO = sem("evO"); trO = sem("trO"); adV = sem("adV")
        cpS = sem("cpS"); cpV1 = sem("cpV1"); cpV = sem("cpV"); trX = sem("trX")
        evX = sem("evX"); wcV = sem("wcV"); wcP = sem("wcP"); wcD = sem("wcD")
        mmA = sem("mmA"); evA = sem("evA"); xabV = sem("xabV")
        mm1 = sem("mm1"); ev1 = sem("ev1")
        tr1 = sem("tr1"); ad1 = sem("ad1")
        ssS = SelfSem(sem("ssS")); ssV = SelfSem(sem("ssV"))
        block = ctx.enter_context(nc.Block())

        yg_r = yg_in.rearrange("(db p) t -> p db t", p=128)

        @block.sync
        def _(s):
            s.dma_start(out=idf[:], in_=idf_in).then_inc(ldc, 16)
            s.dma_start(out=idb[:], in_=idb_in).then_inc(ldc, 16)
            s.dma_start(out=b1t[:], in_=b1_in).then_inc(ldc, 16)
            s.dma_start(out=w16[:, 0:1], in_=co_in).then_inc(ldc, 16)
            s.dma_start(out=w16[:, 1:2], in_=sc_in).then_inc(ldc, 16)
            for q in range(NQ):
                if q >= 1:
                    s.wait_ge(ad1, 64 * q)
                    s.wait_ge(odx[(q - 1) % 2], 32 * ((q - 1) // 2 + 1))
                for i in range(2):
                    gt = q * 2 + i
                    s.dma_start(out=x2q[:, i, :],
                                in_=xb_in[gt * 128:(gt + 1) * 128, :]
                                ).then_inc(ldxq, 16)
                s.dma_start(out=ygb[:], in_=yg_r[:, :, q * 256:(q + 1) * 256]
                            ).then_inc(ldy, 16)
            s.wait_ge(odx[0], 32 * 2)
            s.wait_ge(odx[1], 32 * 2)

        @block.gpsimd
        def _(g):
            g.dma_start(out=atb[:], in_=at_in.rearrange("(db p) e -> p db e", p=128)
                        ).then_inc(ldg6, 16)
            g.dma_start(out=bftb[:], in_=bf_in.rearrange("(rb p) o -> p rb o", p=128)
                        ).then_inc(ldg6, 16)
            g.wait_ge(wcV, 1)
            g.dma_start(out=w16d.ap(), in_=w16[:, 0:1]).then_inc(wcD, 16)
            g.wait_ge(wcD, 16)
            for rb in range(2):
                g.dma_start(out=wcol[:, rb:rb + 1],
                            in_=w16d.ap()[rb * 8:(rb + 1) * 8, :]
                            .rearrange("e x -> e () x").broadcast_to((8, 16, 1))
                            ).then_inc(wcD, 16)
            for q in range(NQ):
                if q >= 1:
                    g.wait_ge(ad1, 64 * q)
                    for i in range(2):
                        gt = (q - 1) * 2 + i
                        g.dma_start(out=x3_out[gt * 128:(gt + 1) * 128, :],
                                    in_=x2q[:, i, :]).then_inc(odx[(q - 1) % 2], 16)
                for ocb in range(32):
                    k = q * 32 + ocb
                    if k >= 2:
                        g.wait_ge(mmO, (k - 1))
                    src = ow_in[:, ocb * 128:(ocb + 1) * 128].rearrange(
                        "(db p) c -> p db c", p=128)
                    for part in range(8):
                        g.dma_start(out=wst[:, k % 2, part * 8:(part + 1) * 8, :],
                                    in_=src[:, part * 8:(part + 1) * 8, :]
                                    ).then_inc(wlo[k % 2], 16)
                for ocb in range(32):
                    k = q * 32 + ocb
                    if k >= 2:
                        g.wait_ge(mm1, (k - 1))
                    src = w1_in[:, ocb * 128:(ocb + 1) * 128].rearrange(
                        "(db p) c -> p db c", p=128)
                    for part in range(4):
                        g.dma_start(out=w1s[:, k % 2, part * 8:(part + 1) * 8, :],
                                    in_=src[:, part * 8:(part + 1) * 8, :]
                                    ).then_inc(wl1[k % 2], 16)
            g.wait_ge(ad1, 64 * NQ)
            for i in range(2):
                gt = (NQ - 1) * 2 + i
                g.dma_start(out=x3_out[gt * 128:(gt + 1) * 128, :],
                            in_=x2q[:, i, :]).then_inc(odx[(NQ - 1) % 2], 16)

        @block.tensor
        def _(t):
            t.wait_ge(ldc, 80)
            t.wait_ge(ldg6, 32)
            for q in range(NQ):
                t.wait_ge(ldy, 16 * (q + 1))
                for ocb in range(32):
                    k = q * 32 + ocb
                    t.wait_ge(wlo[k % 2], 128 * (k // 2 + 1))
                    if k >= 2:
                        t.wait_ge(evO, k - 1)
                    for db in range(NDB2):
                        mm = t.matmul(pmo[k % 2][:], wst[:, k % 2, db, :],
                                      ygb[:, db, :],
                                      start=(db == 0), stop=(db == NDB2 - 1))
                    mm.then_inc(mmO, 1)
                    t.wait_ge(evO, k + 1)
                    if k >= 1:
                        t.wait_ge(adV, 2 * k)
                    for i in range(2):
                        mm = t.transpose(ptr[:, i * 128:(i + 1) * 128],
                                         aotmp[:, i * 128:(i + 1) * 128], idf[:])
                    mm.then_inc(trO, 1)
                t.wait_ge(adV, 64 * (q + 1))
                for i in range(2):
                    gt = q * 2 + i
                    t.wait_ge(cpV, gt + 1)
                    for kk in range(16):
                        pk = gt * 16 + kk
                        if pk >= 1:
                            t.wait_ge(evX, pk)
                        for j in range(2):
                            db = kk * 2 + j
                            mm = t.transpose(pb2[:, j * 128:(j + 1) * 128],
                                             tokbf[:, db * 128:(db + 1) * 128],
                                             idb[:])
                        mm.then_inc(trX, 1)
                t.wait_ge(evX, (q + 1) * 32)
                t.wait_ge(wcD, 48)
                if q >= 1:
                    t.wait_ge(evA, 2 * q)
                for rb in range(2):
                    for db in range(NDB):
                        mm = t.matmul(pxa[rb][:], atb[:, db, rb * 128:(rb + 1) * 128],
                                      xl3T[:, db, :],
                                      start=(db == 0), stop=(db == NDB - 1))
                    mm.then_inc(mmA, 1)
                t.wait_ge(xabV, q + 1)
                for ocb in range(32):
                    k = q * 32 + ocb
                    t.wait_ge(wl1[k % 2], 64 * (k // 2 + 1))
                    if k >= 2:
                        t.wait_ge(ev1, k - 1)
                    for db in range(NDB):
                        t.matmul(pm1[k % 2][:], w1s[:, k % 2, db, :],
                                 xl3T[:, db, :],
                                 start=(db == 0), stop=False)
                    for rb in range(2):
                        mm = t.matmul(pm1[k % 2][:], bftb[:, rb, ocb * 128:(ocb + 1) * 128],
                                      xab[:, rb, :],
                                      start=False, stop=(rb == 1))
                    mm.then_inc(mm1, 1)
                    t.wait_ge(ev1, k + 1)
                    if k >= 1:
                        t.wait_ge(ad1, 2 * k)
                    for i in range(2):
                        mm = t.transpose(ptr[:, i * 128:(i + 1) * 128],
                                         f1tmp[:, i * 128:(i + 1) * 128], idf[:])
                    mm.then_inc(tr1, 1)

        @block.scalar
        def _(s):
            for q in range(NQ):
                for ocb in range(32):
                    k = q * 32 + ocb
                    s.wait_ge(mmO, k + 1)
                    if k >= 1:
                        s.wait_ge(trO, k)
                    s.activation(aotmp[:], pmo[k % 2][:], AF.Identity
                                 ).then_inc(evO, 1)
                for i in range(2):
                    gt = q * 2 + i
                    s.wait_ge(adV, 64 * (q + 1) + 0)
                    s.wait_ge(cpV1, gt + 1)
                    b = stt[:, gt % 2, :]
                    ssS.mark(s.activation(sq[:], x2q[:, i, :], AF.Square,
                                          accum_out=b[:, 1:2]))
                    ssS.wait(s)
                    ssS.mark(s.mul(b[:, 2:3], b[:, 0:1], 1.0 / D))
                    ssS.mark(s.mul(b[:, 3:4], b[:, 1:2], 1.0 / D))
                    ssS.wait(s)
                    ssS.mark(s.activation(b[:, 4:5], b[:, 2:3], AF.Square))
                    ssS.wait(s)
                    ssS.mark(s.mul(b[:, 4:5], b[:, 4:5], -1.0))
                    ssS.wait(s)
                    ssS.mark(s.add(b[:, 4:5], b[:, 3:4], b[:, 4:5]))
                    ssS.wait(s)
                    ssS.mark(s.activation(b[:, 5:6], b[:, 4:5], AF.Ln,
                                          bias=cst[:, 0:1], scale=1.0))
                    ssS.wait(s)
                    s.activation(b[:, 6:7], b[:, 5:6], AF.Exp, bias=0.0, scale=-0.5
                                 ).then_inc(cpS, 1)
                for rb in range(2):
                    s.wait_ge(mmA, 2 * q + rb + 1)
                    s.copy(xaf[:, rb, :], pxa[rb][:]).then_inc(evA, 1)
                for ocb in range(32):
                    k = q * 32 + ocb
                    s.wait_ge(mm1, k + 1)
                    if k >= 1:
                        s.wait_ge(tr1, k)
                    s.activation(f1tmp[:], pm1[k % 2][:], AF.Identity,
                                 bias=b1t[:, ocb:ocb + 1], scale=1.0
                                 ).then_inc(ev1, 1)

        @block.vector
        def _(v):
            v.memset(cst[:], LN_EPS)
            v.wait_ge(ldc, 80)
            ssV.mark(v.tensor_tensor(w16[:, 0:1], w16[:, 0:1], w16[:, 1:2],
                                     ALU.mult))
            ssV.wait(v)
            v.sem_inc(wcV, 1)
            for q in range(NQ):
                for ocb in range(32):
                    k = q * 32 + ocb
                    v.wait_ge(trO, k + 1)
                    v.wait_ge(ldxq, 32 * (q + 1))
                    for i in range(2):
                        vv = v.tensor_tensor(x2q[:, i, ocb * 128:(ocb + 1) * 128],
                                             x2q[:, i, ocb * 128:(ocb + 1) * 128],
                                             ptr[:, i * 128:(i + 1) * 128], ALU.add)
                    vv.then_inc(adV, 2)
                for i in range(2):
                    gt = q * 2 + i
                    v.wait_ge(adV, 64 * (q + 1))
                    b = stt[:, gt % 2, :]
                    ssV.mark(v.reduce_sum(b[:, 0:1], x2q[:, i, :], AX.X))
                    ssV.wait(v)
                    v.sem_inc(cpV1, 1)
                    v.wait_ge(cpS, gt + 1)
                    ssV.mark(v.tensor_tensor(b[:, 7:8], b[:, 2:3], b[:, 6:7],
                                             ALU.mult))
                    ssV.wait(v)
                    v.tensor_scalar(tokbf[:], x2q[:, i, :], b[:, 6:7], b[:, 7:8],
                                    ALU.mult, ALU.subtract).then_inc(cpV, 1)
                    for kk in range(16):
                        pk = gt * 16 + kk
                        v.wait_ge(trX, pk + 1)
                        ssV.mark(v.tensor_copy(
                            xl3T[:, kk * 2:(kk + 1) * 2, i * 128:(i + 1) * 128],
                            pb2[:].rearrange("p (a b) -> p a b", a=2)))
                        ssV.wait(v)
                        v.sem_inc(evX, 1)
                v.wait_ge(evA, 2 * (q + 1))
                v.wait_ge(wcD, 48)
                for rb in range(2):
                    ssV.mark(v.tensor_scalar(xab[:, rb, :], xaf[:, rb, :],
                                             wcol[:, rb:rb + 1], None, ALU.mult))
                    ssV.wait(v)
                v.sem_inc(xabV, 1)
                for ocb in range(32):
                    k = q * 32 + ocb
                    v.wait_ge(tr1, k + 1)
                    for i in range(2):
                        vv = v.tensor_tensor(x2q[:, i, ocb * 128:(ocb + 1) * 128],
                                             x2q[:, i, ocb * 128:(ocb + 1) * 128],
                                             ptr[:, i * 128:(i + 1) * 128], ALU.add)
                    vv.then_inc(ad1, 2)
    return nc


def _ident_b():
    return np.eye(128, dtype=BF)


def _ident_f():
    return np.eye(128, dtype=np.float32)


def kernel(**inputs):
    cores = list(range(C))

    r1 = run_bass_kernel_spmd(build_l1(), host_l1_inputs(inputs), core_ids=cores)
    l1_outs = [r["qkvT"] for r in r1.results]

    r2 = run_bass_kernel_spmd(build_l2(), host_l2_inputs(l1_outs), core_ids=cores)
    ctx = np.zeros((T, D), dtype=BF)
    for c in range(C):
        o = r2.results[c]["ctxq"]
        for j in range(4):
            ctx[:, (c * 4 + j) * HD:(c * 4 + j + 1) * HD] = o[j]

    r3a = run_bass_kernel_spmd(build_l3a(), host_l3a_inputs(inputs, ctx),
                               core_ids=cores)
    x1_full = np.concatenate([r["x1"] for r in r3a.results], 0)
    ln1_full = np.concatenate([r["xln1"] for r in r3a.results], 0)

    r3b = run_bass_kernel_spmd(build_l3b(), host_l3b_inputs(inputs, x1_full, ln1_full),
                               core_ids=cores)
    xb_full = np.concatenate([r["xb"] for r in r3b.results], 0)
    ln2T_full = np.concatenate([r["xln2T"] for r in r3b.results], 1)

    def halo(c):
        lo = c * TL - 3
        out = np.zeros((D, THL), dtype=BF)
        s = max(0, lo)
        out[:, (s - lo):] = ln2T_full[:, s:(c + 1) * TL]
        return out

    inw = inputs["L1_attn_in_proj_w"]
    maps4a = [{"xlh": halo(c), "inw": inw} for c in range(C)]
    r4a = run_bass_kernel_spmd(build_l4a(), maps4a, core_ids=cores)

    cw = np.ascontiguousarray(
        inputs['L1_attn_conv_w'].reshape(64, 128, 4).transpose(1, 0, 2))
    cb = np.ascontiguousarray(inputs['L1_attn_conv_b'].reshape(64, 128).T)
    dbias = np.ascontiguousarray(inputs['L1_attn_dt_bias'].reshape(64, 128).T)
    lnw = np.ascontiguousarray(np.concatenate(
        [inputs['L1_attn_dt_ln_w'], inputs['L1_attn_B_ln_w'],
         inputs['L1_attn_C_ln_w']]).reshape(1, 288))
    xprot = np.ascontiguousarray(inputs['L1_attn_x_proj_w'].T)
    dtprot = np.ascontiguousarray(inputs['L1_attn_dt_proj_w'].T)
    maps4b = [{"xpart": r4a.results[c]["xpart"], "convw_t": cw, "convb_t": cb,
               "xprojw_t": xprot, "lnw_row": lnw, "dtprojw_t": dtprot,
               "dtbias_t": dbias, "ident": _ident_b(), "identF": _ident_f()}
              for c in range(C)]
    r4b = run_bass_kernel_spmd(build_l4b(), maps4b, core_ids=cores)

    xc_full = np.concatenate([r4b.results[c]["xc"] for c in range(C)], 1)
    dt_full = np.concatenate([r4b.results[c]["dt"] for c in range(C)], 1)
    sz_full = np.concatenate([r4a.results[c]["sz"] for c in range(C)], 1)
    B_full = np.concatenate(
        [r4b.results[c]["BC"][:, :, 0:16].transpose(1, 0, 2).reshape(TL, 16)
         for c in range(C)], 0)
    C_full = np.concatenate(
        [r4b.results[c]["BC"][:, :, 16:32].transpose(1, 0, 2).reshape(TL, 16)
         for c in range(C)], 0)
    B_T = np.ascontiguousarray(B_full.T)
    C_T = np.ascontiguousarray(C_full.T)
    Alog = inputs['L1_attn_A_log']
    Dp = inputs['L1_attn_D_param']
    maps5 = []
    for c in range(C):
        sl = slice(c * TL, (c + 1) * TL)
        maps5.append({
            "xc_d": np.ascontiguousarray(xc_full[sl]),
            "dt_d": np.ascontiguousarray(dt_full[sl]),
            "sz_d": np.ascontiguousarray(sz_full[sl]),
            "B_T": B_T, "C_T": C_T,
            "Alog_t": np.ascontiguousarray(
                Alog[sl].reshape(8, 128, 16).transpose(1, 0, 2)),
            "D_t": np.ascontiguousarray(Dp[sl].reshape(8, 128).T),
            "ident": _ident_b()})
    r5 = run_bass_kernel_spmd(build_l5(), maps5, core_ids=cores)

    yg_full = np.concatenate([r5.results[c]["yg"] for c in range(C)], 0)
    opw_t = np.ascontiguousarray(inputs['L1_attn_out_proj_w'].T)
    w1t = np.ascontiguousarray(inputs['L1_ffn_W'].T)
    b1t = np.ascontiguousarray(inputs['L1_ffn_bias'].reshape(32, 128).T)
    At = np.ascontiguousarray(inputs['L1_ffn_A'].reshape(256, D).T)
    Bft = np.ascontiguousarray(inputs['L1_ffn_Bf'].transpose(0, 2, 1).reshape(256, D))
    maps6 = []
    for c in range(C):
        sl = slice(c * TL, (c + 1) * TL)
        maps6.append({
            "ygT": np.ascontiguousarray(yg_full[:, sl]),
            "opw_t": opw_t, "xb": np.ascontiguousarray(xb_full[sl]),
            "w1t": w1t, "b1_t": b1t, "At": At, "Bft": Bft,
            "coeffs": np.ascontiguousarray(inputs['L1_ffn_coeffs'].reshape(16, 1)),
            "scales": np.ascontiguousarray(inputs['L1_ffn_scales'].reshape(16, 1)),
            "identF": _ident_f(), "ident": _ident_b()})
    r6 = run_bass_kernel_spmd(build_l6(), maps6, core_ids=cores)
    x3 = np.concatenate([r["x3"] for r in r6.results], 0)
    return np.ascontiguousarray(x3.reshape(1, T, D)).astype(np.float32)


# version 02
# speedup vs baseline: 1.2756x; 1.2756x over previous
from contextlib import ExitStack
import numpy as np
import ml_dtypes
import concourse.bass as bass
import concourse.mybir as mybir
from concourse.bass_utils import run_bass_kernel_spmd

F32 = mybir.dt.float32
BF16 = mybir.dt.bfloat16
AF = mybir.ActivationFunctionType
ALU = mybir.AluOpType
AX = mybir.AxisListType
BF = ml_dtypes.bfloat16

C = 8
T = 8192
TL = T // C
D = 4096
NH = 32
HD = 128
QKV = 3 * D
LN_EPS = 1e-5



class SelfSem:
    def __init__(self, sem):
        self.sem = sem
        self.n = 0

    def mark(self, inst):
        inst.then_inc(self.sem, 1)
        self.n += 1
        return inst

    def wait(self, eng):
        if self.n:
            eng.wait_ge(self.sem, self.n)


def build_l1():
    nc = bass.Bass("TRN2", target_bir_lowering=False)
    x_in = nc.dram_tensor("x", [TL, D], F32, kind="ExternalInput").ap()
    w_in = nc.dram_tensor("wqkv", [D, QKV], F32, kind="ExternalInput").ap()
    b_in = nc.dram_tensor("qkvb_t", [128, 96], F32, kind="ExternalInput").ap()
    id_in = nc.dram_tensor("ident", [128, 128], BF16, kind="ExternalInput").ap()
    out = nc.dram_tensor("qkvT", [QKV, TL], BF16, kind="ExternalOutput").ap()

    NTT = TL // 128
    NDB = D // 128
    NCBG = QKV // 512
    NUNITS = NCBG * 2 * 4

    with ExitStack() as ctx:
        sb = lambda n, sh, dt: ctx.enter_context(nc.sbuf_tensor(n, sh, dt))
        ps = lambda n, sh, dt: ctx.enter_context(nc.psum_tensor(n, sh, dt))
        sem = lambda n: ctx.enter_context(nc.semaphore(n))
        xt = sb("xt", [128, 2, D], F32)
        sq = sb("sq", [128, D], F32)
        stt = sb("st", [128, 2, 8], F32)
        xln = sb("xln", [128, 2, D], BF16)
        xlnT = sb("xlnT", [128, NDB, TL], BF16)
        idb = sb("idb", [128, 128], BF16)
        bq = sb("bq", [128, 96], F32)
        wst = sb("wst", [128, 2, NDB, 512], BF16)
        stg = sb("stg", [128, 2, 4, 512], BF16)
        cst = sb("cst", [128, 1], F32)
        ptr = [ps(f"ptr{i}", [128, 1024], BF16) for i in range(2)]
        pm = [ps(f"pm{i}", [128, 512], F32) for i in range(6)]
        ldc = sem("ldc")
        ldp = [sem("ld0"), sem("ld1")]
        wlp = [sem("wl0"), sem("wl1")]
        odp = [sem("od0"), sem("od1")]
        cpS = sem("cpS"); cpV1 = sem("cpV1")
        ssS = SelfSem(sem("ssS")); ssV = SelfSem(sem("ssV"))
        cpV = sem("cpV"); trS = sem("trS"); evT = sem("evT"); mmS = sem("mmS")
        evS = sem("evS")
        block = ctx.enter_context(nc.Block())

        @block.sync
        def _(s):
            s.dma_start(out=idb[:], in_=id_in).then_inc(ldc, 16)
            s.dma_start(out=bq[:], in_=b_in).then_inc(ldc, 16)
            for tt in range(NTT):
                if tt >= 2:
                    s.wait_ge(cpV, tt - 1)
                s.dma_start(out=xt[:, tt % 2, :], in_=x_in[tt * 128:(tt + 1) * 128, :]
                            ).then_inc(ldp[tt % 2], 16)
            s.wait_ge(odp[0], 16 * 4 * (NUNITS // 8))
            s.wait_ge(odp[1], 16 * 4 * (NUNITS // 8))

        @block.scalar
        def _(s):
            for tt in range(NTT):
                s.wait_ge(ldp[tt % 2], 16 * (tt // 2 + 1))
                if tt >= 2:
                    s.wait_ge(cpV, tt - 1)
                b = stt[:, tt % 2, :]
                ssS.mark(s.activation(sq[:], xt[:, tt % 2, :], AF.Square,
                                      accum_out=b[:, 1:2]))
                ssS.wait(s)
                s.wait_ge(cpV1, tt + 1)
                ssS.mark(s.mul(b[:, 2:3], b[:, 0:1], 1.0 / D))
                ssS.mark(s.mul(b[:, 3:4], b[:, 1:2], 1.0 / D))
                ssS.wait(s)
                ssS.mark(s.activation(b[:, 4:5], b[:, 2:3], AF.Square))
                ssS.wait(s)
                ssS.mark(s.mul(b[:, 4:5], b[:, 4:5], -1.0))
                ssS.wait(s)
                ssS.mark(s.add(b[:, 4:5], b[:, 3:4], b[:, 4:5]))
                ssS.wait(s)
                ssS.mark(s.activation(b[:, 5:6], b[:, 4:5], AF.Ln, bias=cst[:, 0:1], scale=1.0))
                ssS.wait(s)
                s.activation(b[:, 6:7], b[:, 5:6], AF.Exp, bias=0.0, scale=-0.5
                             ).then_inc(cpS, 1)
                if tt >= 1:
                    for k in range(4):
                        pk = (tt - 1) * 4 + k
                        s.wait_ge(trS, pk + 1)
                        s.copy(xlnT[:, k * 8:(k + 1) * 8,
                                    (tt - 1) * 128: tt * 128],
                               ptr[pk % 2][:].rearrange("p (a b) -> p a b", a=8)
                               ).then_inc(evT, 1)
            for k in range(4):
                pk = (NTT - 1) * 4 + k
                s.wait_ge(trS, pk + 1)
                s.copy(xlnT[:, k * 8:(k + 1) * 8, (NTT - 1) * 128: NTT * 128],
                       ptr[pk % 2][:].rearrange("p (a b) -> p a b", a=8)
                       ).then_inc(evT, 1)
            s.wait_ge(ldc, 32)
            for u in range(NUNITS):
                cbg, r = divmod(u, 8)
                tch, cb = divmod(r, 4)
                s.wait_ge(mmS, u + 1)
                if u >= 2 * 4:
                    s.wait_ge(odp[(u // 4) % 2], 64 * ((u // 4) // 2))
                s.activation(stg[:, (u // 4) % 2, cb, :], pm[u % 6][:], AF.Identity,
                             bias=bq[:, cbg * 4 + cb:cbg * 4 + cb + 1], scale=1.0
                             ).then_inc(evS, 1)

        @block.vector
        def _(v):
            v.memset(cst[:], LN_EPS)
            for tt in range(NTT):
                v.wait_ge(ldp[tt % 2], 16 * (tt // 2 + 1))
                b = stt[:, tt % 2, :]
                v.reduce_sum(b[:, 0:1], xt[:, tt % 2, :], AX.X).then_inc(cpV1, 1)
                v.wait_ge(cpS, tt + 1)
                if tt >= 2:
                    v.wait_ge(trS, (tt - 1) * 4)
                ssV.mark(v.tensor_tensor(b[:, 7:8], b[:, 2:3], b[:, 6:7], ALU.mult))
                ssV.wait(v)
                v.tensor_scalar(xln[:, tt % 2, :], xt[:, tt % 2, :],
                                b[:, 6:7], b[:, 7:8], ALU.mult, ALU.subtract
                                ).then_inc(cpV, 1)

        @block.tensor
        def _(t):
            t.wait_ge(ldc, 32)
            for tt in range(NTT):
                t.wait_ge(cpV, tt + 1)
                for k in range(4):
                    pk = tt * 4 + k
                    if pk >= 2:
                        t.wait_ge(evT, pk - 1)
                    for j in range(8):
                        db = k * 8 + j
                        mm = t.transpose(
                            ptr[pk % 2][:, j * 128:(j + 1) * 128],
                            xln[:, tt % 2, db * 128:(db + 1) * 128], idb[:])
                    mm.then_inc(trS, 1)
            t.wait_ge(evT, NTT * 4)
            for cbg in range(NCBG):
                t.wait_ge(wlp[cbg % 2], 64 * (cbg // 2 + 1))
                for tch in range(2):
                    for cb in range(4):
                        u = cbg * 8 + tch * 4 + cb
                        if u >= 6:
                            t.wait_ge(evS, u - 5)
                        for db in range(NDB):
                            mm = t.matmul(
                                pm[u % 6][:],
                                wst[:, cbg % 2, db, cb * 128:(cb + 1) * 128],
                                xlnT[:, db, tch * 512:(tch + 1) * 512],
                                start=(db == 0), stop=(db == NDB - 1))
                        mm.then_inc(mmS, 1)

        @block.gpsimd
        def _(g):
            def outdmas(cbg2):
                for r in range(8):
                    u = cbg2 * 8 + r
                    tch, cb = divmod(r, 4)
                    g.wait_ge(evS, u + 1)
                    g.dma_start(
                        out=out[cbg2 * 512 + cb * 128: cbg2 * 512 + (cb + 1) * 128,
                                tch * 512:(tch + 1) * 512],
                        in_=stg[:, (u // 4) % 2, cb, :]).then_inc(odp[(u // 4) % 2], 16)
            for cbg in range(NCBG):
                if cbg >= 2:
                    g.wait_ge(mmS, (cbg - 1) * 8)
                src = w_in[:, cbg * 512:(cbg + 1) * 512].rearrange(
                    "(db p) c -> p db c", p=128)
                for q in range(4):
                    g.dma_start(out=wst[:, cbg % 2, q * 8:(q + 1) * 8, :],
                                in_=src[:, q * 8:(q + 1) * 8, :]).then_inc(wlp[cbg % 2], 16)
                if cbg >= 1:
                    outdmas(cbg - 1)
            outdmas(NCBG - 1)
    return nc


def host_l1_inputs(inputs):
    x = np.ascontiguousarray(inputs["L0_attn_x"].reshape(T, D))
    w = inputs["L0_attn_attn_qkvw"]
    bt = np.ascontiguousarray(inputs["L0_attn_attn_qkvb"].reshape(96, 128).T)
    ident = np.eye(128, dtype=BF)
    return [
        {"x": np.ascontiguousarray(x[c * TL:(c + 1) * TL]),
         "wqkv": w, "qkvb_t": bt, "ident": ident}
        for c in range(C)
    ]


def build_l2():
    nc = bass.Bass("TRN2", target_bir_lowering=False)
    qT_in = nc.dram_tensor("qT", [512, T], BF16, kind="ExternalInput").ap()
    kT_in = nc.dram_tensor("kT", [512, T], BF16, kind="ExternalInput").ap()
    vh_in = nc.dram_tensor("vh", [4, T, HD], BF16, kind="ExternalInput").ap()
    idf_in = nc.dram_tensor("identF", [128, 128], F32, kind="ExternalInput").ap()
    out = nc.dram_tensor("ctxq", [4, T, HD], BF16, kind="ExternalOutput").ap()
    mbd = nc.dram_tensor("mbd", [1, 1], F32)

    SCALE = float(1.0 / np.sqrt(HD))
    NQC = T // 512
    NKC = T // 128
    NG = NKC // 2

    with ExitStack() as ctx:
        sb = lambda n, sh, dt: ctx.enter_context(nc.sbuf_tensor(n, sh, dt))
        ps = lambda n, sh, dt: ctx.enter_context(nc.psum_tensor(n, sh, dt))
        sem = lambda n: ctx.enter_context(nc.semaphore(n))
        kt = sb("kt", [128, 2, T], BF16)
        qt = sb("qt", [128, 2, T], BF16)
        vt = sb("vt", [128, 2, NKC, HD], BF16)
        idf = sb("idf", [128, 128], F32)
        onr = sb("onr", [1, 128], BF16)
        onc = sb("onc", [128, 1], BF16)
        pbt = sb("pbt", [128, 2, 1024], BF16)
        mraw = sb("mraw", [1, 2], F32)
        mbb = sb("mbb", [128, 1], F32)
        bias = sb("bias", [128, 1], F32)
        ctmp = sb("ctmp", [128, 512], F32)
        rs = sb("rs", [1, 512], F32)
        rcs = sb("rcs", [128, 4], F32)
        stg = sb("stg", [128, 2, 4, 128], BF16)
        psc = [ps(f"psc{i}", [128, 1024], F32) for i in range(2)]
        pcx = ps("pcx", [128, 512], F32)
        psm = ps("psm", [1, 512], F32)
        ptr = ps("ptr", [128, 512], F32)
        prc = ps("prc", [128, 4], F32)
        ldc = sem("ldc")
        ldh = [sem("ldh0"), sem("ldh1")]
        odp = [sem("od0"), sem("od1")]
        mmsc = sem("mmsc"); expS = sem("expS"); pvS = sem("pvS")
        smpT = sem("smpT"); smpV = sem("smpV"); bcP = sem("bcP"); bcV = sem("bcV")
        evC = sem("evC"); trC = sem("trC"); nrmV = sem("nrmV"); cmpH = sem("cmpH")
        ssV = SelfSem(sem("ssV"))
        block = ctx.enter_context(nc.Block())

        NQ_UNITS = 4 * NQC

        @block.sync
        def _(s):
            s.dma_start(out=idf[:], in_=idf_in).then_inc(ldc, 16)
            for j in range(4):
                if j >= 2:
                    s.wait_ge(cmpH, j - 1)
                s.dma_start(out=kt[:, j % 2, :], in_=kT_in[j * 128:(j + 1) * 128, :]
                            ).then_inc(ldh[j % 2], 16)
                s.dma_start(out=qt[:, j % 2, :], in_=qT_in[j * 128:(j + 1) * 128, :]
                            ).then_inc(ldh[j % 2], 16)
                s.dma_start(out=vt[:, j % 2, :, :],
                            in_=vh_in[j].rearrange("(kc p) c -> p kc c", p=128)
                            ).then_inc(ldh[j % 2], 16)
            s.wait_ge(odp[0], 16 * (NQ_UNITS // 2))
            s.wait_ge(odp[1], 16 * (NQ_UNITS // 2))

        @block.tensor
        def _(t):
            t.wait_ge(ldc, 16)
            nsc = 0
            npv = 0
            for j in range(4):
                t.wait_ge(ldh[j % 2], 48 * (j // 2 + 1))
                if j >= 1:
                    t.wait_ge(expS, 512 * j)
                t.matmul(psc[0][:, 0:512], kt[:, j % 2, 0:128], qt[:, j % 2, 0:512],
                         start=True, stop=True).then_inc(smpT, 1)
                t.wait_ge(bcV, j + 1)
                for qc in range(NQC):
                    u = j * NQC + qc
                    if u >= 1:
                        t.wait_ge(evC, u)
                        t.wait_ge(nrmV, 2 * u - 1)
                    for g in range(NG + 1):
                        if g < NG:
                            if nsc >= 2:
                                t.wait_ge(expS, nsc - 1)
                            for i in range(2):
                                kc = g * 2 + i
                                mm = t.matmul(psc[nsc % 2][:, i * 512:(i + 1) * 512],
                                              kt[:, j % 2, kc * 128:(kc + 1) * 128],
                                              qt[:, j % 2, qc * 512:(qc + 1) * 512],
                                              start=True, stop=True)
                            mm.then_inc(mmsc, 1)
                            nsc += 1
                        if g >= 1:
                            gg = g - 1
                            t.wait_ge(expS, npv + 1)
                            for i in range(2):
                                kc = gg * 2 + i
                                t.matmul(pcx[:], vt[:, j % 2, kc, :],
                                         pbt[:, gg % 2, i * 512:(i + 1) * 512],
                                         start=(kc == 0), stop=(kc == NKC - 1),
                                         skip_group_check=True)
                                mm = t.matmul(psm[:], onc[:],
                                              pbt[:, gg % 2, i * 512:(i + 1) * 512],
                                              start=(kc == 0), stop=(kc == NKC - 1),
                                              skip_group_check=True)
                            mm.then_inc(pvS, 1)
                            npv += 1
                    t.wait_ge(evC, u + 1)
                    if u >= 1:
                        t.wait_ge(nrmV, 2 * u)
                    for i in range(4):
                        mm = t.transpose(ptr[:, i * 128:(i + 1) * 128],
                                         ctmp[:, i * 128:(i + 1) * 128], idf[:])
                        if qc == NQC - 1 and i == 0:
                            mm.then_inc(cmpH, 1)
                    t.wait_ge(nrmV, 2 * u + 1)
                    for i in range(4):
                        mm = t.transpose(prc[:, i:i + 1], rs[:, i * 128:(i + 1) * 128],
                                         idf[0:1, 0:1])
                    mm.then_inc(trC, 1)
                if j == 3:
                    pass

        @block.scalar
        def _(s):
            for j in range(4):
                for qc in range(NQC):
                    u = j * NQC + qc
                    for g in range(NG):
                        gg = u * NG + g
                        s.wait_ge(mmsc, gg + 1)
                        if gg >= 2:
                            s.wait_ge(pvS, gg - 1)
                        s.activation(pbt[:, g % 2, :], psc[gg % 2][:], AF.Exp,
                                     bias=bias[:], scale=SCALE).then_inc(expS, 1)
                    s.wait_ge(pvS, (u + 1) * NG)
                    if u >= 1:
                        s.wait_ge(trC, u)
                    s.copy(ctmp[:], pcx[:]).then_inc(evC, 1)

        @block.vector
        def _(v):
            v.memset(onr[:], 1.0)
            v.memset(onc[:], 1.0)
            for j in range(4):
                v.wait_ge(smpT, j + 1)
                ssV.mark(v.reduce_max(mraw[:, 0:1], psc[0][0:1, 0:512], AX.X))
                ssV.wait(v)
                v.sem_inc(smpV, 1)
                v.wait_ge(bcP, 32 * (j + 1))
                if j >= 1:
                    v.wait_ge(expS, 512 * j)
                v.tensor_scalar(bias[:], mbb[:], -SCALE, None, ALU.mult
                                ).then_inc(bcV, 1)
                for qc in range(NQC):
                    u = j * NQC + qc
                    v.wait_ge(pvS, (u + 1) * NG)
                    if u >= 1:
                        v.wait_ge(trC, u)
                    ssV.mark(v.reciprocal(rs[:], psm[:]))
                    ssV.wait(v)
                    v.sem_inc(nrmV, 1)
                    v.wait_ge(trC, u + 1)
                    ssV.mark(v.tensor_copy(rcs[:, 0:4], prc[:]))
                    ssV.wait(v)
                    if u >= 2:
                        v.wait_ge(odp[u % 2], 16 * (u // 2))
                    for i in range(4):
                        vv = v.tensor_scalar(stg[:, u % 2, i, :],
                                             ptr[:, i * 128:(i + 1) * 128],
                                             rcs[:, i:i + 1], None, ALU.mult)
                    vv.then_inc(nrmV, 1)

        @block.gpsimd
        def _(g):
            for j in range(4):
                g.wait_ge(smpV, j + 1)
                g.dma_start(out=mbd.ap(), in_=mraw[:, 0:1]).then_inc(bcP, 16)
                g.wait_ge(bcP, 32 * j + 16)
                g.dma_start(out=mbb[:], in_=mbd.ap().broadcast_to((128, 1))
                            ).then_inc(bcP, 16)
                for qc in range(NQC):
                    u = j * NQC + qc
                    g.wait_ge(nrmV, 2 * (u + 1))
                    g.dma_start(
                        out=out[j, qc * 512:(qc + 1) * 512, :].rearrange(
                            "(sp p) c -> p sp c", p=128),
                        in_=stg[:, u % 2, :, :]).then_inc(odp[u % 2], 16)
    return nc


def host_l2_inputs(l1_outs):
    qkvT = np.concatenate(l1_outs, axis=1)
    qT, kT, vT = qkvT[0:D], qkvT[D:2 * D], qkvT[2 * D:3 * D]
    identF = np.eye(128, dtype=np.float32)
    maps = []
    for c in range(C):
        rows = slice(c * 512, (c + 1) * 512)
        vh = np.stack([np.ascontiguousarray(vT[c * 512 + j * 128: c * 512 + (j + 1) * 128, :].T)
                       for j in range(4)])
        maps.append({"qT": np.ascontiguousarray(qT[rows]),
                     "kT": np.ascontiguousarray(kT[rows]),
                     "vh": vh, "identF": identF})
    return maps


def build_l3a():
    nc = bass.Bass("TRN2", target_bir_lowering=False)
    ctxT_in = nc.dram_tensor("ctxT", [D, TL], BF16, kind="ExternalInput").ap()
    ow_in = nc.dram_tensor("ow", [D, D], F32, kind="ExternalInput").ap()
    ob_in = nc.dram_tensor("ob_t", [128, 32], F32, kind="ExternalInput").ap()
    x_in = nc.dram_tensor("x", [TL, D], F32, kind="ExternalInput").ap()
    idf_in = nc.dram_tensor("identF", [128, 128], F32, kind="ExternalInput").ap()
    x1_out = nc.dram_tensor("x1", [TL, D], F32, kind="ExternalOutput").ap()
    ln1_out = nc.dram_tensor("xln1", [TL, D], F32, kind="ExternalOutput").ap()

    NDB = D // 128
    NCBG = D // 512

    with ExitStack() as ctx:
        sb = lambda n, sh, dt: ctx.enter_context(nc.sbuf_tensor(n, sh, dt))
        ps = lambda n, sh, dt: ctx.enter_context(nc.psum_tensor(n, sh, dt))
        sem = lambda n: ctx.enter_context(nc.semaphore(n))
        x1b = sb("x1b", [128, 4, D], F32)
        ctxb = sb("ctxb", [128, NDB, 512], BF16)
        wst = sb("wst", [128, 2, NDB, 512], BF16)
        aotmp = sb("aotmp", [128, 512], F32)
        obt = sb("obt", [128, 32], F32)
        idf = sb("idf", [128, 128], F32)
        sq = sb("sq", [128, D], F32)
        stt = sb("stt", [128, 2, 8], F32)
        cst = sb("cst", [128, 1], F32)
        ln1t = sb("ln1t", [128, 1, D], F32)
        pmm = [ps(f"pmm{i}", [128, 512], F32) for i in range(4)]
        ptr = [ps(f"ptr{i}", [128, 512], F32) for i in range(2)]
        ldc = sem("ldc")
        ldx = [sem("ldx0"), sem("ldx1")]
        ldcx = [sem("ldcx0"), sem("ldcx1")]
        wlp = [sem("wl0"), sem("wl1")]
        odx = [sem("odx0"), sem("odx1")]
        odl = [sem("odl0"), sem("odl1")]
        mmS = sem("mmS"); evA = sem("evA"); trS = sem("trS"); adV = sem("adV")
        cpS = sem("cpS"); cpV1 = sem("cpV1"); cpV = sem("cpV")
        ssS = SelfSem(sem("ssS")); ssV = SelfSem(sem("ssV"))
        block = ctx.enter_context(nc.Block())


        @block.sync
        def _(s):
            s.dma_start(out=idf[:], in_=idf_in).then_inc(ldc, 16)
            s.dma_start(out=obt[:], in_=ob_in).then_inc(ldc, 16)
            for h in range(2):
                if h >= 1:
                    s.wait_ge(cpV, 4 * h)
                    s.wait_ge(odx[h - 1], 64)
                for tt in range(4):
                    s.dma_start(out=x1b[:, tt, :],
                                in_=x_in[(h * 4 + tt) * 128:(h * 4 + tt + 1) * 128, :]
                                ).then_inc(ldx[h], 16)
                s.dma_start(out=ctxb[:],
                            in_=ctxT_in[:, h * 512:(h + 1) * 512].rearrange(
                                "(db p) t -> p db t", p=128)).then_inc(ldcx[h], 16)
            s.wait_ge(odx[0], 64); s.wait_ge(odx[1], 64)
            s.wait_ge(odl[0], 128)

        @block.gpsimd
        def _(g):
            for h in range(2):
                for cbg in range(NCBG):
                    k = h * NCBG + cbg
                    if k >= 2:
                        g.wait_ge(mmS, (k - 1) * 4)
                    src = ow_in[:, cbg * 512:(cbg + 1) * 512].rearrange(
                        "(db p) c -> p db c", p=128)
                    for q in range(4):
                        g.dma_start(out=wst[:, k % 2, q * 8:(q + 1) * 8, :],
                                    in_=src[:, q * 8:(q + 1) * 8, :]
                                    ).then_inc(wlp[k % 2], 16)
                g.wait_ge(adV, 128 * (h + 1))
                for tt in range(4):
                    gt = h * 4 + tt
                    g.dma_start(out=x1_out[gt * 128:(gt + 1) * 128, :],
                                in_=x1b[:, tt, :]).then_inc(odx[h], 16)
                    g.wait_ge(cpV, gt + 1)
                    g.dma_start(out=ln1_out[gt * 128:(gt + 1) * 128, :],
                                in_=ln1t[:, 0, :]).then_inc(odl[0], 16)

        @block.tensor
        def _(t):
            t.wait_ge(ldc, 32)
            for h in range(2):
                t.wait_ge(ldcx[h], 16)
                for cbg in range(NCBG):
                    k = h * NCBG + cbg
                    t.wait_ge(wlp[k % 2], 64 * (k // 2 + 1))
                    for cb in range(4):
                        u2 = cbg * 4 + cb
                        u = h * 32 + u2
                        if u >= 4:
                            t.wait_ge(evA, u - 3)
                        for db in range(NDB):
                            mm = t.matmul(pmm[u % 4][:],
                                          wst[:, k % 2, db, cb * 128:(cb + 1) * 128],
                                          ctxb[:, db, :],
                                          start=(db == 0), stop=(db == NDB - 1))
                        mm.then_inc(mmS, 1)
                        t.wait_ge(evA, u + 1)
                        if u >= 2:
                            t.wait_ge(adV, 4 * (u - 1))
                        for i in range(4):
                            mm = t.transpose(ptr[u % 2][:, i * 128:(i + 1) * 128],
                                             aotmp[:, i * 128:(i + 1) * 128], idf[:])
                        mm.then_inc(trS, 1)

        @block.scalar
        def _(s):
            for h in range(2):
                for u2 in range(32):
                    u = h * 32 + u2
                    cbg, cb = divmod(u2, 4)
                    s.wait_ge(mmS, u + 1)
                    if u >= 1:
                        s.wait_ge(trS, u)
                    s.activation(aotmp[:], pmm[u % 4][:], AF.Identity,
                                 bias=obt[:, cbg * 4 + cb:cbg * 4 + cb + 1],
                                 scale=1.0).then_inc(evA, 1)
                for tt in range(4):
                    gt = h * 4 + tt
                    s.wait_ge(adV, 128 * (h + 1))
                    s.wait_ge(cpV1, gt + 1)
                    b = stt[:, gt % 2, :]
                    ssS.mark(s.activation(sq[:], x1b[:, tt, :], AF.Square,
                                          accum_out=b[:, 1:2]))
                    ssS.wait(s)
                    ssS.mark(s.mul(b[:, 2:3], b[:, 0:1], 1.0 / D))
                    ssS.mark(s.mul(b[:, 3:4], b[:, 1:2], 1.0 / D))
                    ssS.wait(s)
                    ssS.mark(s.activation(b[:, 4:5], b[:, 2:3], AF.Square))
                    ssS.wait(s)
                    ssS.mark(s.mul(b[:, 4:5], b[:, 4:5], -1.0))
                    ssS.wait(s)
                    ssS.mark(s.add(b[:, 4:5], b[:, 3:4], b[:, 4:5]))
                    ssS.wait(s)
                    ssS.mark(s.activation(b[:, 5:6], b[:, 4:5], AF.Ln,
                                          bias=cst[:, 0:1], scale=1.0))
                    ssS.wait(s)
                    s.activation(b[:, 6:7], b[:, 5:6], AF.Exp, bias=0.0, scale=-0.5
                                 ).then_inc(cpS, 1)

        @block.vector
        def _(v):
            v.memset(cst[:], LN_EPS)
            for h in range(2):
                for u2 in range(32):
                    u = h * 32 + u2
                    cbg, cb = divmod(u2, 4)
                    v.wait_ge(trS, u + 1)
                    v.wait_ge(ldx[h], 64)
                    oc = cbg * 512 + cb * 128
                    for i in range(4):
                        vv = v.tensor_tensor(x1b[:, i, oc:oc + 128],
                                             x1b[:, i, oc:oc + 128],
                                             ptr[u % 2][:, i * 128:(i + 1) * 128],
                                             ALU.add)
                    vv.then_inc(adV, 4)
                for tt in range(4):
                    gt = h * 4 + tt
                    v.wait_ge(adV, 128 * (h + 1))
                    b = stt[:, gt % 2, :]
                    ssV.mark(v.reduce_sum(b[:, 0:1], x1b[:, tt, :], AX.X))
                    ssV.wait(v)
                    v.sem_inc(cpV1, 1)
                    v.wait_ge(cpS, gt + 1)
                    if gt >= 1:
                        v.wait_ge(odl[0], 16 * gt)
                    ssV.mark(v.tensor_tensor(b[:, 7:8], b[:, 2:3], b[:, 6:7], ALU.mult))
                    ssV.wait(v)
                    v.tensor_scalar(ln1t[:, 0, :], x1b[:, tt, :],
                                    b[:, 6:7], b[:, 7:8], ALU.mult, ALU.subtract
                                    ).then_inc(cpV, 1)

    return nc


def host_l3a_inputs(inputs, l2_ctx):
    x = np.ascontiguousarray(inputs["L0_attn_x"].reshape(T, D))
    ow = inputs["L0_attn_attn_ow"]
    obt = np.ascontiguousarray(inputs["L0_attn_attn_ob"].reshape(32, 128).T)
    identF = np.eye(128, dtype=np.float32)
    ctxT = np.ascontiguousarray(l2_ctx.T)
    return [
        {"ctxT": np.ascontiguousarray(ctxT[:, c * TL:(c + 1) * TL]),
         "ow": ow, "ob_t": obt,
         "x": np.ascontiguousarray(x[c * TL:(c + 1) * TL]),
         "identF": identF}
        for c in range(C)
    ]

SINK_ITERS = 50
SINK_TAU = 0.1
RMS_EPS0 = 1.1920929e-07


def build_l3b():
    nc = bass.Bass("TRN2", target_bir_lowering=False)
    x1_in = nc.dram_tensor("x1", [TL, D], F32, kind="ExternalInput").ap()
    ln1_in = nc.dram_tensor("xln1", [TL, D], F32, kind="ExternalInput").ap()
    wres_in = nc.dram_tensor("wres_t", [D, 24], F32, kind="ExternalInput").ap()
    b24_in = nc.dram_tensor("base24", [24, 1], F32, kind="ExternalInput").ap()
    gate_in = nc.dram_tensor("gate", [1, 1], F32, kind="ExternalInput").ap()
    wbr_in = nc.dram_tensor("wbr", [1024, 1024], F32, kind="ExternalInput").ap()
    idb_in = nc.dram_tensor("ident", [128, 128], BF16, kind="ExternalInput").ap()
    idf_in = nc.dram_tensor("identF", [128, 128], F32, kind="ExternalInput").ap()
    xb_out = nc.dram_tensor("xb", [TL, D], F32, kind="ExternalOutput").ap()
    ln2T_out = nc.dram_tensor("xln2T", [D, TL], BF16, kind="ExternalOutput").ap()

    NDB = D // 128
    NTT = 8
    G = NTT

    sink_steps = []

    def VS(eng):
        def deco(fn):
            sink_steps.append((eng, fn))
            return fn
        return deco

    with ExitStack() as ctx:
        sb = lambda n, sh, dt: ctx.enter_context(nc.sbuf_tensor(n, sh, dt))
        ps = lambda n, sh, dt: ctx.enter_context(nc.psum_tensor(n, sh, dt))
        sem = lambda n: ctx.enter_context(nc.semaphore(n))
        xnT = sb("xnT", [128, NDB, TL], BF16)
        xl1 = sb("xl1", [128, D], F32)
        x1t = sb("x1t", [128, D], F32)
        xbt = sb("xbt", [128, D], F32)
        sq = sb("sq", [128, D], F32)
        tokbf = sb("tokbf", [128, D], BF16)
        stt = sb("stt", [128, 2, 8], F32)
        cst = sb("cst", [128, 2], F32)
        wresb = sb("wresb", [128, NDB, 24], BF16)
        b24 = sb("b24", [24, 1], F32)
        gsb = sb("gsb", [128, 2], F32)
        lsb = sb("lsb", [24, TL], F32)
        ltok = sb("ltok", [128, G, 24], F32)
        Zp = sb("Zp", [128, G, 4, 4], F32)
        t2b = sb("t2b", [128, G, 4, 4], F32)
        escr = sb("escr", [128, G, 4, 4], F32)
        ssb = sb("ssb", [128, G, 4], F32)
        lub = sb("lub", [128, G, 4], F32)
        lvb = sb("lvb", [128, G, 4], F32)
        Mb = sb("Mb", [128, G], F32)
        rcb = sb("rcb", [128, G], F32)
        Hres = sb("Hres", [128, G, 4, 4], F32)
        Hpre = sb("Hpre", [128, G, 4], F32)
        Hpost = sb("Hpost", [128, G, 4], F32)
        sftm = sb("sftm", [128, G, 4], F32)
        xpre = sb("xpre", [128, 2, 1024], F32)
        wbrb = sb("wbrb", [128, 8, 1024], BF16)
        xpreT = sb("xpreT", [128, 8, TL], BF16)
        ysb = sb("ysb", [128, 8, 512], F32)
        idb = sb("idb", [128, 128], BF16)
        idf = sb("idf", [128, 128], F32)
        ptrx = [ps(f"ptrx{i}", [128, 1024], BF16) for i in range(2)]
        plog = ps("plog", [24, 512], F32)
        ptr24 = ps("ptr24", [128, 24], F32)
        pwb = ps("pwb", [128, 512], F32)
        ptry = ps("ptry", [128, 1024], F32)
        ldc = sem("ldc"); ldg = sem("ldg")
        lxd = sem("lxd"); l1d = sem("l1d")
        odb = sem("odb"); odz = sem("odz")
        xf = sem("xf"); x1f = sem("x1f")
        cpS = sem("cpS"); cpV1 = sem("cpV1"); cpV2 = sem("cpV2"); cstR = sem("cstR")
        trX = sem("trX"); evX = sem("evX")
        mmL = sem("mmL"); evL = sem("evL"); trL = sem("trL"); ltV = sem("ltV")
        snkS = sem("snkS"); snkV = sem("snkV")
        xpV = sem("xpV"); trP = sem("trP"); evP = sem("evP")
        mmW = sem("mmW"); evW = sem("evW"); trY = sem("trY"); yuV = sem("yuV")
        cpB = sem("cpB"); trZ = sem("trZ"); evZ = sem("evZ")
        ssS = SelfSem(sem("ssS")); ssV = SelfSem(sem("ssV"))
        block = ctx.enter_context(nc.Block())

        bc44_j = lambda a: a.rearrange("p g j -> p g () j").broadcast_to((128, G, 4, 4))
        bc44_i = lambda a: a.rearrange("p g i -> p g i ()").broadcast_to((128, G, 4, 4))
        bc4 = lambda a: a.rearrange("p g -> p g ()").broadcast_to((128, G, 4))

        @VS('V')
        def _(v):
            ssV.mark(v.reduce_max(Mb[:], ltok[:, :, 0:16], AX.X))
            ssV.wait(v)
            return v.tensor_tensor(t2b[:], ltok[:, :, 0:16].rearrange(
                "p g (i j) -> p g i j", i=4),
                Mb[:].rearrange("p g -> p g ()").rearrange("p g o -> p g o ()").broadcast_to((128, G, 4, 4)),
                ALU.subtract)

        @VS('V')
        def _(v):
            return v.tensor_scalar(Zp[:], t2b[:], 1.0 / SINK_TAU, None, ALU.mult)

        for _it in range(SINK_ITERS):
            if _it == 0:
                @VS('S')
                def _(s):
                    return s.activation(escr[:], Zp[:], AF.Exp)
            else:
                @VS('V')
                def _(v):
                    return v.tensor_tensor(t2b[:], Zp[:], bc44_j(lvb[:]), ALU.subtract)

                @VS('S')
                def _(s):
                    return s.activation(escr[:], t2b[:], AF.Exp)

            @VS('V')
            def _(v):
                return v.reduce_sum(ssb[:], escr[:], AX.X)

            @VS('S')
            def _(s):
                return s.activation(lub[:], ssb[:], AF.Ln)

            @VS('V')
            def _(v):
                return v.tensor_tensor(t2b[:], Zp[:], bc44_i(lub[:]), ALU.subtract)

            @VS('S')
            def _(s):
                return s.activation(escr[:], t2b[:], AF.Exp)

            @VS('V')
            def _(v):
                return v.reduce_sum(ssb[:], escr[:].rearrange("p g i j -> p g j i"),
                                    AX.X)

            @VS('S')
            def _(s):
                return s.activation(lvb[:], ssb[:], AF.Ln)

        @VS('V')
        def _(v):
            return v.tensor_tensor(t2b[:], t2b[:], bc44_j(lvb[:]), ALU.subtract)

        @VS('S')
        def _(s):
            return s.activation(Hres[:], t2b[:], AF.Exp)

        @VS('V')
        def _(v):
            ssV.mark(v.tensor_scalar(Hres[:], Hres[:], gsb[:, 0:1], None, ALU.mult))
            ssV.wait(v)
            d16 = Hres[:].rearrange("p g i j -> p g (i j)")
            return v.tensor_scalar(d16[:, :, 0:16:5], d16[:, :, 0:16:5],
                                   gsb[:, 1:2], None, ALU.add)

        for _lo, _H in ((16, Hpre), (20, Hpost)):
            def _mk(lo, H):
                @VS('V')
                def _(v):
                    ssV.mark(v.reduce_max(Mb[:], ltok[:, :, lo:lo + 4], AX.X))
                    ssV.wait(v)
                    return v.tensor_tensor(sftm[:], ltok[:, :, lo:lo + 4],
                                           bc4(Mb[:]), ALU.subtract)

                @VS('S')
                def _(s):
                    return s.activation(escr[:, :, 0, :], sftm[:], AF.Exp)

                @VS('V')
                def _(v):
                    ssV.mark(v.reduce_sum(Mb[:], escr[:, :, 0, :], AX.X))
                    ssV.wait(v)
                    ssV.mark(v.reciprocal(rcb[:], Mb[:]))
                    ssV.wait(v)
                    return v.tensor_tensor(H[:], escr[:, :, 0, :], bc4(rcb[:]),
                                           ALU.mult)
            _mk(_lo, _H)

        n_sinkV = sum(1 for e, _ in sink_steps if e == 'V')
        n_sinkS = sum(1 for e, _ in sink_steps if e == 'S')

        @block.sync
        def _(s):
            s.dma_start(out=idb[:], in_=idb_in).then_inc(ldc, 16)
            s.dma_start(out=idf[:], in_=idf_in).then_inc(ldc, 16)
            s.dma_start(out=b24[:], in_=b24_in).then_inc(ldc, 16)
            s.dma_start(out=gsb[:, 0:1], in_=gate_in.broadcast_to((128, 1))
                        ).then_inc(ldc, 16)
            for i in range(16):
                gt = i % 8
                if i >= 1:
                    s.wait_ge(xf, i)
                s.dma_start(out=xl1[:], in_=ln1_in[gt * 128:(gt + 1) * 128, :]
                            ).then_inc(lxd, 16)
            for gt in range(8):
                if gt >= 1:
                    s.wait_ge(x1f, gt)
                s.dma_start(out=x1t[:], in_=x1_in[gt * 128:(gt + 1) * 128, :]
                            ).then_inc(l1d, 16)
                s.wait_ge(xf, 16 + gt)
                s.dma_start(out=xl1[:], in_=ln1_in[gt * 128:(gt + 1) * 128, :]
                            ).then_inc(lxd, 16)
            s.wait_ge(odb, NTT * 16)
            s.wait_ge(odz, 64)

        @block.gpsimd
        def _(g):
            g.dma_start(out=wresb[:], in_=wres_in.rearrange("(db p) e -> p db e", p=128)
                        ).then_inc(ldg, 16)
            g.dma_start(out=wbrb[:], in_=wbr_in.rearrange("(db p) c -> p db c", p=128)
                        ).then_inc(ldg, 16)
            for gt in range(NTT):
                g.wait_ge(cpB, gt + 1)
                g.dma_start(out=xb_out[gt * 128:(gt + 1) * 128, :], in_=xbt[:]
                            ).then_inc(odb, 16)
            g.wait_ge(evZ, 32)
            ln2T_r = ln2T_out.rearrange("(db p) t -> p db t", p=128)
            for q in range(4):
                g.dma_start(out=ln2T_r[:, q * 8:(q + 1) * 8, :],
                            in_=xnT[:, q * 8:(q + 1) * 8, :]).then_inc(odz, 16)

        @block.tensor
        def _(t):
            t.wait_ge(ldc, 64)
            for gt in range(NTT):
                t.wait_ge(xf, gt + 1)
                for k in range(4):
                    pk = gt * 4 + k
                    if pk >= 2:
                        t.wait_ge(evX, pk - 1)
                    for j in range(8):
                        db = k * 8 + j
                        mm = t.transpose(ptrx[pk % 2][:, j * 128:(j + 1) * 128],
                                         tokbf[:, db * 128:(db + 1) * 128], idb[:])
                    mm.then_inc(trX, 1)
            t.wait_ge(evX, 32)
            t.wait_ge(ldg, 32)
            for tch in range(2):
                if tch == 1:
                    t.wait_ge(evL, 1)
                for db in range(NDB):
                    mm = t.matmul(plog[:], wresb[:, db, :],
                                  xnT[:, db, tch * 512:(tch + 1) * 512],
                                  start=(db == 0), stop=(db == NDB - 1))
                mm.then_inc(mmL, 1)
            for tt in range(NTT):
                t.wait_ge(evL, 2)
                if tt >= 1:
                    t.wait_ge(ltV, tt)
                mm = t.transpose(ptr24[:], lsb[0:24, tt * 128:(tt + 1) * 128],
                                 idf[0:24, 0:24])
                mm.then_inc(trL, 1)
            for gt in range(NTT):
                t.wait_ge(xpV, gt + 1)
                if gt >= 2:
                    t.wait_ge(evP, gt - 1)
                for j in range(8):
                    mm = t.transpose(ptrx[gt % 2][:, j * 128:(j + 1) * 128],
                                     tokbf[:, j * 128:(j + 1) * 128], idb[:])
                mm.then_inc(trP, 1)
            t.wait_ge(ldg, 32)
            for tch in range(2):
                t.wait_ge(evP, 8)
                for ob in range(8):
                    u = tch * 8 + ob
                    if u >= 1:
                        t.wait_ge(evW, u)
                    for db in range(8):
                        mm = t.matmul(pwb[:], wbrb[:, db, ob * 128:(ob + 1) * 128],
                                      xpreT[:, db, tch * 512:(tch + 1) * 512],
                                      start=(db == 0), stop=(db == 7))
                    mm.then_inc(mmW, 1)
                for tl in range(4):
                    gt = tch * 4 + tl
                    if gt >= 1:
                        zt = gt - 1
                        t.wait_ge(cpV2, zt + 1)
                        for k in range(4):
                            pk = zt * 4 + k
                            t.wait_ge(evZ, max(0, pk - 1))
                            for j in range(8):
                                db = k * 8 + j
                                mm = t.transpose(
                                    ptrx[pk % 2][:, j * 128:(j + 1) * 128],
                                    tokbf[:, db * 128:(db + 1) * 128], idb[:])
                            mm.then_inc(trZ, 1)
                    t.wait_ge(evW, (tch + 1) * 8)
                    if gt >= 1:
                        t.wait_ge(yuV, gt)
                    for j in range(8):
                        mm = t.transpose(ptry[:, j * 128:(j + 1) * 128],
                                         ysb[:, j, tl * 128:(tl + 1) * 128], idf[:])
                    mm.then_inc(trY, 1)
            zt = NTT - 1
            t.wait_ge(cpV2, zt + 1)
            for k in range(4):
                pk = zt * 4 + k
                t.wait_ge(evZ, max(0, pk - 1))
                for j in range(8):
                    db = k * 8 + j
                    mm = t.transpose(ptrx[pk % 2][:, j * 128:(j + 1) * 128],
                                     tokbf[:, db * 128:(db + 1) * 128], idb[:])
                mm.then_inc(trZ, 1)

        @block.scalar
        def _(s):
            s.wait_ge(cstR, 1)
            for gt in range(NTT):
                s.wait_ge(lxd, 16 * (gt + 1))
                b = stt[:, gt % 2, :]
                ssS.mark(s.activation(sq[:], xl1[:], AF.Square, accum_out=b[:, 1:2]))
                ssS.wait(s)
                ssS.mark(s.mul(b[:, 3:4], b[:, 1:2], 1.0 / D))
                ssS.wait(s)
                ssS.mark(s.activation(b[:, 5:6], b[:, 3:4], AF.Ln,
                                      bias=cst[:, 1:2], scale=1.0))
                ssS.wait(s)
                s.activation(b[:, 6:7], b[:, 5:6], AF.Exp, bias=0.0, scale=-0.5
                             ).then_inc(cpS, 1)
                for k in range(4):
                    pk = gt * 4 + k
                    s.wait_ge(trX, pk + 1)
                    s.copy(xnT[:, k * 8:(k + 1) * 8, gt * 128:(gt + 1) * 128],
                           ptrx[pk % 2][:].rearrange("p (a b) -> p a b", a=8)
                           ).then_inc(evX, 1)
            s.wait_ge(ldc, 64)
            ssS.mark(s.activation(gsb[:, 0:1], gsb[:, 0:1], AF.Sigmoid))
            ssS.wait(s)
            s.sem_inc(snkS, 1)
            for tch in range(2):
                s.wait_ge(mmL, tch + 1)
                s.activation(lsb[:, tch * 512:(tch + 1) * 512], plog[:], AF.Identity,
                             bias=b24[:], scale=1.0).then_inc(evL, 1)
            sv = 0
            for eng, fn in sink_steps:
                if eng == 'V':
                    sv += 1
                else:
                    s.wait_ge(snkV, sv)
                    inst = fn(s)
                    ssS.mark(inst)
                    ssS.wait(s)
                    s.sem_inc(snkS, 1)
            for gt in range(NTT):
                s.wait_ge(trP, gt + 1)
                s.copy(xpreT[:, :, gt * 128:(gt + 1) * 128],
                       ptrx[gt % 2][:].rearrange("p (a b) -> p a b", a=8)
                       ).then_inc(evP, 1)
            for gt in range(-1, NTT):
                if gt in (-1, 3):
                    tch = 0 if gt == -1 else 1
                    for ob in range(8):
                        u = tch * 8 + ob
                        s.wait_ge(mmW, u + 1)
                        s.copy(ysb[:, ob, :], pwb[:]).then_inc(evW, 1)
                if gt < 0:
                    continue
                s.wait_ge(cpB, gt + 1)
                b = stt[:, gt % 2, :]
                ssS.mark(s.activation(sq[:], xbt[:], AF.Square, accum_out=b[:, 1:2]))
                ssS.wait(s)
                s.wait_ge(cpV1, gt + 1)
                ssS.mark(s.mul(b[:, 2:3], b[:, 0:1], 1.0 / D))
                ssS.mark(s.mul(b[:, 3:4], b[:, 1:2], 1.0 / D))
                ssS.wait(s)
                ssS.mark(s.activation(b[:, 4:5], b[:, 2:3], AF.Square))
                ssS.wait(s)
                ssS.mark(s.mul(b[:, 4:5], b[:, 4:5], -1.0))
                ssS.wait(s)
                ssS.mark(s.add(b[:, 4:5], b[:, 3:4], b[:, 4:5]))
                ssS.wait(s)
                ssS.mark(s.activation(b[:, 5:6], b[:, 4:5], AF.Ln,
                                      bias=cst[:, 0:1], scale=1.0))
                ssS.wait(s)
                s.activation(b[:, 6:7], b[:, 5:6], AF.Exp, bias=0.0, scale=-0.5
                             ).then_inc(cpS, 1)
                for k in range(4):
                    s.wait_ge(trZ, gt * 4 + k + 1)
                    s.copy(xnT[:, k * 8:(k + 1) * 8, gt * 128:(gt + 1) * 128],
                           ptrx[(gt * 4 + k) % 2][:].rearrange(
                               "p (a b) -> p a b", a=8)).then_inc(evZ, 1)

        @block.vector
        def _(v):
            ssV.mark(v.memset(cst[:, 0:1], LN_EPS))
            ssV.mark(v.memset(cst[:, 1:2], RMS_EPS0))
            ssV.wait(v)
            v.sem_inc(cstR, 1)
            for gt in range(NTT):
                v.wait_ge(cpS, gt + 1)
                b = stt[:, gt % 2, :]
                if gt >= 1:
                    v.wait_ge(trX, gt * 4)
                v.tensor_scalar(tokbf[:], xl1[:], b[:, 6:7], None, ALU.mult
                                ).then_inc(xf, 1)
            v.wait_ge(snkS, 1)
            ssV.mark(v.tensor_scalar(gsb[:, 1:2], gsb[:, 0:1], -1.0, 1.0,
                                     ALU.mult, ALU.add))
            ssV.wait(v)
            for tt in range(NTT):
                v.wait_ge(trL, tt + 1)
                ssV.mark(v.tensor_copy(ltok[:, tt, :], ptr24[:]))
                ssV.wait(v)
                v.sem_inc(ltV, 1)
            ss_count = 1
            for eng, fn in sink_steps:
                if eng == 'S':
                    ss_count += 1
                else:
                    v.wait_ge(snkS, ss_count)
                    inst = fn(v)
                    ssV.mark(inst)
                    ssV.wait(v)
                    v.sem_inc(snkV, 1)
            for gt in range(NTT):
                v.wait_ge(lxd, 16 * (NTT + gt + 1))
                h = lambda n: Hpre[:, gt, n:n + 1]
                ssV.mark(v.tensor_scalar(xpre[:, 0, :], xl1[:, 0:1024], h(0),
                                         None, ALU.mult))
                ssV.wait(v)
                ssV.mark(v.scalar_tensor_tensor(xpre[:, 1, :], xl1[:, 1024:2048],
                                                h(1), xpre[:, 0, :], ALU.mult, ALU.add))
                ssV.wait(v)
                ssV.mark(v.scalar_tensor_tensor(xpre[:, 0, :], xl1[:, 2048:3072],
                                                h(2), xpre[:, 1, :], ALU.mult, ALU.add))
                ssV.wait(v)
                ssV.mark(v.scalar_tensor_tensor(xpre[:, 1, :], xl1[:, 3072:4096],
                                                h(3), xpre[:, 0, :], ALU.mult, ALU.add))
                ssV.wait(v)
                v.sem_inc(xf, 1)
                if gt == 0:
                    v.wait_ge(trX, 32)
                else:
                    v.wait_ge(trP, gt)
                v.tensor_copy(tokbf[:, 0:1024], xpre[:, 1, :]).then_inc(xpV, 1)
            for tch in range(2):
                for tl in range(4):
                    gt = tch * 4 + tl
                    v.wait_ge(lxd, 16 * (2 * NTT + gt + 1))
                    v.wait_ge(trY, gt + 1)
                    if gt >= 1:
                        v.wait_ge(odb, 16 * gt)
                    for i in range(4):
                        dst = xbt[:, i * 1024:(i + 1) * 1024]
                        hs = lambda jj: Hres[:, gt, i, jj:jj + 1]
                        ssV.mark(v.tensor_scalar(dst, xl1[:, 0:1024], hs(0),
                                                 None, ALU.mult))
                        ssV.wait(v)
                        for jj in range(1, 4):
                            ssV.mark(v.scalar_tensor_tensor(
                                dst, xl1[:, jj * 1024:(jj + 1) * 1024], hs(jj),
                                dst, ALU.mult, ALU.add))
                            ssV.wait(v)
                    v.sem_inc(xf, 1)
                    for i in range(4):
                        dst = xbt[:, i * 1024:(i + 1) * 1024]
                        ssV.mark(v.scalar_tensor_tensor(
                            dst, ptry[:], Hpost[:, gt, i:i + 1], dst,
                            ALU.mult, ALU.add))
                        ssV.wait(v)
                    v.sem_inc(yuV, 1)
                    v.wait_ge(l1d, 16 * (gt + 1))
                    ssV.mark(v.tensor_tensor(xbt[:], xbt[:], x1t[:], ALU.add))
                    ssV.wait(v)
                    v.sem_inc(x1f, 1)
                    v.sem_inc(cpB, 1)
                    b = stt[:, gt % 2, :]
                    ssV.mark(v.reduce_sum(b[:, 0:1], xbt[:], AX.X))
                    ssV.wait(v)
                    v.sem_inc(cpV1, 1)
                    v.wait_ge(cpS, NTT + gt + 1)
                    if gt == 0:
                        v.wait_ge(trP, 8)
                    else:
                        v.wait_ge(trZ, gt * 4)
                    ssV.mark(v.tensor_tensor(b[:, 7:8], b[:, 2:3], b[:, 6:7],
                                             ALU.mult))
                    ssV.wait(v)
                    v.tensor_scalar(tokbf[:], xbt[:], b[:, 6:7], b[:, 7:8],
                                    ALU.mult, ALU.subtract).then_inc(cpV2, 1)
    return nc


def host_l3b_inputs(inputs, x1_full, ln1_full):
    wres = np.concatenate([inputs["L0_ffn_W_res"], inputs["L0_ffn_W_pre"],
                           inputs["L0_ffn_W_post"]], 0)
    wres_t = np.ascontiguousarray(wres.T)
    b24 = np.concatenate([inputs["L0_ffn_H_res_base"].reshape(16),
                          inputs["L0_ffn_H_pre_base"],
                          inputs["L0_ffn_H_post_base"]]).reshape(24, 1)
    b24 = np.ascontiguousarray(b24.astype(np.float32))
    gate = inputs["L0_ffn_gate"].reshape(1, 1)
    wbr = inputs["L0_ffn_W_branch"]
    ident = np.eye(128, dtype=BF)
    identF = np.eye(128, dtype=np.float32)
    return [
        {"x1": np.ascontiguousarray(x1_full[c * TL:(c + 1) * TL]),
         "xln1": np.ascontiguousarray(ln1_full[c * TL:(c + 1) * TL]),
         "wres_t": wres_t, "base24": b24, "gate": gate, "wbr": wbr,
         "ident": ident, "identF": identF}
        for c in range(C)
    ]


DIN = 8192
THL = TL + 3


def build_l4a():
    nc = bass.Bass("TRN2", target_bir_lowering=False)
    xlh_in = nc.dram_tensor("xlh", [D, THL], BF16, kind="ExternalInput").ap()
    w_in = nc.dram_tensor("inw", [D, 2 * DIN], F32, kind="ExternalInput").ap()
    xp_out = nc.dram_tensor("xpart", [DIN, THL], F32, kind="ExternalOutput").ap()
    sz_out = nc.dram_tensor("sz", [DIN, TL], F32, kind="ExternalOutput").ap()

    NDB = D // 128
    NCBG = 2 * DIN // 512

    units = []
    for cbg in range(NCBG):
        is_z = cbg >= 16
        for cb in range(4):
            chunks = ((3, 512), (515, 512)) if is_z else ((0, 512), (512, 512), (1024, 3))
            for (lo, n) in chunks:
                units.append((cbg, cb, lo, n, is_z))
    NU = len(units)

    with ExitStack() as ctx:
        sb = lambda n, sh, dt: ctx.enter_context(nc.sbuf_tensor(n, sh, dt))
        ps = lambda n, sh, dt: ctx.enter_context(nc.psum_tensor(n, sh, dt))
        sem = lambda n: ctx.enter_context(nc.semaphore(n))
        xlh = sb("xlh_s", [128, NDB, THL], BF16)
        wst = sb("wst", [128, 2, NDB, 512], BF16)
        stg = sb("stg", [128, 4, 512], F32)
        pin = [ps(f"pin{i}", [128, 512], F32) for i in range(4)]
        ldx = sem("ldx")
        wlp = [sem("wl0"), sem("wl1")]
        odp = [sem("od0"), sem("od1"), sem("od2"), sem("od3")]
        mmS = sem("mmS"); evS = sem("evS")
        block = ctx.enter_context(nc.Block())

        od_cnt = [0, 0, 0, 0]
        od_sched = []
        for i in range(NU):
            par = i % 4
            od_sched.append((par, od_cnt[par]))
            od_cnt[par] += 1

        @block.sync
        def _(s):
            for q in range(4):
                s.dma_start(out=xlh[:, q * 8:(q + 1) * 8, :],
                            in_=xlh_in.rearrange("(db p) t -> p db t", p=128)
                            [:, q * 8:(q + 1) * 8, :]).then_inc(ldx, 16)
            for par in range(4):
                s.wait_ge(odp[par], 16 * od_cnt[par])

        @block.gpsimd
        def _(g):
            for cbg in range(NCBG):
                if cbg >= 2:
                    done = sum(1 for u in units if u[0] < cbg - 1)
                    g.wait_ge(mmS, done)
                src = w_in[:, cbg * 512:(cbg + 1) * 512].rearrange(
                    "(db p) c -> p db c", p=128)
                for q in range(4):
                    g.dma_start(out=wst[:, cbg % 2, q * 8:(q + 1) * 8, :],
                                in_=src[:, q * 8:(q + 1) * 8, :]
                                ).then_inc(wlp[cbg % 2], 16)
                if cbg >= 1:
                    for i, (c2, cb, lo, n, is_z) in enumerate(units):
                        if c2 != cbg - 1:
                            continue
                        par, cnt = od_sched[i]
                        g.wait_ge(evS, i + 1)
                        row0 = (c2 - 16 if is_z else c2) * 512 + cb * 128
                        dst = (sz_out[row0:row0 + 128, lo - 3:lo - 3 + n] if is_z
                               else xp_out[row0:row0 + 128, lo:lo + n])
                        g.dma_start(out=dst, in_=stg[:, par, 0:n]
                                    ).then_inc(odp[par], 16)
            for i, (c2, cb, lo, n, is_z) in enumerate(units):
                if c2 != NCBG - 1:
                    continue
                par, cnt = od_sched[i]
                g.wait_ge(evS, i + 1)
                row0 = (c2 - 16 if is_z else c2) * 512 + cb * 128
                dst = (sz_out[row0:row0 + 128, lo - 3:lo - 3 + n] if is_z
                       else xp_out[row0:row0 + 128, lo:lo + n])
                g.dma_start(out=dst, in_=stg[:, par, 0:n]).then_inc(odp[par], 16)

        @block.tensor
        def _(t):
            t.wait_ge(ldx, 64)
            for i, (cbg, cb, lo, n, is_z) in enumerate(units):
                t.wait_ge(wlp[cbg % 2], 64 * (cbg // 2 + 1))
                if i >= 4:
                    t.wait_ge(evS, i - 3)
                for db in range(NDB):
                    mm = t.matmul(pin[i % 4][:, 0:n],
                                  wst[:, cbg % 2, db, cb * 128:(cb + 1) * 128],
                                  xlh[:, db, lo:lo + n],
                                  start=(db == 0), stop=(db == NDB - 1))
                mm.then_inc(mmS, 1)

        @block.scalar
        def _(s):
            for i, (cbg, cb, lo, n, is_z) in enumerate(units):
                par, cnt = od_sched[i]
                s.wait_ge(mmS, i + 1)
                if cnt >= 1:
                    s.wait_ge(odp[par], 16 * cnt)
                s.activation(stg[:, par, 0:n], pin[i % 4][:, 0:n],
                             AF.Silu if is_z else AF.Identity).then_inc(evS, 1)
    return nc


RMS_EPS1 = 1e-6


def build_l4b():
    nc = bass.Bass("TRN2", target_bir_lowering=False)
    xp_in = nc.dram_tensor("xpart", [DIN, THL], F32, kind="ExternalInput").ap()
    cw_in = nc.dram_tensor("convw_t", [128, 64, 4], F32, kind="ExternalInput").ap()
    cb_in = nc.dram_tensor("convb_t", [128, 64], F32, kind="ExternalInput").ap()
    xw_in = nc.dram_tensor("xprojw_t", [DIN, 288], F32, kind="ExternalInput").ap()
    lw_in = nc.dram_tensor("lnw_row", [1, 288], F32, kind="ExternalInput").ap()
    dw_in = nc.dram_tensor("dtprojw_t", [256, DIN], F32, kind="ExternalInput").ap()
    db_in = nc.dram_tensor("dtbias_t", [128, 64], F32, kind="ExternalInput").ap()
    idb_in = nc.dram_tensor("ident", [128, 128], BF16, kind="ExternalInput").ap()
    idf_in = nc.dram_tensor("identF", [128, 128], F32, kind="ExternalInput").ap()
    xc_out = nc.dram_tensor("xc", [DIN, TL], F32, kind="ExternalOutput").ap()
    dt_out = nc.dram_tensor("dt", [DIN, TL], F32, kind="ExternalOutput").ap()
    bc_out = nc.dram_tensor("BC", [128, 8, 32], F32, kind="ExternalOutput").ap()

    NDB2 = DIN // 128

    with ExitStack() as ctx:
        sb = lambda n, sh, dt: ctx.enter_context(nc.sbuf_tensor(n, sh, dt))
        ps = lambda n, sh, dt: ctx.enter_context(nc.psum_tensor(n, sh, dt))
        sem = lambda n: ctx.enter_context(nc.semaphore(n))
        xpc = sb("xpc", [128, 2, 515], F32)
        cc = sb("cc", [128, 2, 512], F32)
        xcc = sb("xcc", [128, 2, 512], F32)
        xcb = sb("xcb", [128, 2, 512], BF16)
        xprw = sb("xprw", [128, NDB2, 288], BF16)
        cwt = sb("cwt", [128, 64, 4], F32)
        cbt = sb("cbt", [128, 64], F32)
        ssp = sb("ssp", [96, 3, TL], F32)
        stok = sb("stok", [128, 8, 288], F32)
        sq2 = sb("sq2", [128, 8, 288], F32)
        rno = sb("rno", [128, 3, 8], F32)
        sss = sb("sss", [128, 3, 8], F32)
        lnwb = sb("lnwb", [128, 288], F32)
        dtrb = sb("dtrb", [128, 8, 256], BF16)
        dtrT = sb("dtrT", [128, 2, TL], BF16)
        dpw = sb("dpw", [128, 2, DIN], BF16)
        dbt = sb("dbt", [128, 64], F32)
        dtst = sb("dtst", [128, 2, 512], F32)
        etmp = sb("etmp", [128, 512], F32)
        cst = sb("cst", [128, 2], F32)
        idb = sb("idb", [128, 128], BF16)
        idf = sb("idf", [128, 128], F32)
        ps_ssm = [ps(f"pssm{i}", [96, 512], F32) for i in range(3)]
        ptr96 = ps("ptr96", [128, 96], F32)
        ptrd = ps("ptrd", [128, 1024], BF16)
        pdt = [ps(f"pdt{i}", [128, 512], F32) for i in range(2)]
        ldc = sem("ldc"); ldg = sem("ldg")
        ldp = [sem("ldp0"), sem("ldp1")]
        odc = [sem("odc0"), sem("odc1")]
        odd = [sem("odd0"), sem("odd1")]
        odbc = sem("odbc")
        cvV = sem("cvV"); slS = sem("slS"); cbV = sem("cbV"); mmX = sem("mmX")
        evM = sem("evM"); tr9 = sem("tr9"); ev9 = sem("ev9"); rmsV = sem("rmsV")
        nrmS = sem("nrmS"); dcV = sem("dcV"); trD = sem("trD"); evD = sem("evD")
        mmD = sem("mmD"); evE = sem("evE")
        ssS = SelfSem(sem("ssS")); ssV = SelfSem(sem("ssV"))
        block = ctx.enter_context(nc.Block())

        @block.sync
        def _(s):
            s.dma_start(out=idb[:], in_=idb_in).then_inc(ldc, 16)
            s.dma_start(out=idf[:], in_=idf_in).then_inc(ldc, 16)
            s.dma_start(out=cwt[:], in_=cw_in).then_inc(ldc, 16)
            s.dma_start(out=cbt[:], in_=cb_in).then_inc(ldc, 16)
            s.dma_start(out=dbt[:], in_=db_in).then_inc(ldc, 16)
            s.dma_start(out=lnwb[:], in_=lw_in.broadcast_to((128, 288))
                        ).then_inc(ldc, 16)
            for tch in range(2):
                for db in range(NDB2):
                    i = tch * NDB2 + db
                    if i >= 2:
                        s.wait_ge(cvV, i - 1)
                    s.dma_start(out=xpc[:, i % 2, :],
                                in_=xp_in[db * 128:(db + 1) * 128,
                                          tch * 512:tch * 512 + 515]
                                ).then_inc(ldp[i % 2], 16)
            s.wait_ge(odc[0], 16 * NDB2); s.wait_ge(odc[1], 16 * NDB2)
            s.wait_ge(odd[0], 16 * NDB2); s.wait_ge(odd[1], 16 * NDB2)
            s.wait_ge(odbc, 16)

        @block.gpsimd
        def _(g):
            g.dma_start(out=xprw[:], in_=xw_in.rearrange("(db p) e -> p db e", p=128)
                        ).then_inc(ldg, 16)
            g.dma_start(out=dpw[:], in_=dw_in.rearrange("(rb p) d -> p rb d", p=128)
                        ).then_inc(ldg, 16)
            for tch in range(2):
                for db in range(NDB2):
                    i = tch * NDB2 + db
                    g.wait_ge(slS, i + 1)
                    g.dma_start(out=xc_out[db * 128:(db + 1) * 128,
                                           tch * 512:(tch + 1) * 512],
                                in_=xcc[:, i % 2, :]).then_inc(odc[i % 2], 16)
            g.wait_ge(rmsV, 2)
            g.dma_start(out=bc_out, in_=stok[:, :, 256:288]).then_inc(odbc, 16)
            for db in range(NDB2):
                for tch in range(2):
                    i = db * 2 + tch
                    g.wait_ge(evE, i + 1)
                    g.dma_start(out=dt_out[db * 128:(db + 1) * 128,
                                           tch * 512:(tch + 1) * 512],
                                in_=dtst[:, i % 2, :]).then_inc(odd[i % 2], 16)

        @block.vector
        def _(v):
            ssV.mark(v.memset(cst[:, 0:1], RMS_EPS1))
            ssV.mark(v.memset(cst[:, 1:2], 1.0))
            ssV.wait(v)
            v.sem_inc(cbV, 1)
            v.wait_ge(ldc, 96)
            for tch in range(2):
                for db in range(NDB2):
                    i = tch * NDB2 + db
                    v.wait_ge(ldp[i % 2], 16 * (i // 2 + 1))
                    w = lambda k: cwt[:, db, k:k + 1]
                    xs = xpc[:, i % 2, :]
                    ssV.mark(v.tensor_scalar(cc[:, 0, :], xs[:, 0:512], w(0),
                                             cbt[:, db:db + 1], ALU.mult, ALU.add))
                    ssV.wait(v)
                    ssV.mark(v.scalar_tensor_tensor(cc[:, 1, :], xs[:, 1:513], w(1),
                                                    cc[:, 0, :], ALU.mult, ALU.add))
                    ssV.wait(v)
                    ssV.mark(v.scalar_tensor_tensor(cc[:, 0, :], xs[:, 2:514], w(2),
                                                    cc[:, 1, :], ALU.mult, ALU.add))
                    ssV.wait(v)
                    ssV.mark(v.scalar_tensor_tensor(cc[:, 1, :], xs[:, 3:515], w(3),
                                                    cc[:, 0, :], ALU.mult, ALU.add))
                    ssV.wait(v)
                    v.sem_inc(cvV, 1)
                    v.wait_ge(slS, i + 1)
                    if i >= 2:
                        v.wait_ge(mmX, 3 * (i - 1))
                    ssV.mark(v.tensor_copy(xcb[:, i % 2, :], xcc[:, i % 2, :]))
                    ssV.wait(v)
                    v.sem_inc(cbV, 1)
            for c in range(3):
                for tt in range(8):
                    j = c * 8 + tt
                    v.wait_ge(tr9, j + 1)
                    ssV.mark(v.tensor_copy(stok[:, tt, c * 96:(c + 1) * 96], ptr96[:]))
                    ssV.wait(v)
                    v.sem_inc(ev9, 1)
            v.wait_ge(nrmS, 1)
            ssV.mark(v.reduce_sum(sss[:, 0, :], sq2[:, :, 0:256], AX.X))
            ssV.mark(v.reduce_sum(sss[:, 1, :], sq2[:, :, 256:272], AX.X))
            ssV.mark(v.reduce_sum(sss[:, 2, :], sq2[:, :, 272:288], AX.X))
            ssV.wait(v)
            ssV.mark(v.tensor_scalar(sss[:, 0, :], sss[:, 0, :], 1.0 / 256, None, ALU.mult))
            ssV.mark(v.tensor_scalar(sss[:, 1, :], sss[:, 1, :], 1.0 / 16, None, ALU.mult))
            ssV.mark(v.tensor_scalar(sss[:, 2, :], sss[:, 2, :], 1.0 / 16, None, ALU.mult))
            ssV.wait(v)
            v.sem_inc(rmsV, 1)
            v.wait_ge(nrmS, 2)
            for tt in range(8):
                ssV.mark(v.tensor_scalar(stok[:, tt, 0:256], stok[:, tt, 0:256],
                                         rno[:, 0, tt:tt + 1], None, ALU.mult))
                ssV.mark(v.tensor_scalar(stok[:, tt, 256:272], stok[:, tt, 256:272],
                                         rno[:, 1, tt:tt + 1], None, ALU.mult))
                ssV.mark(v.tensor_scalar(stok[:, tt, 272:288], stok[:, tt, 272:288],
                                         rno[:, 2, tt:tt + 1], None, ALU.mult))
                ssV.wait(v)
                ssV.mark(v.tensor_tensor(stok[:, tt, :], stok[:, tt, :], lnwb[:],
                                         ALU.mult))
                ssV.wait(v)
            v.sem_inc(rmsV, 1)
            ssV.mark(v.tensor_copy(dtrb[:], stok[:, :, 0:256]))
            ssV.wait(v)
            v.sem_inc(dcV, 1)
            for rc in range(2):
                v.wait_ge(trD, rc + 1)
                ssV.mark(v.tensor_copy(dtrT[:, rc, :], ptrd[:]))
                ssV.wait(v)
                v.sem_inc(evD, 1)

        @block.scalar
        def _(s):
            s.wait_ge(ldc, 96)
            for tch in range(2):
                for db in range(NDB2):
                    i = tch * NDB2 + db
                    s.wait_ge(cvV, i + 1)
                    if i >= 2:
                        s.wait_ge(odc[i % 2], 16 * (i // 2))
                    s.activation(xcc[:, i % 2, :], cc[:, 1, :], AF.Silu
                                 ).then_inc(slS, 1)
                s.wait_ge(mmX, 3 * NDB2 * (tch + 1))
                for c in range(3):
                    s.copy(ssp[:, c, tch * 512:(tch + 1) * 512], ps_ssm[c][:]
                           ).then_inc(evM, 1)
            s.wait_ge(ev9, 24)
            ssS.mark(s.activation(sq2[:], stok[:], AF.Square))
            ssS.wait(s)
            s.sem_inc(nrmS, 1)
            s.wait_ge(rmsV, 1)
            for seg in range(3):
                ssS.mark(s.activation(rno[:, seg, :], sss[:, seg, :], AF.Ln,
                                      bias=cst[:, 0:1], scale=1.0))
                ssS.wait(s)
                ssS.mark(s.activation(rno[:, seg, :], rno[:, seg, :], AF.Exp,
                                      bias=0.0, scale=-0.5))
                ssS.wait(s)
            s.sem_inc(nrmS, 1)
            for db in range(NDB2):
                for tch in range(2):
                    i = db * 2 + tch
                    s.wait_ge(mmD, i + 1)
                    if i >= 1:
                        s.wait_ge(evE, i)
                    if i >= 2:
                        s.wait_ge(odd[i % 2], 16 * (i // 2))
                    ssS.mark(s.activation(etmp[:], pdt[i % 2][:], AF.Exp,
                                          bias=dbt[:, db:db + 1], scale=1.0))
                    ssS.wait(s)
                    s.activation(dtst[:, i % 2, :], etmp[:], AF.Ln,
                                 bias=cst[:, 1:2], scale=1.0).then_inc(evE, 1)

        @block.tensor
        def _(t):
            t.wait_ge(ldc, 96)
            t.wait_ge(ldg, 32)
            for tch in range(2):
                for db in range(NDB2):
                    i = tch * NDB2 + db
                    t.wait_ge(cbV, i + 2)
                    if tch == 1 and db == 0:
                        t.wait_ge(evM, 3)
                    for c in range(3):
                        mm = t.matmul(ps_ssm[c][:], xprw[:, db, c * 96:(c + 1) * 96],
                                      xcb[:, i % 2, :],
                                      start=(db == 0), stop=(db == NDB2 - 1))
                    mm.then_inc(mmX, 3)
            t.wait_ge(evM, 6)
            for c in range(3):
                for tt in range(8):
                    j = c * 8 + tt
                    if j >= 1:
                        t.wait_ge(ev9, j)
                    mm = t.transpose(ptr96[:], ssp[0:96, c, tt * 128:(tt + 1) * 128],
                                     idf[0:96, 0:96])
                    mm.then_inc(tr9, 1)
            t.wait_ge(dcV, 1)
            for rc in range(2):
                if rc >= 1:
                    t.wait_ge(evD, rc)
                for tt in range(8):
                    mm = t.transpose(ptrd[:, tt * 128:(tt + 1) * 128],
                                     dtrb[:, tt, rc * 128:(rc + 1) * 128], idb[:])
                mm.then_inc(trD, 1)
            t.wait_ge(evD, 2)
            for db in range(NDB2):
                for tch in range(2):
                    i = db * 2 + tch
                    if i >= 2:
                        t.wait_ge(evE, i - 1)
                    for rb in range(2):
                        mm = t.matmul(pdt[i % 2][:],
                                      dpw[:, rb, db * 128:(db + 1) * 128],
                                      dtrT[:, rb, tch * 512:(tch + 1) * 512],
                                      start=(rb == 0), stop=(rb == 1))
                    mm.then_inc(mmD, 1)
    return nc


def build_l5():
    nc = bass.Bass("TRN2", target_bir_lowering=False)
    xc_in = nc.dram_tensor("xc_d", [TL, T], F32, kind="ExternalInput").ap()
    dt_in = nc.dram_tensor("dt_d", [TL, T], F32, kind="ExternalInput").ap()
    sz_in = nc.dram_tensor("sz_d", [TL, T], F32, kind="ExternalInput").ap()
    bt_in = nc.dram_tensor("B_T", [16, T], F32, kind="ExternalInput").ap()
    ct_in = nc.dram_tensor("C_T", [16, T], F32, kind="ExternalInput").ap()
    al_in = nc.dram_tensor("Alog_t", [128, 8, 16], F32, kind="ExternalInput").ap()
    dp_in = nc.dram_tensor("D_t", [128, 8], F32, kind="ExternalInput").ap()
    idb_in = nc.dram_tensor("ident", [128, 128], BF16, kind="ExternalInput").ap()
    yg_out = nc.dram_tensor("yg", [TL, T], BF16, kind="ExternalOutput").ap()

    TC = 1024
    NCH = T // TC
    NN = 16

    with ExitStack() as ctx:
        sb = lambda n, sh, dt: ctx.enter_context(nc.sbuf_tensor(n, sh, dt))
        ps = lambda n, sh, dt: ctx.enter_context(nc.psum_tensor(n, sh, dt))
        sem = lambda n: ctx.enter_context(nc.semaphore(n))
        dtc = sb("dtc", [128, 8, TC], F32)
        xcc = sb("xcc", [128, 8, TC], F32)
        Brep = sb("Brep", [128, NN, TC], BF16)
        Crep = sb("Crep", [128, NN, TC], BF16)
        ub = sb("ub", [128, 8, TC], BF16)
        dAb = sb("dAb", [128, 2, TC], F32)
        dxb = sb("dxb", [128, 2, TC], BF16)
        hq = sb("hq", [128, 2, TC], BF16)
        chb = sb("chb", [128, 2, TC], BF16)
        carr = sb("carr", [128, 128, 2], BF16)
        szc = sb("szc", [128, 2, TC], F32)
        t1b = sb("t1b", [128, TC], F32)
        ygst = sb("ygst", [128, 2, TC], BF16)
        acol = sb("acol", [128, 8, 16], F32)
        dpt = sb("dpt", [128, 8], F32)
        idb = sb("idb", [128, 128], BF16)
        py = [ps(f"py{i}", [128, TC], F32) for i in range(2)]
        ldc = sem("ldc"); ldt = sem("ldt"); ldb = sem("ldb")
        lsz = [sem("lsz0"), sem("lsz1")]
        ody = [sem("ody0"), sem("ody1")]
        aS = sem("aS"); uD = sem("uD"); chV = sem("chV"); myU = sem("myU")
        gtV = sem("gtV"); upV = sem("upV"); acS = sem("acS")
        semP = sem("semP")
        ssS = SelfSem(sem("ssS")); ssV = SelfSem(sem("ssV"))
        block = ctx.enter_context(nc.Block())

        dt_r = dt_in.rearrange("(db p) t -> p db t", p=128)
        xc_r = xc_in.rearrange("(db p) t -> p db t", p=128)
        sz_r = sz_in.rearrange("(db p) t -> p db t", p=128)
        yg_r = yg_out.rearrange("(db p) t -> p db t", p=128)

        @block.sync
        def _(s):
            s.dma_start(out=idb[:], in_=idb_in).then_inc(ldc, 16)
            s.dma_start(out=acol[:], in_=al_in).then_inc(ldc, 16)
            s.dma_start(out=dpt[:], in_=dp_in).then_inc(ldc, 16)
            for tc in range(NCH):
                if tc >= 1:
                    s.wait_ge(aS, 128 * tc)
                    s.wait_ge(gtV, 8 * tc)
                s.dma_start(out=dtc[:], in_=dt_r[:, :, tc * TC:(tc + 1) * TC]
                            ).then_inc(ldt, 16)
                s.dma_start(out=xcc[:], in_=xc_r[:, :, tc * TC:(tc + 1) * TC]
                            ).then_inc(ldt, 16)
                for db in range(8):
                    gidx = tc * 8 + db
                    if gidx >= 1:
                        pg = gidx - 1
                        s.wait_ge(gtV, pg + 1)
                        s.dma_start(out=yg_r[:, pg % 8, (pg // 8) * TC:(pg // 8 + 1) * TC],
                                    in_=ygst[:, pg % 2, :]).then_inc(ody[pg % 2], 16)
                    if gidx >= 2:
                        s.wait_ge(gtV, gidx - 1)
                    s.dma_start(out=szc[:, gidx % 2, :],
                                in_=sz_r[:, db, tc * TC:(tc + 1) * TC]
                                ).then_inc(lsz[gidx % 2], 16)
            pg = NCH * 8 - 1
            s.wait_ge(gtV, pg + 1)
            s.dma_start(out=yg_r[:, pg % 8, (pg // 8) * TC:(pg // 8 + 1) * TC],
                        in_=ygst[:, pg % 2, :]).then_inc(ody[pg % 2], 16)
            s.wait_ge(ody[0], 16 * 32)
            s.wait_ge(ody[1], 16 * 32)

        @block.gpsimd
        def _(g):
            for tc in range(NCH):
                if tc >= 1:
                    g.wait_ge(uD, 128 * tc)
                for n in range(NN):
                    g.dma_start(out=Brep[:, n, :],
                                in_=bt_in[n:n + 1, tc * TC:(tc + 1) * TC]
                                .broadcast_to((128, TC))).then_inc(ldb, 16)
                    g.dma_start(out=Crep[:, n, :],
                                in_=ct_in[n:n + 1, tc * TC:(tc + 1) * TC]
                                .broadcast_to((128, TC))).then_inc(ldb, 16)


        @block.scalar
        def _(s):
            s.wait_ge(ldc, 48)
            ssS.mark(s.activation(acol[:], acol[:], AF.Exp))
            ssS.wait(s)
            ssS.mark(s.mul(acol[:], acol[:], -1.0))
            ssS.wait(s)
            s.sem_inc(acS, 1)
            for tc in range(NCH):
                s.wait_ge(ldt, 32 * (tc + 1))
                for db in range(8):
                    for n in range(NN):
                        u = tc * 128 + db * 16 + n
                        if u >= 2:
                            s.wait_ge(uD, u - 1)
                        s.activation(dAb[:, u % 2, :], dtc[:, db, :], AF.Exp,
                                     bias=0.0, scale=acol[:, db, n:n + 1]
                                     ).then_inc(aS, 1)

        @block.vector
        def _(v):
            v.wait_ge(ldc, 48)
            for tc in range(NCH):
                cp = tc % 2
                v.wait_ge(ldt, 32 * (tc + 1))
                for db in range(8):
                    ssV.mark(v.tensor_tensor(ub[:, db, :], dtc[:, db, :],
                                             xcc[:, db, :], ALU.mult))
                    ssV.wait(v)
                v.wait_ge(ldb, 512 * (tc + 1))
                for db in range(8):
                    for n in range(NN):
                        u = tc * 128 + db * 16 + n
                        cidx = db * 16 + n
                        v.wait_ge(aS, u + 1)
                        ssV.mark(v.tensor_tensor(dxb[:, u % 2, :], ub[:, db, :],
                                                 Brep[:, n, :], ALU.mult))
                        ssV.wait(v)
                        if tc == 0:
                            init = 0.0
                        else:
                            pp = (tc - 1) % 2
                            init = carr[:, cidx:cidx + 1, pp:pp + 1]
                        ssV.mark(v.tensor_tensor_scan(hq[:, u % 2, :],
                                                      dAb[:, u % 2, :],
                                                      dxb[:, u % 2, :], init,
                                                      ALU.mult, ALU.add))
                        ssV.wait(v)
                        ssV.mark(v.tensor_copy(
                            carr[:, cidx:cidx + 1, (tc % 2):(tc % 2) + 1],
                            hq[:, u % 2, TC - 1:TC]))
                        if u >= 2:
                            v.wait_ge(myU, u - 1)
                        ssV.mark(v.tensor_tensor(chb[:, u % 2, :], hq[:, u % 2, :],
                                                 Crep[:, n, :], ALU.mult))
                        ssV.wait(v)
                        v.sem_inc(chV, 1)
                        v.sem_inc(uD, 1)
                    gidx = tc * 8 + db
                    v.wait_ge(myU, tc * 128 + (db + 1) * 16)
                    v.wait_ge(lsz[gidx % 2], 16 * (gidx // 2 + 1))
                    if gidx >= 2:
                        v.wait_ge(ody[gidx % 2], 16 * (gidx // 2))
                    ssV.mark(v.scalar_tensor_tensor(t1b[:], xcc[:, db, :],
                                                    dpt[:, db:db + 1], py[db % 2][:],
                                                    ALU.mult, ALU.add))
                    ssV.wait(v)
                    ssV.mark(v.tensor_tensor(ygst[:, gidx % 2, :], t1b[:],
                                             szc[:, gidx % 2, :], ALU.mult))
                    ssV.wait(v)
                    v.sem_inc(gtV, 1)

        @block.tensor
        def _(t):
            t.wait_ge(ldc, 48)
            for tc in range(NCH):
                for db in range(8):
                    gidx = tc * 8 + db
                    if gidx >= 2:
                        t.wait_ge(gtV, gidx - 1)
                    for n in range(NN):
                        u = tc * 128 + db * 16 + n
                        t.wait_ge(chV, u + 1)
                        for c in range(2):
                            mm = t.matmul(py[db % 2][:, c * 512:(c + 1) * 512],
                                          idb[:], chb[:, u % 2, c * 512:(c + 1) * 512],
                                          start=(n == 0), stop=(n == NN - 1),
                                          skip_group_check=True)
                        mm.then_inc(myU, 1)
    return nc


def build_l6():
    nc = bass.Bass("TRN2", target_bir_lowering=False)
    yg_in = nc.dram_tensor("ygT", [DIN, TL], BF16, kind="ExternalInput").ap()
    ow_in = nc.dram_tensor("opw_q", [32, 128, 64, 128], F32, kind="ExternalInput").ap()
    xb_in = nc.dram_tensor("xb", [TL, D], F32, kind="ExternalInput").ap()
    w1_in = nc.dram_tensor("w1q", [32, 128, 32, 128], F32, kind="ExternalInput").ap()
    b1_in = nc.dram_tensor("b1_t", [128, 32], F32, kind="ExternalInput").ap()
    at_in = nc.dram_tensor("At", [D, 256], F32, kind="ExternalInput").ap()
    bf_in = nc.dram_tensor("Bft", [256, D], F32, kind="ExternalInput").ap()
    co_in = nc.dram_tensor("coeffs", [16, 1], F32, kind="ExternalInput").ap()
    sc_in = nc.dram_tensor("scales", [16, 1], F32, kind="ExternalInput").ap()
    idf_in = nc.dram_tensor("identF", [128, 128], F32, kind="ExternalInput").ap()
    idb_in = nc.dram_tensor("ident", [128, 128], BF16, kind="ExternalInput").ap()
    x3_out = nc.dram_tensor("x3", [TL, D], F32, kind="ExternalOutput").ap()

    NDB = 32
    NDB2 = 64
    NQ = 4
    w16d = nc.dram_tensor("w16d", [16, 1], F32)

    with ExitStack() as ctx:
        sb = lambda n, sh, dt: ctx.enter_context(nc.sbuf_tensor(n, sh, dt))
        ps = lambda n, sh, dt: ctx.enter_context(nc.psum_tensor(n, sh, dt))
        sem = lambda n: ctx.enter_context(nc.semaphore(n))
        ygb = sb("ygb", [128, NDB2, 256], BF16)
        wst = sb("wst", [128, 2, NDB2, 128], BF16)
        x2q = sb("x2q", [128, 2, D], F32)
        aotmp = sb("aotmp", [128, 256], F32)
        sq = sb("sq", [128, D], F32)
        stt = sb("stt", [128, 2, 8], F32)
        cst = sb("cst", [128, 1], F32)
        tokbf = sb("tokbf", [128, D], BF16)
        xl3T = sb("xl3T", [128, NDB, 256], BF16)
        w1s = sb("w1s", [128, 2, NDB, 128], BF16)
        atb = sb("atb", [128, NDB, 256], BF16)
        bftb = sb("bftb", [128, 2, D], BF16)
        b1t = sb("b1t", [128, 32], F32)
        w16 = sb("w16", [16, 2], F32)
        wcol = sb("wcol", [128, 2], F32)
        xab = sb("xab", [128, 2, 256], BF16)
        xaf = sb("xaf", [128, 2, 256], F32)
        f1tmp = sb("f1tmp", [128, 256], F32)
        idb = sb("idb", [128, 128], BF16)
        idf = sb("idf", [128, 128], F32)
        pmo = [ps(f"pmo{i}", [128, 256], F32) for i in range(2)]
        ptr = ps("ptr", [128, 256], F32)
        pb2 = ps("pb2", [128, 256], BF16)
        pm1 = [ps(f"pm1{i}", [128, 256], F32) for i in range(2)]
        pxa = [ps(f"pxa{i}", [128, 256], F32) for i in range(2)]
        ldc = sem("ldc"); ldg6 = sem("ldg6"); ldy = sem("ldy"); ldxq = sem("ldxq")
        wlo = [sem("wlo0"), sem("wlo1")]
        wl1 = [sem("wl10"), sem("wl11")]
        odx = [sem("odx0"), sem("odx1")]
        mmO = sem("mmO"); evO = sem("evO"); trO = sem("trO"); adV = sem("adV")
        cpS = sem("cpS"); cpV1 = sem("cpV1"); cpV = sem("cpV"); trX = sem("trX")
        evX = sem("evX"); wcV = sem("wcV"); wcP = sem("wcP"); wcD = sem("wcD")
        mmA = sem("mmA"); evA = sem("evA"); xabV = sem("xabV")
        mm1 = sem("mm1"); ev1 = sem("ev1")
        tr1 = sem("tr1"); ad1 = sem("ad1")
        ssS = SelfSem(sem("ssS")); ssV = SelfSem(sem("ssV"))
        block = ctx.enter_context(nc.Block())

        yg_r = yg_in.rearrange("(db p) t -> p db t", p=128)

        @block.sync
        def _(s):
            s.dma_start(out=idf[:], in_=idf_in).then_inc(ldc, 16)
            s.dma_start(out=idb[:], in_=idb_in).then_inc(ldc, 16)
            s.dma_start(out=b1t[:], in_=b1_in).then_inc(ldc, 16)
            s.dma_start(out=w16[:, 0:1], in_=co_in).then_inc(ldc, 16)
            s.dma_start(out=w16[:, 1:2], in_=sc_in).then_inc(ldc, 16)
            for q in range(NQ):
                if q >= 1:
                    s.wait_ge(ad1, 64 * q)
                    s.wait_ge(odx[(q - 1) % 2], 32 * ((q - 1) // 2 + 1))
                for i in range(2):
                    gt = q * 2 + i
                    s.dma_start(out=x2q[:, i, :],
                                in_=xb_in[gt * 128:(gt + 1) * 128, :]
                                ).then_inc(ldxq, 16)
                s.dma_start(out=ygb[:], in_=yg_r[:, :, q * 256:(q + 1) * 256]
                            ).then_inc(ldy, 16)
            s.wait_ge(odx[0], 32 * 2)
            s.wait_ge(odx[1], 32 * 2)

        @block.gpsimd
        def _(g):
            g.dma_start(out=atb[:], in_=at_in.rearrange("(db p) e -> p db e", p=128)
                        ).then_inc(ldg6, 16)
            g.dma_start(out=bftb[:], in_=bf_in.rearrange("(rb p) o -> p rb o", p=128)
                        ).then_inc(ldg6, 16)
            g.wait_ge(wcV, 1)
            g.dma_start(out=w16d.ap(), in_=w16[:, 0:1]).then_inc(wcD, 16)
            g.wait_ge(wcD, 16)
            for rb in range(2):
                g.dma_start(out=wcol[:, rb:rb + 1],
                            in_=w16d.ap()[rb * 8:(rb + 1) * 8, :]
                            .rearrange("e x -> e () x").broadcast_to((8, 16, 1))
                            ).then_inc(wcD, 16)
            for q in range(NQ):
                if q >= 1:
                    g.wait_ge(ad1, 64 * q)
                    for i in range(2):
                        gt = (q - 1) * 2 + i
                        g.dma_start(out=x3_out[gt * 128:(gt + 1) * 128, :],
                                    in_=x2q[:, i, :]).then_inc(odx[(q - 1) % 2], 16)
                for ocb in range(32):
                    k = q * 32 + ocb
                    if k >= 2:
                        g.wait_ge(mmO, (k - 1))
                    g.dma_start(out=wst[:, k % 2, :, :], in_=ow_in[ocb]
                                ).then_inc(wlo[k % 2], 16)
                for ocb in range(32):
                    k = q * 32 + ocb
                    if k >= 2:
                        g.wait_ge(mm1, (k - 1))
                    g.dma_start(out=w1s[:, k % 2, :, :], in_=w1_in[ocb]
                                ).then_inc(wl1[k % 2], 16)
            g.wait_ge(ad1, 64 * NQ)
            for i in range(2):
                gt = (NQ - 1) * 2 + i
                g.dma_start(out=x3_out[gt * 128:(gt + 1) * 128, :],
                            in_=x2q[:, i, :]).then_inc(odx[(NQ - 1) % 2], 16)

        @block.tensor
        def _(t):
            t.wait_ge(ldc, 80)
            t.wait_ge(ldg6, 32)
            for q in range(NQ):
                t.wait_ge(ldy, 16 * (q + 1))
                for ocb in range(32):
                    k = q * 32 + ocb
                    t.wait_ge(wlo[k % 2], 16 * (k // 2 + 1))
                    if k >= 2:
                        t.wait_ge(evO, k - 1)
                    for db in range(NDB2):
                        mm = t.matmul(pmo[k % 2][:], wst[:, k % 2, db, :],
                                      ygb[:, db, :],
                                      start=(db == 0), stop=(db == NDB2 - 1))
                    mm.then_inc(mmO, 1)
                    t.wait_ge(evO, k + 1)
                    if k >= 1:
                        t.wait_ge(adV, 2 * k)
                    for i in range(2):
                        mm = t.transpose(ptr[:, i * 128:(i + 1) * 128],
                                         aotmp[:, i * 128:(i + 1) * 128], idf[:])
                    mm.then_inc(trO, 1)
                t.wait_ge(adV, 64 * (q + 1))
                for i in range(2):
                    gt = q * 2 + i
                    t.wait_ge(cpV, gt + 1)
                    for kk in range(16):
                        pk = gt * 16 + kk
                        if pk >= 1:
                            t.wait_ge(evX, pk)
                        for j in range(2):
                            db = kk * 2 + j
                            mm = t.transpose(pb2[:, j * 128:(j + 1) * 128],
                                             tokbf[:, db * 128:(db + 1) * 128],
                                             idb[:])
                        mm.then_inc(trX, 1)
                t.wait_ge(evX, (q + 1) * 32)
                t.wait_ge(wcD, 48)
                if q >= 1:
                    t.wait_ge(evA, 2 * q)
                for rb in range(2):
                    for db in range(NDB):
                        mm = t.matmul(pxa[rb][:], atb[:, db, rb * 128:(rb + 1) * 128],
                                      xl3T[:, db, :],
                                      start=(db == 0), stop=(db == NDB - 1))
                    mm.then_inc(mmA, 1)
                t.wait_ge(xabV, q + 1)
                for ocb in range(32):
                    k = q * 32 + ocb
                    t.wait_ge(wl1[k % 2], 16 * (k // 2 + 1))
                    if k >= 2:
                        t.wait_ge(ev1, k - 1)
                    for db in range(NDB):
                        t.matmul(pm1[k % 2][:], w1s[:, k % 2, db, :],
                                 xl3T[:, db, :],
                                 start=(db == 0), stop=False)
                    for rb in range(2):
                        mm = t.matmul(pm1[k % 2][:], bftb[:, rb, ocb * 128:(ocb + 1) * 128],
                                      xab[:, rb, :],
                                      start=False, stop=(rb == 1))
                    mm.then_inc(mm1, 1)
                    t.wait_ge(ev1, k + 1)
                    if k >= 1:
                        t.wait_ge(ad1, 2 * k)
                    for i in range(2):
                        mm = t.transpose(ptr[:, i * 128:(i + 1) * 128],
                                         f1tmp[:, i * 128:(i + 1) * 128], idf[:])
                    mm.then_inc(tr1, 1)

        @block.scalar
        def _(s):
            for q in range(NQ):
                for ocb in range(32):
                    k = q * 32 + ocb
                    s.wait_ge(mmO, k + 1)
                    if k >= 1:
                        s.wait_ge(trO, k)
                    s.activation(aotmp[:], pmo[k % 2][:], AF.Identity
                                 ).then_inc(evO, 1)
                for i in range(2):
                    gt = q * 2 + i
                    s.wait_ge(adV, 64 * (q + 1) + 0)
                    s.wait_ge(cpV1, gt + 1)
                    b = stt[:, gt % 2, :]
                    ssS.mark(s.activation(sq[:], x2q[:, i, :], AF.Square,
                                          accum_out=b[:, 1:2]))
                    ssS.wait(s)
                    ssS.mark(s.mul(b[:, 2:3], b[:, 0:1], 1.0 / D))
                    ssS.mark(s.mul(b[:, 3:4], b[:, 1:2], 1.0 / D))
                    ssS.wait(s)
                    ssS.mark(s.activation(b[:, 4:5], b[:, 2:3], AF.Square))
                    ssS.wait(s)
                    ssS.mark(s.mul(b[:, 4:5], b[:, 4:5], -1.0))
                    ssS.wait(s)
                    ssS.mark(s.add(b[:, 4:5], b[:, 3:4], b[:, 4:5]))
                    ssS.wait(s)
                    ssS.mark(s.activation(b[:, 5:6], b[:, 4:5], AF.Ln,
                                          bias=cst[:, 0:1], scale=1.0))
                    ssS.wait(s)
                    s.activation(b[:, 6:7], b[:, 5:6], AF.Exp, bias=0.0, scale=-0.5
                                 ).then_inc(cpS, 1)
                for rb in range(2):
                    s.wait_ge(mmA, 2 * q + rb + 1)
                    s.copy(xaf[:, rb, :], pxa[rb][:]).then_inc(evA, 1)
                for ocb in range(32):
                    k = q * 32 + ocb
                    s.wait_ge(mm1, k + 1)
                    if k >= 1:
                        s.wait_ge(tr1, k)
                    s.activation(f1tmp[:], pm1[k % 2][:], AF.Identity,
                                 bias=b1t[:, ocb:ocb + 1], scale=1.0
                                 ).then_inc(ev1, 1)

        @block.vector
        def _(v):
            v.memset(cst[:], LN_EPS)
            v.wait_ge(ldc, 80)
            ssV.mark(v.tensor_tensor(w16[:, 0:1], w16[:, 0:1], w16[:, 1:2],
                                     ALU.mult))
            ssV.wait(v)
            v.sem_inc(wcV, 1)
            for q in range(NQ):
                for ocb in range(32):
                    k = q * 32 + ocb
                    v.wait_ge(trO, k + 1)
                    v.wait_ge(ldxq, 32 * (q + 1))
                    for i in range(2):
                        vv = v.tensor_tensor(x2q[:, i, ocb * 128:(ocb + 1) * 128],
                                             x2q[:, i, ocb * 128:(ocb + 1) * 128],
                                             ptr[:, i * 128:(i + 1) * 128], ALU.add)
                    vv.then_inc(adV, 2)
                for i in range(2):
                    gt = q * 2 + i
                    v.wait_ge(adV, 64 * (q + 1))
                    b = stt[:, gt % 2, :]
                    ssV.mark(v.reduce_sum(b[:, 0:1], x2q[:, i, :], AX.X))
                    ssV.wait(v)
                    v.sem_inc(cpV1, 1)
                    v.wait_ge(cpS, gt + 1)
                    ssV.mark(v.tensor_tensor(b[:, 7:8], b[:, 2:3], b[:, 6:7],
                                             ALU.mult))
                    ssV.wait(v)
                    v.tensor_scalar(tokbf[:], x2q[:, i, :], b[:, 6:7], b[:, 7:8],
                                    ALU.mult, ALU.subtract).then_inc(cpV, 1)
                    for kk in range(16):
                        pk = gt * 16 + kk
                        v.wait_ge(trX, pk + 1)
                        ssV.mark(v.tensor_copy(
                            xl3T[:, kk * 2:(kk + 1) * 2, i * 128:(i + 1) * 128],
                            pb2[:].rearrange("p (a b) -> p a b", a=2)))
                        ssV.wait(v)
                        v.sem_inc(evX, 1)
                v.wait_ge(evA, 2 * (q + 1))
                v.wait_ge(wcD, 48)
                for rb in range(2):
                    ssV.mark(v.tensor_scalar(xab[:, rb, :], xaf[:, rb, :],
                                             wcol[:, rb:rb + 1], None, ALU.mult))
                    ssV.wait(v)
                v.sem_inc(xabV, 1)
                for ocb in range(32):
                    k = q * 32 + ocb
                    v.wait_ge(tr1, k + 1)
                    for i in range(2):
                        vv = v.tensor_tensor(x2q[:, i, ocb * 128:(ocb + 1) * 128],
                                             x2q[:, i, ocb * 128:(ocb + 1) * 128],
                                             ptr[:, i * 128:(i + 1) * 128], ALU.add)
                    vv.then_inc(ad1, 2)
    return nc


def _ident_b():
    return np.eye(128, dtype=BF)


def _ident_f():
    return np.eye(128, dtype=np.float32)


def kernel(**inputs):
    cores = list(range(C))

    r1 = run_bass_kernel_spmd(build_l1(), host_l1_inputs(inputs), core_ids=cores)
    l1_outs = [r["qkvT"] for r in r1.results]

    r2 = run_bass_kernel_spmd(build_l2(), host_l2_inputs(l1_outs), core_ids=cores)
    ctx = np.zeros((T, D), dtype=BF)
    for c in range(C):
        o = r2.results[c]["ctxq"]
        for j in range(4):
            ctx[:, (c * 4 + j) * HD:(c * 4 + j + 1) * HD] = o[j]

    r3a = run_bass_kernel_spmd(build_l3a(), host_l3a_inputs(inputs, ctx),
                               core_ids=cores)
    x1_full = np.concatenate([r["x1"] for r in r3a.results], 0)
    ln1_full = np.concatenate([r["xln1"] for r in r3a.results], 0)

    r3b = run_bass_kernel_spmd(build_l3b(), host_l3b_inputs(inputs, x1_full, ln1_full),
                               core_ids=cores)
    xb_full = np.concatenate([r["xb"] for r in r3b.results], 0)
    ln2T_full = np.concatenate([r["xln2T"] for r in r3b.results], 1)

    def halo(c):
        lo = c * TL - 3
        out = np.zeros((D, THL), dtype=BF)
        s = max(0, lo)
        out[:, (s - lo):] = ln2T_full[:, s:(c + 1) * TL]
        return out

    inw = inputs["L1_attn_in_proj_w"]
    maps4a = [{"xlh": halo(c), "inw": inw} for c in range(C)]
    r4a = run_bass_kernel_spmd(build_l4a(), maps4a, core_ids=cores)

    cw = np.ascontiguousarray(
        inputs['L1_attn_conv_w'].reshape(64, 128, 4).transpose(1, 0, 2))
    cb = np.ascontiguousarray(inputs['L1_attn_conv_b'].reshape(64, 128).T)
    dbias = np.ascontiguousarray(inputs['L1_attn_dt_bias'].reshape(64, 128).T)
    lnw = np.ascontiguousarray(np.concatenate(
        [inputs['L1_attn_dt_ln_w'], inputs['L1_attn_B_ln_w'],
         inputs['L1_attn_C_ln_w']]).reshape(1, 288))
    xprot = np.ascontiguousarray(inputs['L1_attn_x_proj_w'].T)
    dtprot = np.ascontiguousarray(inputs['L1_attn_dt_proj_w'].T)
    maps4b = [{"xpart": r4a.results[c]["xpart"], "convw_t": cw, "convb_t": cb,
               "xprojw_t": xprot, "lnw_row": lnw, "dtprojw_t": dtprot,
               "dtbias_t": dbias, "ident": _ident_b(), "identF": _ident_f()}
              for c in range(C)]
    r4b = run_bass_kernel_spmd(build_l4b(), maps4b, core_ids=cores)

    xc_full = np.concatenate([r4b.results[c]["xc"] for c in range(C)], 1)
    dt_full = np.concatenate([r4b.results[c]["dt"] for c in range(C)], 1)
    sz_full = np.concatenate([r4a.results[c]["sz"] for c in range(C)], 1)
    B_full = np.concatenate(
        [r4b.results[c]["BC"][:, :, 0:16].transpose(1, 0, 2).reshape(TL, 16)
         for c in range(C)], 0)
    C_full = np.concatenate(
        [r4b.results[c]["BC"][:, :, 16:32].transpose(1, 0, 2).reshape(TL, 16)
         for c in range(C)], 0)
    B_T = np.ascontiguousarray(B_full.T)
    C_T = np.ascontiguousarray(C_full.T)
    Alog = inputs['L1_attn_A_log']
    Dp = inputs['L1_attn_D_param']
    maps5 = []
    for c in range(C):
        sl = slice(c * TL, (c + 1) * TL)
        maps5.append({
            "xc_d": np.ascontiguousarray(xc_full[sl]),
            "dt_d": np.ascontiguousarray(dt_full[sl]),
            "sz_d": np.ascontiguousarray(sz_full[sl]),
            "B_T": B_T, "C_T": C_T,
            "Alog_t": np.ascontiguousarray(
                Alog[sl].reshape(8, 128, 16).transpose(1, 0, 2)),
            "D_t": np.ascontiguousarray(Dp[sl].reshape(8, 128).T),
            "ident": _ident_b()})
    r5 = run_bass_kernel_spmd(build_l5(), maps5, core_ids=cores)

    yg_full = np.concatenate([r5.results[c]["yg"] for c in range(C)], 0)
    opw_q = np.ascontiguousarray(
        inputs['L1_attn_out_proj_w'].T.reshape(64, 128, 32, 128).transpose(2, 1, 0, 3))
    w1q = np.ascontiguousarray(
        inputs['L1_ffn_W'].T.reshape(32, 128, 32, 128).transpose(2, 1, 0, 3))
    b1t = np.ascontiguousarray(inputs['L1_ffn_bias'].reshape(32, 128).T)
    At = np.ascontiguousarray(inputs['L1_ffn_A'].reshape(256, D).T)
    Bft = np.ascontiguousarray(inputs['L1_ffn_Bf'].transpose(0, 2, 1).reshape(256, D))
    maps6 = []
    for c in range(C):
        sl = slice(c * TL, (c + 1) * TL)
        maps6.append({
            "ygT": np.ascontiguousarray(yg_full[:, sl]),
            "opw_q": opw_q, "xb": np.ascontiguousarray(xb_full[sl]),
            "w1q": w1q, "b1_t": b1t, "At": At, "Bft": Bft,
            "coeffs": np.ascontiguousarray(inputs['L1_ffn_coeffs'].reshape(16, 1)),
            "scales": np.ascontiguousarray(inputs['L1_ffn_scales'].reshape(16, 1)),
            "identF": _ident_f(), "ident": _ident_b()})
    r6 = run_bass_kernel_spmd(build_l6(), maps6, core_ids=cores)
    x3 = np.concatenate([r["x3"] for r in r6.results], 0)
    return np.ascontiguousarray(x3.reshape(1, T, D)).astype(np.float32)


# version 04 (final)
# speedup vs baseline: 1.3001x; 1.0192x over previous
from contextlib import ExitStack
import numpy as np
import ml_dtypes
import concourse.bass as bass
import concourse.mybir as mybir
from concourse.bass_utils import run_bass_kernel_spmd

F32 = mybir.dt.float32
BF16 = mybir.dt.bfloat16
AF = mybir.ActivationFunctionType
ALU = mybir.AluOpType
AX = mybir.AxisListType
BF = ml_dtypes.bfloat16

C = 8
T = 8192
TL = T // C
D = 4096
NH = 32
HD = 128
QKV = 3 * D
LN_EPS = 1e-5



class SelfSem:
    def __init__(self, sem):
        self.sem = sem
        self.n = 0

    def mark(self, inst):
        inst.then_inc(self.sem, 1)
        self.n += 1
        return inst

    def wait(self, eng):
        if self.n:
            eng.wait_ge(self.sem, self.n)


def build_l1():
    nc = bass.Bass("TRN2", target_bir_lowering=False)
    x_in = nc.dram_tensor("x", [TL, D], F32, kind="ExternalInput").ap()
    w_in = nc.dram_tensor("wqkv", [D, QKV], F32, kind="ExternalInput").ap()
    b_in = nc.dram_tensor("qkvb_t", [128, 96], F32, kind="ExternalInput").ap()
    id_in = nc.dram_tensor("ident", [128, 128], BF16, kind="ExternalInput").ap()
    out = nc.dram_tensor("qkvT", [QKV, TL], BF16, kind="ExternalOutput").ap()

    NTT = TL // 128
    NDB = D // 128
    NCBG = QKV // 512
    NUNITS = NCBG * 2 * 4

    with ExitStack() as ctx:
        sb = lambda n, sh, dt: ctx.enter_context(nc.sbuf_tensor(n, sh, dt))
        ps = lambda n, sh, dt: ctx.enter_context(nc.psum_tensor(n, sh, dt))
        sem = lambda n: ctx.enter_context(nc.semaphore(n))
        xt = sb("xt", [128, 2, D], F32)
        sq = sb("sq", [128, D], F32)
        stt = sb("st", [128, 2, 8], F32)
        xln = sb("xln", [128, 2, D], BF16)
        xlnT = sb("xlnT", [128, NDB, TL], BF16)
        idb = sb("idb", [128, 128], BF16)
        bq = sb("bq", [128, 96], F32)
        wst = sb("wst", [128, 2, NDB, 512], BF16)
        stg = sb("stg", [128, 2, 4, 512], BF16)
        cst = sb("cst", [128, 1], F32)
        ptr = [ps(f"ptr{i}", [128, 1024], BF16) for i in range(2)]
        pm = [ps(f"pm{i}", [128, 512], F32) for i in range(6)]
        ldc = sem("ldc")
        ldp = [sem("ld0"), sem("ld1")]
        wlp = [sem("wl0"), sem("wl1")]
        odp = [sem("od0"), sem("od1")]
        cpS = sem("cpS"); cpV1 = sem("cpV1")
        ssS = SelfSem(sem("ssS")); ssV = SelfSem(sem("ssV"))
        cpV = sem("cpV"); trS = sem("trS"); evT = sem("evT"); mmS = sem("mmS")
        evS = sem("evS")
        block = ctx.enter_context(nc.Block())

        @block.sync
        def _(s):
            s.dma_start(out=idb[:], in_=id_in).then_inc(ldc, 16)
            s.dma_start(out=bq[:], in_=b_in).then_inc(ldc, 16)
            for tt in range(NTT):
                if tt >= 2:
                    s.wait_ge(cpV, tt - 1)
                s.dma_start(out=xt[:, tt % 2, :], in_=x_in[tt * 128:(tt + 1) * 128, :]
                            ).then_inc(ldp[tt % 2], 16)
            s.wait_ge(odp[0], 16 * 4 * (NUNITS // 8))
            s.wait_ge(odp[1], 16 * 4 * (NUNITS // 8))

        @block.scalar
        def _(s):
            for tt in range(NTT):
                s.wait_ge(ldp[tt % 2], 16 * (tt // 2 + 1))
                if tt >= 2:
                    s.wait_ge(cpV, tt - 1)
                b = stt[:, tt % 2, :]
                ssS.mark(s.activation(sq[:], xt[:, tt % 2, :], AF.Square,
                                      accum_out=b[:, 1:2]))
                ssS.wait(s)
                s.wait_ge(cpV1, tt + 1)
                ssS.mark(s.mul(b[:, 2:3], b[:, 0:1], 1.0 / D))
                ssS.mark(s.mul(b[:, 3:4], b[:, 1:2], 1.0 / D))
                ssS.wait(s)
                ssS.mark(s.activation(b[:, 4:5], b[:, 2:3], AF.Square))
                ssS.wait(s)
                ssS.mark(s.mul(b[:, 4:5], b[:, 4:5], -1.0))
                ssS.wait(s)
                ssS.mark(s.add(b[:, 4:5], b[:, 3:4], b[:, 4:5]))
                ssS.wait(s)
                ssS.mark(s.activation(b[:, 5:6], b[:, 4:5], AF.Ln, bias=cst[:, 0:1], scale=1.0))
                ssS.wait(s)
                s.activation(b[:, 6:7], b[:, 5:6], AF.Exp, bias=0.0, scale=-0.5
                             ).then_inc(cpS, 1)
                if tt >= 1:
                    for k in range(4):
                        pk = (tt - 1) * 4 + k
                        s.wait_ge(trS, pk + 1)
                        s.copy(xlnT[:, k * 8:(k + 1) * 8,
                                    (tt - 1) * 128: tt * 128],
                               ptr[pk % 2][:].rearrange("p (a b) -> p a b", a=8)
                               ).then_inc(evT, 1)
            for k in range(4):
                pk = (NTT - 1) * 4 + k
                s.wait_ge(trS, pk + 1)
                s.copy(xlnT[:, k * 8:(k + 1) * 8, (NTT - 1) * 128: NTT * 128],
                       ptr[pk % 2][:].rearrange("p (a b) -> p a b", a=8)
                       ).then_inc(evT, 1)
            s.wait_ge(ldc, 32)
            for u in range(NUNITS):
                cbg, r = divmod(u, 8)
                tch, cb = divmod(r, 4)
                s.wait_ge(mmS, u + 1)
                if u >= 2 * 4:
                    s.wait_ge(odp[(u // 4) % 2], 64 * ((u // 4) // 2))
                s.activation(stg[:, (u // 4) % 2, cb, :], pm[u % 6][:], AF.Identity,
                             bias=bq[:, cbg * 4 + cb:cbg * 4 + cb + 1], scale=1.0
                             ).then_inc(evS, 1)

        @block.vector
        def _(v):
            v.memset(cst[:], LN_EPS)
            for tt in range(NTT):
                v.wait_ge(ldp[tt % 2], 16 * (tt // 2 + 1))
                b = stt[:, tt % 2, :]
                v.reduce_sum(b[:, 0:1], xt[:, tt % 2, :], AX.X).then_inc(cpV1, 1)
                v.wait_ge(cpS, tt + 1)
                if tt >= 2:
                    v.wait_ge(trS, (tt - 1) * 4)
                ssV.mark(v.tensor_tensor(b[:, 7:8], b[:, 2:3], b[:, 6:7], ALU.mult))
                ssV.wait(v)
                v.tensor_scalar(xln[:, tt % 2, :], xt[:, tt % 2, :],
                                b[:, 6:7], b[:, 7:8], ALU.mult, ALU.subtract
                                ).then_inc(cpV, 1)

        @block.tensor
        def _(t):
            t.wait_ge(ldc, 32)
            for tt in range(NTT):
                t.wait_ge(cpV, tt + 1)
                for k in range(4):
                    pk = tt * 4 + k
                    if pk >= 2:
                        t.wait_ge(evT, pk - 1)
                    for j in range(8):
                        db = k * 8 + j
                        mm = t.transpose(
                            ptr[pk % 2][:, j * 128:(j + 1) * 128],
                            xln[:, tt % 2, db * 128:(db + 1) * 128], idb[:])
                    mm.then_inc(trS, 1)
            t.wait_ge(evT, NTT * 4)
            for cbg in range(NCBG):
                t.wait_ge(wlp[cbg % 2], 64 * (cbg // 2 + 1))
                for tch in range(2):
                    for cb in range(4):
                        u = cbg * 8 + tch * 4 + cb
                        if u >= 6:
                            t.wait_ge(evS, u - 5)
                        for db in range(NDB):
                            mm = t.matmul(
                                pm[u % 6][:],
                                wst[:, cbg % 2, db, cb * 128:(cb + 1) * 128],
                                xlnT[:, db, tch * 512:(tch + 1) * 512],
                                start=(db == 0), stop=(db == NDB - 1))
                        mm.then_inc(mmS, 1)

        @block.gpsimd
        def _(g):
            def outdmas(cbg2):
                for r in range(8):
                    u = cbg2 * 8 + r
                    tch, cb = divmod(r, 4)
                    g.wait_ge(evS, u + 1)
                    g.dma_start(
                        out=out[cbg2 * 512 + cb * 128: cbg2 * 512 + (cb + 1) * 128,
                                tch * 512:(tch + 1) * 512],
                        in_=stg[:, (u // 4) % 2, cb, :]).then_inc(odp[(u // 4) % 2], 16)
            for cbg in range(NCBG):
                if cbg >= 2:
                    g.wait_ge(mmS, (cbg - 1) * 8)
                src = w_in[:, cbg * 512:(cbg + 1) * 512].rearrange(
                    "(db p) c -> p db c", p=128)
                for q in range(4):
                    g.dma_start(out=wst[:, cbg % 2, q * 8:(q + 1) * 8, :],
                                in_=src[:, q * 8:(q + 1) * 8, :]).then_inc(wlp[cbg % 2], 16)
                if cbg >= 1:
                    outdmas(cbg - 1)
            outdmas(NCBG - 1)
    return nc


def host_l1_inputs(inputs):
    x = np.ascontiguousarray(inputs["L0_attn_x"].reshape(T, D))
    w = inputs["L0_attn_attn_qkvw"]
    bt = np.ascontiguousarray(inputs["L0_attn_attn_qkvb"].reshape(96, 128).T)
    ident = np.eye(128, dtype=BF)
    return [
        {"x": np.ascontiguousarray(x[c * TL:(c + 1) * TL]),
         "wqkv": w, "qkvb_t": bt, "ident": ident}
        for c in range(C)
    ]


def build_l2():
    nc = bass.Bass("TRN2", target_bir_lowering=False)
    qT_in = nc.dram_tensor("qT", [512, T], BF16, kind="ExternalInput").ap()
    kT_in = nc.dram_tensor("kT", [512, T], BF16, kind="ExternalInput").ap()
    vh_in = nc.dram_tensor("vh", [4, T, HD], BF16, kind="ExternalInput").ap()
    idf_in = nc.dram_tensor("identF", [128, 128], F32, kind="ExternalInput").ap()
    out = nc.dram_tensor("ctxq", [4, T, HD], BF16, kind="ExternalOutput").ap()
    mbd = nc.dram_tensor("mbd", [1, 1], F32)

    SCALE = float(1.0 / np.sqrt(HD))
    NQC = T // 512
    NKC = T // 128
    NG = NKC // 2

    with ExitStack() as ctx:
        sb = lambda n, sh, dt: ctx.enter_context(nc.sbuf_tensor(n, sh, dt))
        ps = lambda n, sh, dt: ctx.enter_context(nc.psum_tensor(n, sh, dt))
        sem = lambda n: ctx.enter_context(nc.semaphore(n))
        kt = sb("kt", [128, 2, T], BF16)
        qt = sb("qt", [128, 2, T], BF16)
        vt = sb("vt", [128, 2, NKC, HD], BF16)
        idf = sb("idf", [128, 128], F32)
        onr = sb("onr", [1, 128], BF16)
        onc = sb("onc", [128, 1], BF16)
        pbt = sb("pbt", [128, 2, 1024], BF16)
        mraw = sb("mraw", [1, 2], F32)
        mbb = sb("mbb", [128, 1], F32)
        bias = sb("bias", [128, 1], F32)
        ctmp = sb("ctmp", [128, 512], F32)
        rs = sb("rs", [1, 512], F32)
        rcs = sb("rcs", [128, 4], F32)
        stg = sb("stg", [128, 2, 4, 128], BF16)
        psc = [ps(f"psc{i}", [128, 1024], F32) for i in range(2)]
        pcx = ps("pcx", [128, 512], F32)
        psm = ps("psm", [1, 512], F32)
        ptr = ps("ptr", [128, 512], F32)
        prc = ps("prc", [128, 4], F32)
        ldc = sem("ldc")
        ldh = [sem("ldh0"), sem("ldh1")]
        odp = [sem("od0"), sem("od1")]
        mmsc = sem("mmsc"); expS = sem("expS"); pvS = sem("pvS")
        smpT = sem("smpT"); smpV = sem("smpV"); bcP = sem("bcP"); bcV = sem("bcV")
        evC = sem("evC"); trC = sem("trC"); nrmV = sem("nrmV"); cmpH = sem("cmpH")
        ssV = SelfSem(sem("ssV"))
        block = ctx.enter_context(nc.Block())

        NQ_UNITS = 4 * NQC

        @block.sync
        def _(s):
            s.dma_start(out=idf[:], in_=idf_in).then_inc(ldc, 16)
            for j in range(4):
                if j >= 2:
                    s.wait_ge(cmpH, j - 1)
                s.dma_start(out=kt[:, j % 2, :], in_=kT_in[j * 128:(j + 1) * 128, :]
                            ).then_inc(ldh[j % 2], 16)
                s.dma_start(out=qt[:, j % 2, :], in_=qT_in[j * 128:(j + 1) * 128, :]
                            ).then_inc(ldh[j % 2], 16)
                s.dma_start(out=vt[:, j % 2, :, :],
                            in_=vh_in[j].rearrange("(kc p) c -> p kc c", p=128)
                            ).then_inc(ldh[j % 2], 16)
            s.wait_ge(odp[0], 16 * (NQ_UNITS // 2))
            s.wait_ge(odp[1], 16 * (NQ_UNITS // 2))

        @block.tensor
        def _(t):
            t.wait_ge(ldc, 16)
            nsc = 0
            npv = 0
            for j in range(4):
                t.wait_ge(ldh[j % 2], 48 * (j // 2 + 1))
                if j >= 1:
                    t.wait_ge(expS, 512 * j)
                t.matmul(psc[0][:, 0:512], kt[:, j % 2, 0:128], qt[:, j % 2, 0:512],
                         start=True, stop=True).then_inc(smpT, 1)
                t.wait_ge(bcV, j + 1)
                for qc in range(NQC):
                    u = j * NQC + qc
                    if u >= 1:
                        t.wait_ge(evC, u)
                        t.wait_ge(nrmV, 2 * u - 1)
                    for g in range(NG + 1):
                        if g < NG:
                            if nsc >= 2:
                                t.wait_ge(expS, nsc - 1)
                            for i in range(2):
                                kc = g * 2 + i
                                mm = t.matmul(psc[nsc % 2][:, i * 512:(i + 1) * 512],
                                              kt[:, j % 2, kc * 128:(kc + 1) * 128],
                                              qt[:, j % 2, qc * 512:(qc + 1) * 512],
                                              start=True, stop=True)
                            mm.then_inc(mmsc, 1)
                            nsc += 1
                        if g >= 1:
                            gg = g - 1
                            t.wait_ge(expS, npv + 1)
                            for i in range(2):
                                kc = gg * 2 + i
                                t.matmul(pcx[:], vt[:, j % 2, kc, :],
                                         pbt[:, gg % 2, i * 512:(i + 1) * 512],
                                         start=(kc == 0), stop=(kc == NKC - 1),
                                         skip_group_check=True)
                                mm = t.matmul(psm[:], onc[:],
                                              pbt[:, gg % 2, i * 512:(i + 1) * 512],
                                              start=(kc == 0), stop=(kc == NKC - 1),
                                              skip_group_check=True)
                            mm.then_inc(pvS, 1)
                            npv += 1
                    t.wait_ge(evC, u + 1)
                    if u >= 1:
                        t.wait_ge(nrmV, 2 * u)
                    for i in range(4):
                        mm = t.transpose(ptr[:, i * 128:(i + 1) * 128],
                                         ctmp[:, i * 128:(i + 1) * 128], idf[:])
                        if qc == NQC - 1 and i == 0:
                            mm.then_inc(cmpH, 1)
                    t.wait_ge(nrmV, 2 * u + 1)
                    for i in range(4):
                        mm = t.transpose(prc[:, i:i + 1], rs[:, i * 128:(i + 1) * 128],
                                         idf[0:1, 0:1])
                    mm.then_inc(trC, 1)
                if j == 3:
                    pass

        @block.scalar
        def _(s):
            for j in range(4):
                for qc in range(NQC):
                    u = j * NQC + qc
                    for g in range(NG):
                        gg = u * NG + g
                        s.wait_ge(mmsc, gg + 1)
                        if gg >= 2:
                            s.wait_ge(pvS, gg - 1)
                        s.activation(pbt[:, g % 2, :], psc[gg % 2][:], AF.Exp,
                                     bias=bias[:], scale=SCALE).then_inc(expS, 1)
                    s.wait_ge(pvS, (u + 1) * NG)
                    if u >= 1:
                        s.wait_ge(trC, u)
                    s.copy(ctmp[:], pcx[:]).then_inc(evC, 1)

        @block.vector
        def _(v):
            v.memset(onr[:], 1.0)
            v.memset(onc[:], 1.0)
            for j in range(4):
                v.wait_ge(smpT, j + 1)
                ssV.mark(v.reduce_max(mraw[:, 0:1], psc[0][0:1, 0:512], AX.X))
                ssV.wait(v)
                v.sem_inc(smpV, 1)
                v.wait_ge(bcP, 32 * (j + 1))
                if j >= 1:
                    v.wait_ge(expS, 512 * j)
                v.tensor_scalar(bias[:], mbb[:], -SCALE, None, ALU.mult
                                ).then_inc(bcV, 1)
                for qc in range(NQC):
                    u = j * NQC + qc
                    v.wait_ge(pvS, (u + 1) * NG)
                    if u >= 1:
                        v.wait_ge(trC, u)
                    ssV.mark(v.reciprocal(rs[:], psm[:]))
                    ssV.wait(v)
                    v.sem_inc(nrmV, 1)
                    v.wait_ge(trC, u + 1)
                    ssV.mark(v.tensor_copy(rcs[:, 0:4], prc[:]))
                    ssV.wait(v)
                    if u >= 2:
                        v.wait_ge(odp[u % 2], 16 * (u // 2))
                    for i in range(4):
                        vv = v.tensor_scalar(stg[:, u % 2, i, :],
                                             ptr[:, i * 128:(i + 1) * 128],
                                             rcs[:, i:i + 1], None, ALU.mult)
                    vv.then_inc(nrmV, 1)

        @block.gpsimd
        def _(g):
            for j in range(4):
                g.wait_ge(smpV, j + 1)
                g.dma_start(out=mbd.ap(), in_=mraw[:, 0:1]).then_inc(bcP, 16)
                g.wait_ge(bcP, 32 * j + 16)
                g.dma_start(out=mbb[:], in_=mbd.ap().broadcast_to((128, 1))
                            ).then_inc(bcP, 16)
                for qc in range(NQC):
                    u = j * NQC + qc
                    g.wait_ge(nrmV, 2 * (u + 1))
                    g.dma_start(
                        out=out[j, qc * 512:(qc + 1) * 512, :].rearrange(
                            "(sp p) c -> p sp c", p=128),
                        in_=stg[:, u % 2, :, :]).then_inc(odp[u % 2], 16)
    return nc


def host_l2_inputs(l1_outs):
    qkvT = np.concatenate(l1_outs, axis=1)
    qT, kT, vT = qkvT[0:D], qkvT[D:2 * D], qkvT[2 * D:3 * D]
    identF = np.eye(128, dtype=np.float32)
    maps = []
    for c in range(C):
        rows = slice(c * 512, (c + 1) * 512)
        vh = np.stack([np.ascontiguousarray(vT[c * 512 + j * 128: c * 512 + (j + 1) * 128, :].T)
                       for j in range(4)])
        maps.append({"qT": np.ascontiguousarray(qT[rows]),
                     "kT": np.ascontiguousarray(kT[rows]),
                     "vh": vh, "identF": identF})
    return maps


def build_l3a():
    nc = bass.Bass("TRN2", target_bir_lowering=False)
    ctxT_in = nc.dram_tensor("ctxT", [D, TL], BF16, kind="ExternalInput").ap()
    ow_in = nc.dram_tensor("ow", [D, D], F32, kind="ExternalInput").ap()
    ob_in = nc.dram_tensor("ob_t", [128, 32], F32, kind="ExternalInput").ap()
    x_in = nc.dram_tensor("x", [TL, D], F32, kind="ExternalInput").ap()
    idf_in = nc.dram_tensor("identF", [128, 128], F32, kind="ExternalInput").ap()
    x1_out = nc.dram_tensor("x1", [TL, D], F32, kind="ExternalOutput").ap()
    ln1_out = nc.dram_tensor("xln1", [TL, D], F32, kind="ExternalOutput").ap()

    NDB = D // 128
    NCBG = D // 512

    with ExitStack() as ctx:
        sb = lambda n, sh, dt: ctx.enter_context(nc.sbuf_tensor(n, sh, dt))
        ps = lambda n, sh, dt: ctx.enter_context(nc.psum_tensor(n, sh, dt))
        sem = lambda n: ctx.enter_context(nc.semaphore(n))
        x1b = sb("x1b", [128, 4, D], F32)
        ctxb = sb("ctxb", [128, NDB, 512], BF16)
        wst = sb("wst", [128, 2, NDB, 512], BF16)
        aotmp = sb("aotmp", [128, 512], F32)
        obt = sb("obt", [128, 32], F32)
        idf = sb("idf", [128, 128], F32)
        sq = sb("sq", [128, D], F32)
        stt = sb("stt", [128, 2, 8], F32)
        cst = sb("cst", [128, 1], F32)
        ln1t = sb("ln1t", [128, 1, D], F32)
        pmm = [ps(f"pmm{i}", [128, 512], F32) for i in range(4)]
        ptr = [ps(f"ptr{i}", [128, 512], F32) for i in range(2)]
        ldc = sem("ldc")
        ldx = [sem("ldx0"), sem("ldx1")]
        ldcx = [sem("ldcx0"), sem("ldcx1")]
        wlp = [sem("wl0"), sem("wl1")]
        odx = [sem("odx0"), sem("odx1")]
        odl = [sem("odl0"), sem("odl1")]
        mmS = sem("mmS"); evA = sem("evA"); trS = sem("trS"); adV = sem("adV")
        cpS = sem("cpS"); cpV1 = sem("cpV1"); cpV = sem("cpV")
        ssS = SelfSem(sem("ssS")); ssV = SelfSem(sem("ssV"))
        block = ctx.enter_context(nc.Block())


        @block.sync
        def _(s):
            s.dma_start(out=idf[:], in_=idf_in).then_inc(ldc, 16)
            s.dma_start(out=obt[:], in_=ob_in).then_inc(ldc, 16)
            for h in range(2):
                if h >= 1:
                    s.wait_ge(cpV, 4 * h)
                    s.wait_ge(odx[h - 1], 64)
                for tt in range(4):
                    s.dma_start(out=x1b[:, tt, :],
                                in_=x_in[(h * 4 + tt) * 128:(h * 4 + tt + 1) * 128, :]
                                ).then_inc(ldx[h], 16)
                s.dma_start(out=ctxb[:],
                            in_=ctxT_in[:, h * 512:(h + 1) * 512].rearrange(
                                "(db p) t -> p db t", p=128)).then_inc(ldcx[h], 16)
            s.wait_ge(odx[0], 64); s.wait_ge(odx[1], 64)
            s.wait_ge(odl[0], 128)

        @block.gpsimd
        def _(g):
            for h in range(2):
                for cbg in range(NCBG):
                    k = h * NCBG + cbg
                    if k >= 2:
                        g.wait_ge(mmS, (k - 1) * 4)
                    src = ow_in[:, cbg * 512:(cbg + 1) * 512].rearrange(
                        "(db p) c -> p db c", p=128)
                    for q in range(4):
                        g.dma_start(out=wst[:, k % 2, q * 8:(q + 1) * 8, :],
                                    in_=src[:, q * 8:(q + 1) * 8, :]
                                    ).then_inc(wlp[k % 2], 16)
                g.wait_ge(adV, 128 * (h + 1))
                for tt in range(4):
                    gt = h * 4 + tt
                    g.dma_start(out=x1_out[gt * 128:(gt + 1) * 128, :],
                                in_=x1b[:, tt, :]).then_inc(odx[h], 16)
                    g.wait_ge(cpV, gt + 1)
                    g.dma_start(out=ln1_out[gt * 128:(gt + 1) * 128, :],
                                in_=ln1t[:, 0, :]).then_inc(odl[0], 16)

        @block.tensor
        def _(t):
            t.wait_ge(ldc, 32)
            for h in range(2):
                t.wait_ge(ldcx[h], 16)
                for cbg in range(NCBG):
                    k = h * NCBG + cbg
                    t.wait_ge(wlp[k % 2], 64 * (k // 2 + 1))
                    for cb in range(4):
                        u2 = cbg * 4 + cb
                        u = h * 32 + u2
                        if u >= 4:
                            t.wait_ge(evA, u - 3)
                        for db in range(NDB):
                            mm = t.matmul(pmm[u % 4][:],
                                          wst[:, k % 2, db, cb * 128:(cb + 1) * 128],
                                          ctxb[:, db, :],
                                          start=(db == 0), stop=(db == NDB - 1))
                        mm.then_inc(mmS, 1)
                        t.wait_ge(evA, u + 1)
                        if u >= 2:
                            t.wait_ge(adV, 4 * (u - 1))
                        for i in range(4):
                            mm = t.transpose(ptr[u % 2][:, i * 128:(i + 1) * 128],
                                             aotmp[:, i * 128:(i + 1) * 128], idf[:])
                        mm.then_inc(trS, 1)

        @block.scalar
        def _(s):
            for h in range(2):
                for u2 in range(32):
                    u = h * 32 + u2
                    cbg, cb = divmod(u2, 4)
                    s.wait_ge(mmS, u + 1)
                    if u >= 1:
                        s.wait_ge(trS, u)
                    s.activation(aotmp[:], pmm[u % 4][:], AF.Identity,
                                 bias=obt[:, cbg * 4 + cb:cbg * 4 + cb + 1],
                                 scale=1.0).then_inc(evA, 1)
                for tt in range(4):
                    gt = h * 4 + tt
                    s.wait_ge(adV, 128 * (h + 1))
                    s.wait_ge(cpV1, gt + 1)
                    b = stt[:, gt % 2, :]
                    ssS.mark(s.activation(sq[:], x1b[:, tt, :], AF.Square,
                                          accum_out=b[:, 1:2]))
                    ssS.wait(s)
                    ssS.mark(s.mul(b[:, 2:3], b[:, 0:1], 1.0 / D))
                    ssS.mark(s.mul(b[:, 3:4], b[:, 1:2], 1.0 / D))
                    ssS.wait(s)
                    ssS.mark(s.activation(b[:, 4:5], b[:, 2:3], AF.Square))
                    ssS.wait(s)
                    ssS.mark(s.mul(b[:, 4:5], b[:, 4:5], -1.0))
                    ssS.wait(s)
                    ssS.mark(s.add(b[:, 4:5], b[:, 3:4], b[:, 4:5]))
                    ssS.wait(s)
                    ssS.mark(s.activation(b[:, 5:6], b[:, 4:5], AF.Ln,
                                          bias=cst[:, 0:1], scale=1.0))
                    ssS.wait(s)
                    s.activation(b[:, 6:7], b[:, 5:6], AF.Exp, bias=0.0, scale=-0.5
                                 ).then_inc(cpS, 1)

        @block.vector
        def _(v):
            v.memset(cst[:], LN_EPS)
            for h in range(2):
                for u2 in range(32):
                    u = h * 32 + u2
                    cbg, cb = divmod(u2, 4)
                    v.wait_ge(trS, u + 1)
                    v.wait_ge(ldx[h], 64)
                    oc = cbg * 512 + cb * 128
                    for i in range(4):
                        vv = v.tensor_tensor(x1b[:, i, oc:oc + 128],
                                             x1b[:, i, oc:oc + 128],
                                             ptr[u % 2][:, i * 128:(i + 1) * 128],
                                             ALU.add)
                    vv.then_inc(adV, 4)
                for tt in range(4):
                    gt = h * 4 + tt
                    v.wait_ge(adV, 128 * (h + 1))
                    b = stt[:, gt % 2, :]
                    ssV.mark(v.reduce_sum(b[:, 0:1], x1b[:, tt, :], AX.X))
                    ssV.wait(v)
                    v.sem_inc(cpV1, 1)
                    v.wait_ge(cpS, gt + 1)
                    if gt >= 1:
                        v.wait_ge(odl[0], 16 * gt)
                    ssV.mark(v.tensor_tensor(b[:, 7:8], b[:, 2:3], b[:, 6:7], ALU.mult))
                    ssV.wait(v)
                    v.tensor_scalar(ln1t[:, 0, :], x1b[:, tt, :],
                                    b[:, 6:7], b[:, 7:8], ALU.mult, ALU.subtract
                                    ).then_inc(cpV, 1)

    return nc


def host_l3a_inputs(inputs, l2_ctx):
    x = np.ascontiguousarray(inputs["L0_attn_x"].reshape(T, D))
    ow = inputs["L0_attn_attn_ow"]
    obt = np.ascontiguousarray(inputs["L0_attn_attn_ob"].reshape(32, 128).T)
    identF = np.eye(128, dtype=np.float32)
    ctxT = np.ascontiguousarray(l2_ctx.T)
    return [
        {"ctxT": np.ascontiguousarray(ctxT[:, c * TL:(c + 1) * TL]),
         "ow": ow, "ob_t": obt,
         "x": np.ascontiguousarray(x[c * TL:(c + 1) * TL]),
         "identF": identF}
        for c in range(C)
    ]

SINK_ITERS = 50
SINK_TAU = 0.1
RMS_EPS0 = 1.1920929e-07


def build_l3b():
    nc = bass.Bass("TRN2", target_bir_lowering=False)
    x1_in = nc.dram_tensor("x1", [TL, D], F32, kind="ExternalInput").ap()
    ln1_in = nc.dram_tensor("xln1", [TL, D], F32, kind="ExternalInput").ap()
    wres_in = nc.dram_tensor("wres_t", [D, 24], F32, kind="ExternalInput").ap()
    b24_in = nc.dram_tensor("base24", [24, 1], F32, kind="ExternalInput").ap()
    gate_in = nc.dram_tensor("gate", [1, 1], F32, kind="ExternalInput").ap()
    wbr_in = nc.dram_tensor("wbr", [1024, 1024], F32, kind="ExternalInput").ap()
    idb_in = nc.dram_tensor("ident", [128, 128], BF16, kind="ExternalInput").ap()
    idf_in = nc.dram_tensor("identF", [128, 128], F32, kind="ExternalInput").ap()
    xb_out = nc.dram_tensor("xb", [TL, D], F32, kind="ExternalOutput").ap()
    ln2T_out = nc.dram_tensor("xln2T", [D, TL], BF16, kind="ExternalOutput").ap()

    NDB = D // 128
    NTT = 8
    G = NTT

    sink_steps = []

    def VS(eng):
        def deco(fn):
            sink_steps.append((eng, fn))
            return fn
        return deco

    with ExitStack() as ctx:
        sb = lambda n, sh, dt: ctx.enter_context(nc.sbuf_tensor(n, sh, dt))
        ps = lambda n, sh, dt: ctx.enter_context(nc.psum_tensor(n, sh, dt))
        sem = lambda n: ctx.enter_context(nc.semaphore(n))
        xnT = sb("xnT", [128, NDB, TL], BF16)
        xl1 = sb("xl1", [128, D], F32)
        x1t = sb("x1t", [128, D], F32)
        xbt = sb("xbt", [128, D], F32)
        sq = sb("sq", [128, D], F32)
        tokbf = sb("tokbf", [128, D], BF16)
        stt = sb("stt", [128, 2, 8], F32)
        cst = sb("cst", [128, 2], F32)
        wresb = sb("wresb", [128, NDB, 24], BF16)
        b24 = sb("b24", [24, 1], F32)
        gsb = sb("gsb", [128, 2], F32)
        lsb = sb("lsb", [24, TL], F32)
        ltok = sb("ltok", [128, G, 24], F32)
        Zp = sb("Zp", [128, G, 4, 4], F32)
        t2b = sb("t2b", [128, G, 4, 4], F32)
        escr = sb("escr", [128, G, 4, 4], F32)
        ssb = sb("ssb", [128, G, 4], F32)
        lub = sb("lub", [128, G, 4], F32)
        lvb = sb("lvb", [128, G, 4], F32)
        Mb = sb("Mb", [128, G], F32)
        rcb = sb("rcb", [128, G], F32)
        mx1 = sb("mx1", [128, G, 4], F32)
        mx2 = sb("mx2", [128, G, 4], F32)
        Hres = sb("Hres", [128, G, 4, 4], F32)
        Hpre = sb("Hpre", [128, G, 4], F32)
        Hpost = sb("Hpost", [128, G, 4], F32)
        sftm = sb("sftm", [128, G, 4], F32)
        xpre = sb("xpre", [128, 2, 1024], F32)
        wbrb = sb("wbrb", [128, 8, 1024], BF16)
        xpreT = sb("xpreT", [128, 8, TL], BF16)
        ysb = sb("ysb", [128, 8, 512], F32)
        idb = sb("idb", [128, 128], BF16)
        idf = sb("idf", [128, 128], F32)
        ptrx = [ps(f"ptrx{i}", [128, 1024], BF16) for i in range(2)]
        plog = ps("plog", [24, 512], F32)
        ptr24 = ps("ptr24", [128, 24], F32)
        pwb = ps("pwb", [128, 512], F32)
        ptry = ps("ptry", [128, 1024], F32)
        ldc = sem("ldc"); ldg = sem("ldg")
        lxd = sem("lxd"); l1d = sem("l1d")
        odb = sem("odb"); odz = sem("odz")
        xf = sem("xf"); x1f = sem("x1f")
        cpS = sem("cpS"); cpV1 = sem("cpV1"); cpV2 = sem("cpV2"); cstR = sem("cstR")
        trX = sem("trX"); evX = sem("evX")
        mmL = sem("mmL"); evL = sem("evL"); trL = sem("trL"); ltV = sem("ltV")
        snkS = sem("snkS"); snkV = sem("snkV")
        xpV = sem("xpV"); trP = sem("trP"); evP = sem("evP")
        mmW = sem("mmW"); evW = sem("evW"); trY = sem("trY"); yuV = sem("yuV")
        cpB = sem("cpB"); trZ = sem("trZ"); evZ = sem("evZ")
        ssS = SelfSem(sem("ssS")); ssV = SelfSem(sem("ssV"))
        block = ctx.enter_context(nc.Block())

        bc44_j = lambda a: a.rearrange("p g j -> p g () j").broadcast_to((128, G, 4, 4))
        bc44_i = lambda a: a.rearrange("p g i -> p g i ()").broadcast_to((128, G, 4, 4))
        bc4 = lambda a: a.rearrange("p g -> p g ()").broadcast_to((128, G, 4))

        @VS('V')
        def _(v):
            ssV.mark(v.reduce_max(Mb[:], ltok[:, :, 0:16], AX.X))
            ssV.wait(v)
            return v.tensor_tensor(t2b[:], ltok[:, :, 0:16].rearrange(
                "p g (i j) -> p g i j", i=4),
                Mb[:].rearrange("p g -> p g ()").rearrange("p g o -> p g o ()").broadcast_to((128, G, 4, 4)),
                ALU.subtract)

        @VS('V')
        def _(v):
            return v.tensor_scalar(Zp[:], t2b[:], 1.0 / SINK_TAU, None, ALU.mult)

        bc4j = lambda a: a.rearrange("p g j -> p g () j").broadcast_to((128, G, 4, 4))
        bc4i = lambda a: a.rearrange("p g i -> p g i ()").broadcast_to((128, G, 4, 4))

        for _it in range(SINK_ITERS):
            @VS('V')
            def _(v, it=_it):
                if it > 0:
                    ssV.mark(v.tensor_tensor(lvb[:], lvb[:], mx2[:], ALU.add))
                    ssV.wait(v)
                    ssV.mark(v.tensor_tensor(t2b[:], Zp[:], bc4j(lvb[:]),
                                             ALU.subtract))
                    ssV.wait(v)
                else:
                    ssV.mark(v.tensor_copy(t2b[:], Zp[:]))
                    ssV.wait(v)
                ssV.mark(v.reduce_max(mx1[:], t2b[:], AX.X))
                ssV.wait(v)
                return v.tensor_tensor(t2b[:], t2b[:], bc4i(mx1[:]), ALU.subtract)

            @VS('S')
            def _(s):
                return s.activation(escr[:], t2b[:], AF.Exp)

            @VS('V')
            def _(v):
                return v.reduce_sum(ssb[:], escr[:], AX.X)

            @VS('S')
            def _(s):
                return s.activation(lub[:], ssb[:], AF.Ln)

            @VS('V')
            def _(v):
                ssV.mark(v.tensor_tensor(lub[:], lub[:], mx1[:], ALU.add))
                ssV.wait(v)
                ssV.mark(v.tensor_tensor(t2b[:], Zp[:], bc4i(lub[:]), ALU.subtract))
                ssV.wait(v)
                ssV.mark(v.reduce_max(mx2[:], t2b[:].rearrange("p g i j -> p g j i"),
                                      AX.X))
                ssV.wait(v)
                return v.tensor_tensor(escr[:], t2b[:], bc4j(mx2[:]), ALU.subtract)

            @VS('S')
            def _(s):
                return s.activation(escr[:], escr[:], AF.Exp)

            @VS('V')
            def _(v):
                return v.reduce_sum(ssb[:], escr[:].rearrange("p g i j -> p g j i"),
                                    AX.X)

            @VS('S')
            def _(s):
                return s.activation(lvb[:], ssb[:], AF.Ln)

        @VS('V')
        def _(v):
            ssV.mark(v.tensor_tensor(lvb[:], lvb[:], mx2[:], ALU.add))
            ssV.wait(v)
            return v.tensor_tensor(t2b[:], t2b[:], bc4j(lvb[:]), ALU.subtract)

        @VS('S')
        def _(s):
            return s.activation(Hres[:], t2b[:], AF.Exp)

        @VS('V')
        def _(v):
            ssV.mark(v.tensor_scalar(Hres[:], Hres[:], gsb[:, 0:1], None, ALU.mult))
            ssV.wait(v)
            d16 = Hres[:].rearrange("p g i j -> p g (i j)")
            return v.tensor_scalar(d16[:, :, 0:16:5], d16[:, :, 0:16:5],
                                   gsb[:, 1:2], None, ALU.add)

        for _lo, _H in ((16, Hpre), (20, Hpost)):
            def _mk(lo, H):
                @VS('V')
                def _(v):
                    ssV.mark(v.reduce_max(Mb[:], ltok[:, :, lo:lo + 4], AX.X))
                    ssV.wait(v)
                    return v.tensor_tensor(sftm[:], ltok[:, :, lo:lo + 4],
                                           bc4(Mb[:]), ALU.subtract)

                @VS('S')
                def _(s):
                    return s.activation(escr[:, :, 0, :], sftm[:], AF.Exp)

                @VS('V')
                def _(v):
                    ssV.mark(v.reduce_sum(Mb[:], escr[:, :, 0, :], AX.X))
                    ssV.wait(v)
                    ssV.mark(v.reciprocal(rcb[:], Mb[:]))
                    ssV.wait(v)
                    return v.tensor_tensor(H[:], escr[:, :, 0, :], bc4(rcb[:]),
                                           ALU.mult)
            _mk(_lo, _H)

        n_sinkV = sum(1 for e, _ in sink_steps if e == 'V')
        n_sinkS = sum(1 for e, _ in sink_steps if e == 'S')

        @block.sync
        def _(s):
            s.dma_start(out=idb[:], in_=idb_in).then_inc(ldc, 16)
            s.dma_start(out=idf[:], in_=idf_in).then_inc(ldc, 16)
            s.dma_start(out=b24[:], in_=b24_in).then_inc(ldc, 16)
            s.dma_start(out=gsb[:, 0:1], in_=gate_in.broadcast_to((128, 1))
                        ).then_inc(ldc, 16)
            for i in range(16):
                gt = i % 8
                if i >= 1:
                    s.wait_ge(xf, i)
                s.dma_start(out=xl1[:], in_=ln1_in[gt * 128:(gt + 1) * 128, :]
                            ).then_inc(lxd, 16)
            for gt in range(8):
                if gt >= 1:
                    s.wait_ge(x1f, gt)
                s.dma_start(out=x1t[:], in_=x1_in[gt * 128:(gt + 1) * 128, :]
                            ).then_inc(l1d, 16)
                s.wait_ge(xf, 16 + gt)
                s.dma_start(out=xl1[:], in_=ln1_in[gt * 128:(gt + 1) * 128, :]
                            ).then_inc(lxd, 16)
            s.wait_ge(odb, NTT * 16)
            s.wait_ge(odz, 64)

        @block.gpsimd
        def _(g):
            g.dma_start(out=wresb[:], in_=wres_in.rearrange("(db p) e -> p db e", p=128)
                        ).then_inc(ldg, 16)
            g.dma_start(out=wbrb[:], in_=wbr_in.rearrange("(db p) c -> p db c", p=128)
                        ).then_inc(ldg, 16)
            for gt in range(NTT):
                g.wait_ge(cpB, gt + 1)
                g.dma_start(out=xb_out[gt * 128:(gt + 1) * 128, :], in_=xbt[:]
                            ).then_inc(odb, 16)
            g.wait_ge(evZ, 32)
            ln2T_r = ln2T_out.rearrange("(db p) t -> p db t", p=128)
            for q in range(4):
                g.dma_start(out=ln2T_r[:, q * 8:(q + 1) * 8, :],
                            in_=xnT[:, q * 8:(q + 1) * 8, :]).then_inc(odz, 16)

        @block.tensor
        def _(t):
            t.wait_ge(ldc, 64)
            for gt in range(NTT):
                t.wait_ge(xf, gt + 1)
                for k in range(4):
                    pk = gt * 4 + k
                    if pk >= 2:
                        t.wait_ge(evX, pk - 1)
                    for j in range(8):
                        db = k * 8 + j
                        mm = t.transpose(ptrx[pk % 2][:, j * 128:(j + 1) * 128],
                                         tokbf[:, db * 128:(db + 1) * 128], idb[:])
                    mm.then_inc(trX, 1)
            t.wait_ge(evX, 32)
            t.wait_ge(ldg, 32)
            for tch in range(2):
                if tch == 1:
                    t.wait_ge(evL, 1)
                for db in range(NDB):
                    mm = t.matmul(plog[:], wresb[:, db, :],
                                  xnT[:, db, tch * 512:(tch + 1) * 512],
                                  start=(db == 0), stop=(db == NDB - 1))
                mm.then_inc(mmL, 1)
            for tt in range(NTT):
                t.wait_ge(evL, 2)
                if tt >= 1:
                    t.wait_ge(ltV, tt)
                mm = t.transpose(ptr24[:], lsb[0:24, tt * 128:(tt + 1) * 128],
                                 idf[0:24, 0:24])
                mm.then_inc(trL, 1)
            for gt in range(NTT):
                t.wait_ge(xpV, gt + 1)
                if gt >= 2:
                    t.wait_ge(evP, gt - 1)
                for j in range(8):
                    mm = t.transpose(ptrx[gt % 2][:, j * 128:(j + 1) * 128],
                                     tokbf[:, j * 128:(j + 1) * 128], idb[:])
                mm.then_inc(trP, 1)
            t.wait_ge(ldg, 32)
            for tch in range(2):
                t.wait_ge(evP, 8)
                for ob in range(8):
                    u = tch * 8 + ob
                    if u >= 1:
                        t.wait_ge(evW, u)
                    for db in range(8):
                        mm = t.matmul(pwb[:], wbrb[:, db, ob * 128:(ob + 1) * 128],
                                      xpreT[:, db, tch * 512:(tch + 1) * 512],
                                      start=(db == 0), stop=(db == 7))
                    mm.then_inc(mmW, 1)
                for tl in range(4):
                    gt = tch * 4 + tl
                    if gt >= 1:
                        zt = gt - 1
                        t.wait_ge(cpV2, zt + 1)
                        for k in range(4):
                            pk = zt * 4 + k
                            t.wait_ge(evZ, max(0, pk - 1))
                            for j in range(8):
                                db = k * 8 + j
                                mm = t.transpose(
                                    ptrx[pk % 2][:, j * 128:(j + 1) * 128],
                                    tokbf[:, db * 128:(db + 1) * 128], idb[:])
                            mm.then_inc(trZ, 1)
                    t.wait_ge(evW, (tch + 1) * 8)
                    if gt >= 1:
                        t.wait_ge(yuV, gt)
                    for j in range(8):
                        mm = t.transpose(ptry[:, j * 128:(j + 1) * 128],
                                         ysb[:, j, tl * 128:(tl + 1) * 128], idf[:])
                    mm.then_inc(trY, 1)
            zt = NTT - 1
            t.wait_ge(cpV2, zt + 1)
            for k in range(4):
                pk = zt * 4 + k
                t.wait_ge(evZ, max(0, pk - 1))
                for j in range(8):
                    db = k * 8 + j
                    mm = t.transpose(ptrx[pk % 2][:, j * 128:(j + 1) * 128],
                                     tokbf[:, db * 128:(db + 1) * 128], idb[:])
                mm.then_inc(trZ, 1)

        @block.scalar
        def _(s):
            s.wait_ge(cstR, 1)
            for gt in range(NTT):
                s.wait_ge(lxd, 16 * (gt + 1))
                b = stt[:, gt % 2, :]
                ssS.mark(s.activation(sq[:], xl1[:], AF.Square, accum_out=b[:, 1:2]))
                ssS.wait(s)
                ssS.mark(s.mul(b[:, 3:4], b[:, 1:2], 1.0 / D))
                ssS.wait(s)
                ssS.mark(s.activation(b[:, 5:6], b[:, 3:4], AF.Ln,
                                      bias=cst[:, 1:2], scale=1.0))
                ssS.wait(s)
                s.activation(b[:, 6:7], b[:, 5:6], AF.Exp, bias=0.0, scale=-0.5
                             ).then_inc(cpS, 1)
                for k in range(4):
                    pk = gt * 4 + k
                    s.wait_ge(trX, pk + 1)
                    s.copy(xnT[:, k * 8:(k + 1) * 8, gt * 128:(gt + 1) * 128],
                           ptrx[pk % 2][:].rearrange("p (a b) -> p a b", a=8)
                           ).then_inc(evX, 1)
            s.wait_ge(ldc, 64)
            ssS.mark(s.activation(gsb[:, 0:1], gsb[:, 0:1], AF.Sigmoid))
            ssS.wait(s)
            s.sem_inc(snkS, 1)
            for tch in range(2):
                s.wait_ge(mmL, tch + 1)
                s.activation(lsb[:, tch * 512:(tch + 1) * 512], plog[:], AF.Identity,
                             bias=b24[:], scale=1.0).then_inc(evL, 1)
            sv = 0
            for eng, fn in sink_steps:
                if eng == 'V':
                    sv += 1
                else:
                    s.wait_ge(snkV, sv)
                    inst = fn(s)
                    ssS.mark(inst)
                    ssS.wait(s)
                    s.sem_inc(snkS, 1)
            for gt in range(NTT):
                s.wait_ge(trP, gt + 1)
                s.copy(xpreT[:, :, gt * 128:(gt + 1) * 128],
                       ptrx[gt % 2][:].rearrange("p (a b) -> p a b", a=8)
                       ).then_inc(evP, 1)
            for gt in range(-1, NTT):
                if gt in (-1, 3):
                    tch = 0 if gt == -1 else 1
                    for ob in range(8):
                        u = tch * 8 + ob
                        s.wait_ge(mmW, u + 1)
                        s.copy(ysb[:, ob, :], pwb[:]).then_inc(evW, 1)
                if gt < 0:
                    continue
                s.wait_ge(cpB, gt + 1)
                b = stt[:, gt % 2, :]
                ssS.mark(s.activation(sq[:], xbt[:], AF.Square, accum_out=b[:, 1:2]))
                ssS.wait(s)
                s.wait_ge(cpV1, gt + 1)
                ssS.mark(s.mul(b[:, 2:3], b[:, 0:1], 1.0 / D))
                ssS.mark(s.mul(b[:, 3:4], b[:, 1:2], 1.0 / D))
                ssS.wait(s)
                ssS.mark(s.activation(b[:, 4:5], b[:, 2:3], AF.Square))
                ssS.wait(s)
                ssS.mark(s.mul(b[:, 4:5], b[:, 4:5], -1.0))
                ssS.wait(s)
                ssS.mark(s.add(b[:, 4:5], b[:, 3:4], b[:, 4:5]))
                ssS.wait(s)
                ssS.mark(s.activation(b[:, 5:6], b[:, 4:5], AF.Ln,
                                      bias=cst[:, 0:1], scale=1.0))
                ssS.wait(s)
                s.activation(b[:, 6:7], b[:, 5:6], AF.Exp, bias=0.0, scale=-0.5
                             ).then_inc(cpS, 1)
                for k in range(4):
                    s.wait_ge(trZ, gt * 4 + k + 1)
                    s.copy(xnT[:, k * 8:(k + 1) * 8, gt * 128:(gt + 1) * 128],
                           ptrx[(gt * 4 + k) % 2][:].rearrange(
                               "p (a b) -> p a b", a=8)).then_inc(evZ, 1)

        @block.vector
        def _(v):
            ssV.mark(v.memset(cst[:, 0:1], LN_EPS))
            ssV.mark(v.memset(cst[:, 1:2], RMS_EPS0))
            ssV.wait(v)
            v.sem_inc(cstR, 1)
            for gt in range(NTT):
                v.wait_ge(cpS, gt + 1)
                b = stt[:, gt % 2, :]
                if gt >= 1:
                    v.wait_ge(trX, gt * 4)
                v.tensor_scalar(tokbf[:], xl1[:], b[:, 6:7], None, ALU.mult
                                ).then_inc(xf, 1)
            v.wait_ge(snkS, 1)
            ssV.mark(v.tensor_scalar(gsb[:, 1:2], gsb[:, 0:1], -1.0, 1.0,
                                     ALU.mult, ALU.add))
            ssV.wait(v)
            for tt in range(NTT):
                v.wait_ge(trL, tt + 1)
                ssV.mark(v.tensor_copy(ltok[:, tt, :], ptr24[:]))
                ssV.wait(v)
                v.sem_inc(ltV, 1)
            ss_count = 1
            for eng, fn in sink_steps:
                if eng == 'S':
                    ss_count += 1
                else:
                    v.wait_ge(snkS, ss_count)
                    inst = fn(v)
                    ssV.mark(inst)
                    ssV.wait(v)
                    v.sem_inc(snkV, 1)
            for gt in range(NTT):
                v.wait_ge(lxd, 16 * (NTT + gt + 1))
                h = lambda n: Hpre[:, gt, n:n + 1]
                ssV.mark(v.tensor_scalar(xpre[:, 0, :], xl1[:, 0:1024], h(0),
                                         None, ALU.mult))
                ssV.wait(v)
                ssV.mark(v.scalar_tensor_tensor(xpre[:, 1, :], xl1[:, 1024:2048],
                                                h(1), xpre[:, 0, :], ALU.mult, ALU.add))
                ssV.wait(v)
                ssV.mark(v.scalar_tensor_tensor(xpre[:, 0, :], xl1[:, 2048:3072],
                                                h(2), xpre[:, 1, :], ALU.mult, ALU.add))
                ssV.wait(v)
                ssV.mark(v.scalar_tensor_tensor(xpre[:, 1, :], xl1[:, 3072:4096],
                                                h(3), xpre[:, 0, :], ALU.mult, ALU.add))
                ssV.wait(v)
                v.sem_inc(xf, 1)
                if gt == 0:
                    v.wait_ge(trX, 32)
                else:
                    v.wait_ge(trP, gt)
                v.tensor_copy(tokbf[:, 0:1024], xpre[:, 1, :]).then_inc(xpV, 1)
            for tch in range(2):
                for tl in range(4):
                    gt = tch * 4 + tl
                    v.wait_ge(lxd, 16 * (2 * NTT + gt + 1))
                    v.wait_ge(trY, gt + 1)
                    if gt >= 1:
                        v.wait_ge(odb, 16 * gt)
                    for i in range(4):
                        dst = xbt[:, i * 1024:(i + 1) * 1024]
                        hs = lambda jj: Hres[:, gt, i, jj:jj + 1]
                        ssV.mark(v.tensor_scalar(dst, xl1[:, 0:1024], hs(0),
                                                 None, ALU.mult))
                        ssV.wait(v)
                        for jj in range(1, 4):
                            ssV.mark(v.scalar_tensor_tensor(
                                dst, xl1[:, jj * 1024:(jj + 1) * 1024], hs(jj),
                                dst, ALU.mult, ALU.add))
                            ssV.wait(v)
                    v.sem_inc(xf, 1)
                    for i in range(4):
                        dst = xbt[:, i * 1024:(i + 1) * 1024]
                        ssV.mark(v.scalar_tensor_tensor(
                            dst, ptry[:], Hpost[:, gt, i:i + 1], dst,
                            ALU.mult, ALU.add))
                        ssV.wait(v)
                    v.sem_inc(yuV, 1)
                    v.wait_ge(l1d, 16 * (gt + 1))
                    ssV.mark(v.tensor_tensor(xbt[:], xbt[:], x1t[:], ALU.add))
                    ssV.wait(v)
                    v.sem_inc(x1f, 1)
                    v.sem_inc(cpB, 1)
                    b = stt[:, gt % 2, :]
                    ssV.mark(v.reduce_sum(b[:, 0:1], xbt[:], AX.X))
                    ssV.wait(v)
                    v.sem_inc(cpV1, 1)
                    v.wait_ge(cpS, NTT + gt + 1)
                    if gt == 0:
                        v.wait_ge(trP, 8)
                    else:
                        v.wait_ge(trZ, gt * 4)
                    ssV.mark(v.tensor_tensor(b[:, 7:8], b[:, 2:3], b[:, 6:7],
                                             ALU.mult))
                    ssV.wait(v)
                    v.tensor_scalar(tokbf[:], xbt[:], b[:, 6:7], b[:, 7:8],
                                    ALU.mult, ALU.subtract).then_inc(cpV2, 1)
    return nc


def host_l3b_inputs(inputs, x1_full, ln1_full):
    wres = np.concatenate([inputs["L0_ffn_W_res"], inputs["L0_ffn_W_pre"],
                           inputs["L0_ffn_W_post"]], 0)
    wres_t = np.ascontiguousarray(wres.T)
    b24 = np.concatenate([inputs["L0_ffn_H_res_base"].reshape(16),
                          inputs["L0_ffn_H_pre_base"],
                          inputs["L0_ffn_H_post_base"]]).reshape(24, 1)
    b24 = np.ascontiguousarray(b24.astype(np.float32))
    gate = inputs["L0_ffn_gate"].reshape(1, 1)
    wbr = inputs["L0_ffn_W_branch"]
    ident = np.eye(128, dtype=BF)
    identF = np.eye(128, dtype=np.float32)
    return [
        {"x1": np.ascontiguousarray(x1_full[c * TL:(c + 1) * TL]),
         "xln1": np.ascontiguousarray(ln1_full[c * TL:(c + 1) * TL]),
         "wres_t": wres_t, "base24": b24, "gate": gate, "wbr": wbr,
         "ident": ident, "identF": identF}
        for c in range(C)
    ]


DIN = 8192
THL = TL + 3


def build_l4a():
    nc = bass.Bass("TRN2", target_bir_lowering=False)
    xlh_in = nc.dram_tensor("xlh", [D, TL], BF16, kind="ExternalInput").ap()
    w_in = nc.dram_tensor("inw", [D, 2 * DIN], F32, kind="ExternalInput").ap()
    xp_out = nc.dram_tensor("xpart", [DIN, TL], F32, kind="ExternalOutput").ap()
    sz_out = nc.dram_tensor("sz", [DIN, TL], F32, kind="ExternalOutput").ap()

    NDB = D // 128
    NCBG = 2 * DIN // 512

    units = []
    for cbg in range(NCBG):
        is_z = cbg >= 16
        for cb in range(4):
            for (lo, n) in ((0, 512), (512, 512)):
                units.append((cbg, cb, lo, n, is_z))
    NU = len(units)

    with ExitStack() as ctx:
        sb = lambda n, sh, dt: ctx.enter_context(nc.sbuf_tensor(n, sh, dt))
        ps = lambda n, sh, dt: ctx.enter_context(nc.psum_tensor(n, sh, dt))
        sem = lambda n: ctx.enter_context(nc.semaphore(n))
        xlh = sb("xlh_s", [128, NDB, TL], BF16)
        wst = sb("wst", [128, 2, NDB, 512], BF16)
        stg = sb("stg", [128, 4, 512], F32)
        pin = [ps(f"pin{i}", [128, 512], F32) for i in range(4)]
        ldx = sem("ldx")
        wlp = [sem("wl0"), sem("wl1")]
        odp = [sem("od0"), sem("od1"), sem("od2"), sem("od3")]
        mmS = sem("mmS"); evS = sem("evS")
        block = ctx.enter_context(nc.Block())

        od_cnt = [0, 0, 0, 0]
        od_sched = []
        for i in range(NU):
            par = i % 4
            od_sched.append((par, od_cnt[par]))
            od_cnt[par] += 1

        @block.sync
        def _(s):
            for q in range(4):
                s.dma_start(out=xlh[:, q * 8:(q + 1) * 8, :],
                            in_=xlh_in.rearrange("(db p) t -> p db t", p=128)
                            [:, q * 8:(q + 1) * 8, :]).then_inc(ldx, 16)
            for par in range(4):
                s.wait_ge(odp[par], 16 * od_cnt[par])

        @block.gpsimd
        def _(g):
            for cbg in range(NCBG):
                if cbg >= 2:
                    done = sum(1 for u in units if u[0] < cbg - 1)
                    g.wait_ge(mmS, done)
                src = w_in[:, cbg * 512:(cbg + 1) * 512].rearrange(
                    "(db p) c -> p db c", p=128)
                for q in range(4):
                    g.dma_start(out=wst[:, cbg % 2, q * 8:(q + 1) * 8, :],
                                in_=src[:, q * 8:(q + 1) * 8, :]
                                ).then_inc(wlp[cbg % 2], 16)
                if cbg >= 1:
                    for i, (c2, cb, lo, n, is_z) in enumerate(units):
                        if c2 != cbg - 1:
                            continue
                        par, cnt = od_sched[i]
                        g.wait_ge(evS, i + 1)
                        row0 = (c2 - 16 if is_z else c2) * 512 + cb * 128
                        dst = (sz_out if is_z else xp_out)[row0:row0 + 128, lo:lo + n]
                        g.dma_start(out=dst, in_=stg[:, par, 0:n]
                                    ).then_inc(odp[par], 16)
            for i, (c2, cb, lo, n, is_z) in enumerate(units):
                if c2 != NCBG - 1:
                    continue
                par, cnt = od_sched[i]
                g.wait_ge(evS, i + 1)
                row0 = (c2 - 16 if is_z else c2) * 512 + cb * 128
                dst = (sz_out if is_z else xp_out)[row0:row0 + 128, lo:lo + n]
                g.dma_start(out=dst, in_=stg[:, par, 0:n]).then_inc(odp[par], 16)

        @block.tensor
        def _(t):
            t.wait_ge(ldx, 64)
            for i, (cbg, cb, lo, n, is_z) in enumerate(units):
                t.wait_ge(wlp[cbg % 2], 64 * (cbg // 2 + 1))
                if i >= 4:
                    t.wait_ge(evS, i - 3)
                for db in range(NDB):
                    mm = t.matmul(pin[i % 4][:, 0:n],
                                  wst[:, cbg % 2, db, cb * 128:(cb + 1) * 128],
                                  xlh[:, db, lo:lo + n],
                                  start=(db == 0), stop=(db == NDB - 1))
                mm.then_inc(mmS, 1)

        @block.scalar
        def _(s):
            for i, (cbg, cb, lo, n, is_z) in enumerate(units):
                par, cnt = od_sched[i]
                s.wait_ge(mmS, i + 1)
                if cnt >= 1:
                    s.wait_ge(odp[par], 16 * cnt)
                s.activation(stg[:, par, 0:n], pin[i % 4][:, 0:n],
                             AF.Silu if is_z else AF.Identity).then_inc(evS, 1)
    return nc


RMS_EPS1 = 1e-6


def build_l4b():
    nc = bass.Bass("TRN2", target_bir_lowering=False)
    xp_in = nc.dram_tensor("xpart", [DIN, THL], F32, kind="ExternalInput").ap()
    cw_in = nc.dram_tensor("convw_t", [128, 64, 4], F32, kind="ExternalInput").ap()
    cb_in = nc.dram_tensor("convb_t", [128, 64], F32, kind="ExternalInput").ap()
    xw_in = nc.dram_tensor("xprojw_t", [DIN, 288], F32, kind="ExternalInput").ap()
    lw_in = nc.dram_tensor("lnw_row", [1, 288], F32, kind="ExternalInput").ap()
    dw_in = nc.dram_tensor("dtprojw_t", [256, DIN], F32, kind="ExternalInput").ap()
    db_in = nc.dram_tensor("dtbias_t", [128, 64], F32, kind="ExternalInput").ap()
    idb_in = nc.dram_tensor("ident", [128, 128], BF16, kind="ExternalInput").ap()
    idf_in = nc.dram_tensor("identF", [128, 128], F32, kind="ExternalInput").ap()
    xc_out = nc.dram_tensor("xc", [DIN, TL], F32, kind="ExternalOutput").ap()
    dt_out = nc.dram_tensor("dt", [DIN, TL], F32, kind="ExternalOutput").ap()
    bc_out = nc.dram_tensor("BC", [128, 8, 32], F32, kind="ExternalOutput").ap()

    NDB2 = DIN // 128

    with ExitStack() as ctx:
        sb = lambda n, sh, dt: ctx.enter_context(nc.sbuf_tensor(n, sh, dt))
        ps = lambda n, sh, dt: ctx.enter_context(nc.psum_tensor(n, sh, dt))
        sem = lambda n: ctx.enter_context(nc.semaphore(n))
        xpc = sb("xpc", [128, 2, 515], F32)
        cc = sb("cc", [128, 2, 512], F32)
        xcc = sb("xcc", [128, 2, 512], F32)
        xcb = sb("xcb", [128, 2, 512], BF16)
        xprw = sb("xprw", [128, NDB2, 288], BF16)
        cwt = sb("cwt", [128, 64, 4], F32)
        cbt = sb("cbt", [128, 64], F32)
        ssp = sb("ssp", [96, 3, TL], F32)
        stok = sb("stok", [128, 8, 288], F32)
        sq2 = sb("sq2", [128, 8, 288], F32)
        rno = sb("rno", [128, 3, 8], F32)
        sss = sb("sss", [128, 3, 8], F32)
        lnwb = sb("lnwb", [128, 288], F32)
        dtrb = sb("dtrb", [128, 8, 256], BF16)
        dtrT = sb("dtrT", [128, 2, TL], BF16)
        dpw = sb("dpw", [128, 2, DIN], BF16)
        dbt = sb("dbt", [128, 64], F32)
        dtst = sb("dtst", [128, 2, 512], F32)
        etmp = sb("etmp", [128, 512], F32)
        cst = sb("cst", [128, 2], F32)
        idb = sb("idb", [128, 128], BF16)
        idf = sb("idf", [128, 128], F32)
        ps_ssm = [ps(f"pssm{i}", [96, 512], F32) for i in range(3)]
        ptr96 = ps("ptr96", [128, 96], F32)
        ptrd = ps("ptrd", [128, 1024], BF16)
        pdt = [ps(f"pdt{i}", [128, 512], F32) for i in range(2)]
        ldc = sem("ldc"); ldg = sem("ldg")
        ldp = [sem("ldp0"), sem("ldp1")]
        odc = [sem("odc0"), sem("odc1")]
        odd = [sem("odd0"), sem("odd1")]
        odbc = sem("odbc")
        cvV = sem("cvV"); slS = sem("slS"); cbV = sem("cbV"); mmX = sem("mmX")
        evM = sem("evM"); tr9 = sem("tr9"); ev9 = sem("ev9"); rmsV = sem("rmsV")
        nrmS = sem("nrmS"); dcV = sem("dcV"); trD = sem("trD"); evD = sem("evD")
        mmD = sem("mmD"); evE = sem("evE")
        ssS = SelfSem(sem("ssS")); ssV = SelfSem(sem("ssV"))
        block = ctx.enter_context(nc.Block())

        @block.sync
        def _(s):
            s.dma_start(out=idb[:], in_=idb_in).then_inc(ldc, 16)
            s.dma_start(out=idf[:], in_=idf_in).then_inc(ldc, 16)
            s.dma_start(out=cwt[:], in_=cw_in).then_inc(ldc, 16)
            s.dma_start(out=cbt[:], in_=cb_in).then_inc(ldc, 16)
            s.dma_start(out=dbt[:], in_=db_in).then_inc(ldc, 16)
            s.dma_start(out=lnwb[:], in_=lw_in.broadcast_to((128, 288))
                        ).then_inc(ldc, 16)
            for tch in range(2):
                for db in range(NDB2):
                    i = tch * NDB2 + db
                    if i >= 2:
                        s.wait_ge(cvV, i - 1)
                    s.dma_start(out=xpc[:, i % 2, :],
                                in_=xp_in[db * 128:(db + 1) * 128,
                                          tch * 512:tch * 512 + 515]
                                ).then_inc(ldp[i % 2], 16)
            s.wait_ge(odc[0], 16 * NDB2); s.wait_ge(odc[1], 16 * NDB2)
            s.wait_ge(odd[0], 16 * NDB2); s.wait_ge(odd[1], 16 * NDB2)
            s.wait_ge(odbc, 16)

        @block.gpsimd
        def _(g):
            g.dma_start(out=xprw[:], in_=xw_in.rearrange("(db p) e -> p db e", p=128)
                        ).then_inc(ldg, 16)
            g.dma_start(out=dpw[:], in_=dw_in.rearrange("(rb p) d -> p rb d", p=128)
                        ).then_inc(ldg, 16)
            for tch in range(2):
                for db in range(NDB2):
                    i = tch * NDB2 + db
                    g.wait_ge(slS, i + 1)
                    g.dma_start(out=xc_out[db * 128:(db + 1) * 128,
                                           tch * 512:(tch + 1) * 512],
                                in_=xcc[:, i % 2, :]).then_inc(odc[i % 2], 16)
            g.wait_ge(rmsV, 2)
            g.dma_start(out=bc_out, in_=stok[:, :, 256:288]).then_inc(odbc, 16)
            for db in range(NDB2):
                for tch in range(2):
                    i = db * 2 + tch
                    g.wait_ge(evE, i + 1)
                    g.dma_start(out=dt_out[db * 128:(db + 1) * 128,
                                           tch * 512:(tch + 1) * 512],
                                in_=dtst[:, i % 2, :]).then_inc(odd[i % 2], 16)

        @block.vector
        def _(v):
            ssV.mark(v.memset(cst[:, 0:1], RMS_EPS1))
            ssV.mark(v.memset(cst[:, 1:2], 1.0))
            ssV.wait(v)
            v.sem_inc(cbV, 1)
            v.wait_ge(ldc, 96)
            for tch in range(2):
                for db in range(NDB2):
                    i = tch * NDB2 + db
                    v.wait_ge(ldp[i % 2], 16 * (i // 2 + 1))
                    w = lambda k: cwt[:, db, k:k + 1]
                    xs = xpc[:, i % 2, :]
                    ssV.mark(v.tensor_scalar(cc[:, 0, :], xs[:, 0:512], w(0),
                                             cbt[:, db:db + 1], ALU.mult, ALU.add))
                    ssV.wait(v)
                    ssV.mark(v.scalar_tensor_tensor(cc[:, 1, :], xs[:, 1:513], w(1),
                                                    cc[:, 0, :], ALU.mult, ALU.add))
                    ssV.wait(v)
                    ssV.mark(v.scalar_tensor_tensor(cc[:, 0, :], xs[:, 2:514], w(2),
                                                    cc[:, 1, :], ALU.mult, ALU.add))
                    ssV.wait(v)
                    ssV.mark(v.scalar_tensor_tensor(cc[:, 1, :], xs[:, 3:515], w(3),
                                                    cc[:, 0, :], ALU.mult, ALU.add))
                    ssV.wait(v)
                    v.sem_inc(cvV, 1)
                    v.wait_ge(slS, i + 1)
                    if i >= 2:
                        v.wait_ge(mmX, 3 * (i - 1))
                    ssV.mark(v.tensor_copy(xcb[:, i % 2, :], xcc[:, i % 2, :]))
                    ssV.wait(v)
                    v.sem_inc(cbV, 1)
            for c in range(3):
                for tt in range(8):
                    j = c * 8 + tt
                    v.wait_ge(tr9, j + 1)
                    ssV.mark(v.tensor_copy(stok[:, tt, c * 96:(c + 1) * 96], ptr96[:]))
                    ssV.wait(v)
                    v.sem_inc(ev9, 1)
            v.wait_ge(nrmS, 1)
            ssV.mark(v.reduce_sum(sss[:, 0, :], sq2[:, :, 0:256], AX.X))
            ssV.mark(v.reduce_sum(sss[:, 1, :], sq2[:, :, 256:272], AX.X))
            ssV.mark(v.reduce_sum(sss[:, 2, :], sq2[:, :, 272:288], AX.X))
            ssV.wait(v)
            ssV.mark(v.tensor_scalar(sss[:, 0, :], sss[:, 0, :], 1.0 / 256, None, ALU.mult))
            ssV.mark(v.tensor_scalar(sss[:, 1, :], sss[:, 1, :], 1.0 / 16, None, ALU.mult))
            ssV.mark(v.tensor_scalar(sss[:, 2, :], sss[:, 2, :], 1.0 / 16, None, ALU.mult))
            ssV.wait(v)
            v.sem_inc(rmsV, 1)
            v.wait_ge(nrmS, 2)
            for tt in range(8):
                ssV.mark(v.tensor_scalar(stok[:, tt, 0:256], stok[:, tt, 0:256],
                                         rno[:, 0, tt:tt + 1], None, ALU.mult))
                ssV.mark(v.tensor_scalar(stok[:, tt, 256:272], stok[:, tt, 256:272],
                                         rno[:, 1, tt:tt + 1], None, ALU.mult))
                ssV.mark(v.tensor_scalar(stok[:, tt, 272:288], stok[:, tt, 272:288],
                                         rno[:, 2, tt:tt + 1], None, ALU.mult))
                ssV.wait(v)
                ssV.mark(v.tensor_tensor(stok[:, tt, :], stok[:, tt, :], lnwb[:],
                                         ALU.mult))
                ssV.wait(v)
            v.sem_inc(rmsV, 1)
            ssV.mark(v.tensor_copy(dtrb[:], stok[:, :, 0:256]))
            ssV.wait(v)
            v.sem_inc(dcV, 1)
            for rc in range(2):
                v.wait_ge(trD, rc + 1)
                ssV.mark(v.tensor_copy(dtrT[:, rc, :], ptrd[:]))
                ssV.wait(v)
                v.sem_inc(evD, 1)

        @block.scalar
        def _(s):
            s.wait_ge(ldc, 96)
            for tch in range(2):
                for db in range(NDB2):
                    i = tch * NDB2 + db
                    s.wait_ge(cvV, i + 1)
                    if i >= 2:
                        s.wait_ge(odc[i % 2], 16 * (i // 2))
                    s.activation(xcc[:, i % 2, :], cc[:, 1, :], AF.Silu
                                 ).then_inc(slS, 1)
                s.wait_ge(mmX, 3 * NDB2 * (tch + 1))
                for c in range(3):
                    s.copy(ssp[:, c, tch * 512:(tch + 1) * 512], ps_ssm[c][:]
                           ).then_inc(evM, 1)
            s.wait_ge(ev9, 24)
            ssS.mark(s.activation(sq2[:], stok[:], AF.Square))
            ssS.wait(s)
            s.sem_inc(nrmS, 1)
            s.wait_ge(rmsV, 1)
            for seg in range(3):
                ssS.mark(s.activation(rno[:, seg, :], sss[:, seg, :], AF.Ln,
                                      bias=cst[:, 0:1], scale=1.0))
                ssS.wait(s)
                ssS.mark(s.activation(rno[:, seg, :], rno[:, seg, :], AF.Exp,
                                      bias=0.0, scale=-0.5))
                ssS.wait(s)
            s.sem_inc(nrmS, 1)
            for db in range(NDB2):
                for tch in range(2):
                    i = db * 2 + tch
                    s.wait_ge(mmD, i + 1)
                    if i >= 1:
                        s.wait_ge(evE, i)
                    if i >= 2:
                        s.wait_ge(odd[i % 2], 16 * (i // 2))
                    ssS.mark(s.activation(etmp[:], pdt[i % 2][:], AF.Exp,
                                          bias=dbt[:, db:db + 1], scale=1.0))
                    ssS.wait(s)
                    s.activation(dtst[:, i % 2, :], etmp[:], AF.Ln,
                                 bias=cst[:, 1:2], scale=1.0).then_inc(evE, 1)

        @block.tensor
        def _(t):
            t.wait_ge(ldc, 96)
            t.wait_ge(ldg, 32)
            for tch in range(2):
                for db in range(NDB2):
                    i = tch * NDB2 + db
                    t.wait_ge(cbV, i + 2)
                    if tch == 1 and db == 0:
                        t.wait_ge(evM, 3)
                    for c in range(3):
                        mm = t.matmul(ps_ssm[c][:], xprw[:, db, c * 96:(c + 1) * 96],
                                      xcb[:, i % 2, :],
                                      start=(db == 0), stop=(db == NDB2 - 1))
                    mm.then_inc(mmX, 3)
            t.wait_ge(evM, 6)
            for c in range(3):
                for tt in range(8):
                    j = c * 8 + tt
                    if j >= 1:
                        t.wait_ge(ev9, j)
                    mm = t.transpose(ptr96[:], ssp[0:96, c, tt * 128:(tt + 1) * 128],
                                     idf[0:96, 0:96])
                    mm.then_inc(tr9, 1)
            t.wait_ge(dcV, 1)
            for rc in range(2):
                if rc >= 1:
                    t.wait_ge(evD, rc)
                for tt in range(8):
                    mm = t.transpose(ptrd[:, tt * 128:(tt + 1) * 128],
                                     dtrb[:, tt, rc * 128:(rc + 1) * 128], idb[:])
                mm.then_inc(trD, 1)
            t.wait_ge(evD, 2)
            for db in range(NDB2):
                for tch in range(2):
                    i = db * 2 + tch
                    if i >= 2:
                        t.wait_ge(evE, i - 1)
                    for rb in range(2):
                        mm = t.matmul(pdt[i % 2][:],
                                      dpw[:, rb, db * 128:(db + 1) * 128],
                                      dtrT[:, rb, tch * 512:(tch + 1) * 512],
                                      start=(rb == 0), stop=(rb == 1))
                    mm.then_inc(mmD, 1)
    return nc


def build_l5():
    nc = bass.Bass("TRN2", target_bir_lowering=False)
    xc_in = nc.dram_tensor("xc_d", [TL, T], F32, kind="ExternalInput").ap()
    dt_in = nc.dram_tensor("dt_d", [TL, T], F32, kind="ExternalInput").ap()
    sz_in = nc.dram_tensor("sz_d", [TL, T], F32, kind="ExternalInput").ap()
    bt_in = nc.dram_tensor("B_T", [16, T], F32, kind="ExternalInput").ap()
    ct_in = nc.dram_tensor("C_T", [16, T], F32, kind="ExternalInput").ap()
    al_in = nc.dram_tensor("Alog_t", [128, 8, 16], F32, kind="ExternalInput").ap()
    dp_in = nc.dram_tensor("D_t", [128, 8], F32, kind="ExternalInput").ap()
    idb_in = nc.dram_tensor("ident", [128, 128], BF16, kind="ExternalInput").ap()
    yg_out = nc.dram_tensor("yg", [TL, T], BF16, kind="ExternalOutput").ap()

    TC = 1024
    NCH = T // TC
    NN = 16

    with ExitStack() as ctx:
        sb = lambda n, sh, dt: ctx.enter_context(nc.sbuf_tensor(n, sh, dt))
        ps = lambda n, sh, dt: ctx.enter_context(nc.psum_tensor(n, sh, dt))
        sem = lambda n: ctx.enter_context(nc.semaphore(n))
        dtc = sb("dtc", [128, 8, TC], F32)
        xcc = sb("xcc", [128, 8, TC], F32)
        Brep = sb("Brep", [128, NN, TC], BF16)
        Crep = sb("Crep", [128, NN, TC], BF16)
        ub = sb("ub", [128, 8, TC], BF16)
        dAb = sb("dAb", [128, 2, TC], F32)
        dxb = sb("dxb", [128, 2, TC], BF16)
        hq = sb("hq", [128, 2, TC], BF16)
        chb = sb("chb", [128, 2, TC], BF16)
        carr = sb("carr", [128, 128, 2], BF16)
        szc = sb("szc", [128, 2, TC], F32)
        t1b = sb("t1b", [128, TC], F32)
        ygst = sb("ygst", [128, 2, TC], BF16)
        acol = sb("acol", [128, 8, 16], F32)
        dpt = sb("dpt", [128, 8], F32)
        idb = sb("idb", [128, 128], BF16)
        py = [ps(f"py{i}", [128, TC], F32) for i in range(2)]
        ldc = sem("ldc"); ldt = sem("ldt"); ldb = sem("ldb")
        lsz = [sem("lsz0"), sem("lsz1")]
        ody = [sem("ody0"), sem("ody1")]
        aS = sem("aS"); uD = sem("uD"); chV = sem("chV"); myU = sem("myU")
        gtV = sem("gtV"); upV = sem("upV"); acS = sem("acS")
        semP = sem("semP")
        ssS = SelfSem(sem("ssS")); ssV = SelfSem(sem("ssV"))
        block = ctx.enter_context(nc.Block())

        dt_r = dt_in.rearrange("(db p) t -> p db t", p=128)
        xc_r = xc_in.rearrange("(db p) t -> p db t", p=128)
        sz_r = sz_in.rearrange("(db p) t -> p db t", p=128)
        yg_r = yg_out.rearrange("(db p) t -> p db t", p=128)

        @block.sync
        def _(s):
            s.dma_start(out=idb[:], in_=idb_in).then_inc(ldc, 16)
            s.dma_start(out=acol[:], in_=al_in).then_inc(ldc, 16)
            s.dma_start(out=dpt[:], in_=dp_in).then_inc(ldc, 16)
            for tc in range(NCH):
                if tc >= 1:
                    s.wait_ge(aS, 128 * tc)
                    s.wait_ge(gtV, 8 * tc)
                    s.wait_ge(chV, 128 * tc)
                s.dma_start(out=dtc[:], in_=dt_r[:, :, tc * TC:(tc + 1) * TC]
                            ).then_inc(ldt, 16)
                s.dma_start(out=xcc[:], in_=xc_r[:, :, tc * TC:(tc + 1) * TC]
                            ).then_inc(ldt, 16)
                for db in range(8):
                    gidx = tc * 8 + db
                    if gidx >= 1:
                        pg = gidx - 1
                        s.wait_ge(gtV, pg + 1)
                        s.dma_start(out=yg_r[:, pg % 8, (pg // 8) * TC:(pg // 8 + 1) * TC],
                                    in_=ygst[:, pg % 2, :]).then_inc(ody[pg % 2], 16)
                    if gidx >= 2:
                        s.wait_ge(gtV, gidx - 1)
                    s.dma_start(out=szc[:, gidx % 2, :],
                                in_=sz_r[:, db, tc * TC:(tc + 1) * TC]
                                ).then_inc(lsz[gidx % 2], 16)
            pg = NCH * 8 - 1
            s.wait_ge(gtV, pg + 1)
            s.dma_start(out=yg_r[:, pg % 8, (pg // 8) * TC:(pg // 8 + 1) * TC],
                        in_=ygst[:, pg % 2, :]).then_inc(ody[pg % 2], 16)
            s.wait_ge(ody[0], 16 * 32)
            s.wait_ge(ody[1], 16 * 32)

        @block.gpsimd
        def _(g):
            for tc in range(NCH):
                if tc >= 1:
                    g.wait_ge(chV, 128 * tc)
                for n in range(NN):
                    g.dma_start(out=Brep[:, n, :],
                                in_=bt_in[n:n + 1, tc * TC:(tc + 1) * TC]
                                .broadcast_to((128, TC))).then_inc(ldb, 16)
                    g.dma_start(out=Crep[:, n, :],
                                in_=ct_in[n:n + 1, tc * TC:(tc + 1) * TC]
                                .broadcast_to((128, TC))).then_inc(ldb, 16)


        @block.scalar
        def _(s):
            s.wait_ge(ldc, 48)
            ssS.mark(s.activation(acol[:], acol[:], AF.Exp))
            ssS.wait(s)
            ssS.mark(s.mul(acol[:], acol[:], -1.0))
            ssS.wait(s)
            s.sem_inc(acS, 1)
            for tc in range(NCH):
                s.wait_ge(ldt, 32 * (tc + 1))
                for db in range(8):
                    for n in range(NN):
                        u = tc * 128 + db * 16 + n
                        if u >= 2:
                            s.wait_ge(chV, u - 1)
                        s.activation(dAb[:, u % 2, :], dtc[:, db, :], AF.Exp,
                                     bias=0.0, scale=acol[:, db, n:n + 1]
                                     ).then_inc(aS, 1)

        @block.vector
        def _(v):
            v.wait_ge(ldc, 48)
            for tc in range(NCH):
                cp = tc % 2
                v.wait_ge(ldt, 32 * (tc + 1))
                for db in range(8):
                    ssV.mark(v.tensor_tensor(ub[:, db, :], dtc[:, db, :],
                                             xcc[:, db, :], ALU.mult))
                    ssV.wait(v)
                v.wait_ge(ldb, 512 * (tc + 1))
                for db in range(8):
                    for n in range(NN):
                        u = tc * 128 + db * 16 + n
                        cidx = db * 16 + n
                        v.wait_ge(aS, u + 1)
                        ssV.mark(v.tensor_tensor(dxb[:, u % 2, :], ub[:, db, :],
                                                 Brep[:, n, :], ALU.mult))
                        ssV.wait(v)
                        if tc == 0:
                            init = 0.0
                        else:
                            pp = (tc - 1) % 2
                            init = carr[:, cidx:cidx + 1, pp:pp + 1]
                        ssV.mark(v.tensor_tensor_scan(hq[:, u % 2, :],
                                                      dAb[:, u % 2, :],
                                                      dxb[:, u % 2, :], init,
                                                      ALU.mult, ALU.add))
                        ssV.wait(v)
                        v.tensor_copy(carr[:, cidx:cidx + 1, (tc % 2):(tc % 2) + 1],
                                      hq[:, u % 2, TC - 1:TC])
                        if u >= 2:
                            v.wait_ge(myU, u - 1)
                        v.tensor_tensor(chb[:, u % 2, :], hq[:, u % 2, :],
                                        Crep[:, n, :], ALU.mult
                                        ).then_inc(chV, 1)
                        v.sem_inc(uD, 1)
                    gidx = tc * 8 + db
                    v.wait_ge(myU, tc * 128 + (db + 1) * 16)
                    v.wait_ge(lsz[gidx % 2], 16 * (gidx // 2 + 1))
                    if gidx >= 2:
                        v.wait_ge(ody[gidx % 2], 16 * (gidx // 2))
                    ssV.mark(v.scalar_tensor_tensor(t1b[:], xcc[:, db, :],
                                                    dpt[:, db:db + 1], py[db % 2][:],
                                                    ALU.mult, ALU.add))
                    ssV.wait(v)
                    ssV.mark(v.tensor_tensor(ygst[:, gidx % 2, :], t1b[:],
                                             szc[:, gidx % 2, :], ALU.mult))
                    ssV.wait(v)
                    v.sem_inc(gtV, 1)

        @block.tensor
        def _(t):
            t.wait_ge(ldc, 48)
            for tc in range(NCH):
                for db in range(8):
                    gidx = tc * 8 + db
                    if gidx >= 2:
                        t.wait_ge(gtV, gidx - 1)
                    for n in range(NN):
                        u = tc * 128 + db * 16 + n
                        t.wait_ge(chV, u + 1)
                        for c in range(2):
                            mm = t.matmul(py[db % 2][:, c * 512:(c + 1) * 512],
                                          idb[:], chb[:, u % 2, c * 512:(c + 1) * 512],
                                          start=(n == 0), stop=(n == NN - 1),
                                          skip_group_check=True)
                        mm.then_inc(myU, 1)
    return nc


def build_l6():
    nc = bass.Bass("TRN2", target_bir_lowering=False)
    yg_in = nc.dram_tensor("ygT", [DIN, TL], BF16, kind="ExternalInput").ap()
    ow_in = nc.dram_tensor("opw_q", [32, 128, 64, 128], F32, kind="ExternalInput").ap()
    xb_in = nc.dram_tensor("xb", [TL, D], F32, kind="ExternalInput").ap()
    w1_in = nc.dram_tensor("w1q", [32, 128, 32, 128], F32, kind="ExternalInput").ap()
    b1_in = nc.dram_tensor("b1_t", [128, 32], F32, kind="ExternalInput").ap()
    at_in = nc.dram_tensor("At", [D, 256], F32, kind="ExternalInput").ap()
    bf_in = nc.dram_tensor("Bft", [256, D], F32, kind="ExternalInput").ap()
    co_in = nc.dram_tensor("coeffs", [16, 1], F32, kind="ExternalInput").ap()
    sc_in = nc.dram_tensor("scales", [16, 1], F32, kind="ExternalInput").ap()
    idf_in = nc.dram_tensor("identF", [128, 128], F32, kind="ExternalInput").ap()
    idb_in = nc.dram_tensor("ident", [128, 128], BF16, kind="ExternalInput").ap()
    x3_out = nc.dram_tensor("x3", [TL, D], F32, kind="ExternalOutput").ap()

    NDB = 32
    NDB2 = 64
    NQ = 4
    w16d = nc.dram_tensor("w16d", [16, 1], F32)

    with ExitStack() as ctx:
        sb = lambda n, sh, dt: ctx.enter_context(nc.sbuf_tensor(n, sh, dt))
        ps = lambda n, sh, dt: ctx.enter_context(nc.psum_tensor(n, sh, dt))
        sem = lambda n: ctx.enter_context(nc.semaphore(n))
        ygb = sb("ygb", [128, NDB2, 256], BF16)
        wst = sb("wst", [128, 2, NDB2, 128], BF16)
        x2q = sb("x2q", [128, 2, D], F32)
        aotmp = sb("aotmp", [128, 256], F32)
        sq = sb("sq", [128, D], F32)
        stt = sb("stt", [128, 2, 8], F32)
        cst = sb("cst", [128, 1], F32)
        tokbf = sb("tokbf", [128, D], BF16)
        xl3T = sb("xl3T", [128, NDB, 256], BF16)
        w1s = sb("w1s", [128, 2, NDB, 128], BF16)
        atb = sb("atb", [128, NDB, 256], BF16)
        bftb = sb("bftb", [128, 2, D], BF16)
        b1t = sb("b1t", [128, 32], F32)
        w16 = sb("w16", [16, 2], F32)
        wcol = sb("wcol", [128, 2], F32)
        xab = sb("xab", [128, 2, 256], BF16)
        xaf = sb("xaf", [128, 2, 256], F32)
        f1tmp = sb("f1tmp", [128, 256], F32)
        idb = sb("idb", [128, 128], BF16)
        idf = sb("idf", [128, 128], F32)
        pmo = [ps(f"pmo{i}", [128, 256], F32) for i in range(2)]
        ptr = ps("ptr", [128, 256], F32)
        pb2 = ps("pb2", [128, 256], BF16)
        pm1 = [ps(f"pm1{i}", [128, 256], F32) for i in range(2)]
        pxa = [ps(f"pxa{i}", [128, 256], F32) for i in range(2)]
        ldc = sem("ldc"); ldg6 = sem("ldg6"); ldy = sem("ldy"); ldxq = sem("ldxq")
        wlo = [sem("wlo0"), sem("wlo1")]
        wl1 = [sem("wl10"), sem("wl11")]
        odx = [sem("odx0"), sem("odx1")]
        mmO = sem("mmO"); evO = sem("evO"); trO = sem("trO"); adV = sem("adV")
        cpS = sem("cpS"); cpV1 = sem("cpV1"); cpV = sem("cpV"); trX = sem("trX")
        evX = sem("evX"); wcV = sem("wcV"); wcP = sem("wcP"); wcD = sem("wcD")
        mmA = sem("mmA"); evA = sem("evA"); xabV = sem("xabV")
        mm1 = sem("mm1"); ev1 = sem("ev1")
        tr1 = sem("tr1"); ad1 = sem("ad1")
        ssS = SelfSem(sem("ssS")); ssV = SelfSem(sem("ssV"))
        block = ctx.enter_context(nc.Block())

        yg_r = yg_in.rearrange("(db p) t -> p db t", p=128)

        @block.sync
        def _(s):
            s.dma_start(out=idf[:], in_=idf_in).then_inc(ldc, 16)
            s.dma_start(out=idb[:], in_=idb_in).then_inc(ldc, 16)
            s.dma_start(out=b1t[:], in_=b1_in).then_inc(ldc, 16)
            s.dma_start(out=w16[:, 0:1], in_=co_in).then_inc(ldc, 16)
            s.dma_start(out=w16[:, 1:2], in_=sc_in).then_inc(ldc, 16)
            for q in range(NQ):
                if q >= 1:
                    s.wait_ge(ad1, 64 * q)
                    s.wait_ge(odx[(q - 1) % 2], 32 * ((q - 1) // 2 + 1))
                for i in range(2):
                    gt = q * 2 + i
                    s.dma_start(out=x2q[:, i, :],
                                in_=xb_in[gt * 128:(gt + 1) * 128, :]
                                ).then_inc(ldxq, 16)
                s.dma_start(out=ygb[:], in_=yg_r[:, :, q * 256:(q + 1) * 256]
                            ).then_inc(ldy, 16)
            s.wait_ge(odx[0], 32 * 2)
            s.wait_ge(odx[1], 32 * 2)

        @block.gpsimd
        def _(g):
            g.dma_start(out=atb[:], in_=at_in.rearrange("(db p) e -> p db e", p=128)
                        ).then_inc(ldg6, 16)
            g.dma_start(out=bftb[:], in_=bf_in.rearrange("(rb p) o -> p rb o", p=128)
                        ).then_inc(ldg6, 16)
            g.wait_ge(wcV, 1)
            g.dma_start(out=w16d.ap(), in_=w16[:, 0:1]).then_inc(wcD, 16)
            g.wait_ge(wcD, 16)
            for rb in range(2):
                g.dma_start(out=wcol[:, rb:rb + 1],
                            in_=w16d.ap()[rb * 8:(rb + 1) * 8, :]
                            .rearrange("e x -> e () x").broadcast_to((8, 16, 1))
                            ).then_inc(wcD, 16)
            for q in range(NQ):
                if q >= 1:
                    g.wait_ge(ad1, 64 * q)
                    for i in range(2):
                        gt = (q - 1) * 2 + i
                        g.dma_start(out=x3_out[gt * 128:(gt + 1) * 128, :],
                                    in_=x2q[:, i, :]).then_inc(odx[(q - 1) % 2], 16)
                for ocb in range(32):
                    k = q * 32 + ocb
                    if k >= 2:
                        g.wait_ge(mmO, (k - 1))
                    g.dma_start(out=wst[:, k % 2, :, :], in_=ow_in[ocb]
                                ).then_inc(wlo[k % 2], 16)
                for ocb in range(32):
                    k = q * 32 + ocb
                    if k >= 2:
                        g.wait_ge(mm1, (k - 1))
                    g.dma_start(out=w1s[:, k % 2, :, :], in_=w1_in[ocb]
                                ).then_inc(wl1[k % 2], 16)
            g.wait_ge(ad1, 64 * NQ)
            for i in range(2):
                gt = (NQ - 1) * 2 + i
                g.dma_start(out=x3_out[gt * 128:(gt + 1) * 128, :],
                            in_=x2q[:, i, :]).then_inc(odx[(NQ - 1) % 2], 16)

        @block.tensor
        def _(t):
            t.wait_ge(ldc, 80)
            t.wait_ge(ldg6, 32)
            for q in range(NQ):
                t.wait_ge(ldy, 16 * (q + 1))
                for ocb in range(32):
                    k = q * 32 + ocb
                    t.wait_ge(wlo[k % 2], 16 * (k // 2 + 1))
                    if k >= 2:
                        t.wait_ge(evO, k - 1)
                    for db in range(NDB2):
                        mm = t.matmul(pmo[k % 2][:], wst[:, k % 2, db, :],
                                      ygb[:, db, :],
                                      start=(db == 0), stop=(db == NDB2 - 1))
                    mm.then_inc(mmO, 1)
                    t.wait_ge(evO, k + 1)
                    if k >= 1:
                        t.wait_ge(adV, 2 * k)
                    for i in range(2):
                        mm = t.transpose(ptr[:, i * 128:(i + 1) * 128],
                                         aotmp[:, i * 128:(i + 1) * 128], idf[:])
                    mm.then_inc(trO, 1)
                t.wait_ge(adV, 64 * (q + 1))
                for i in range(2):
                    gt = q * 2 + i
                    t.wait_ge(cpV, gt + 1)
                    for kk in range(16):
                        pk = gt * 16 + kk
                        if pk >= 1:
                            t.wait_ge(evX, pk)
                        for j in range(2):
                            db = kk * 2 + j
                            mm = t.transpose(pb2[:, j * 128:(j + 1) * 128],
                                             tokbf[:, db * 128:(db + 1) * 128],
                                             idb[:])
                        mm.then_inc(trX, 1)
                t.wait_ge(evX, (q + 1) * 32)
                t.wait_ge(wcD, 48)
                if q >= 1:
                    t.wait_ge(evA, 2 * q)
                for rb in range(2):
                    for db in range(NDB):
                        mm = t.matmul(pxa[rb][:], atb[:, db, rb * 128:(rb + 1) * 128],
                                      xl3T[:, db, :],
                                      start=(db == 0), stop=(db == NDB - 1))
                    mm.then_inc(mmA, 1)
                t.wait_ge(xabV, q + 1)
                for ocb in range(32):
                    k = q * 32 + ocb
                    t.wait_ge(wl1[k % 2], 16 * (k // 2 + 1))
                    if k >= 2:
                        t.wait_ge(ev1, k - 1)
                    for db in range(NDB):
                        t.matmul(pm1[k % 2][:], w1s[:, k % 2, db, :],
                                 xl3T[:, db, :],
                                 start=(db == 0), stop=False)
                    for rb in range(2):
                        mm = t.matmul(pm1[k % 2][:], bftb[:, rb, ocb * 128:(ocb + 1) * 128],
                                      xab[:, rb, :],
                                      start=False, stop=(rb == 1))
                    mm.then_inc(mm1, 1)
                    t.wait_ge(ev1, k + 1)
                    if k >= 1:
                        t.wait_ge(ad1, 2 * k)
                    for i in range(2):
                        mm = t.transpose(ptr[:, i * 128:(i + 1) * 128],
                                         f1tmp[:, i * 128:(i + 1) * 128], idf[:])
                    mm.then_inc(tr1, 1)

        @block.scalar
        def _(s):
            for q in range(NQ):
                for ocb in range(32):
                    k = q * 32 + ocb
                    s.wait_ge(mmO, k + 1)
                    if k >= 1:
                        s.wait_ge(trO, k)
                    s.activation(aotmp[:], pmo[k % 2][:], AF.Identity
                                 ).then_inc(evO, 1)
                for i in range(2):
                    gt = q * 2 + i
                    s.wait_ge(adV, 64 * (q + 1) + 0)
                    s.wait_ge(cpV1, gt + 1)
                    b = stt[:, gt % 2, :]
                    ssS.mark(s.activation(sq[:], x2q[:, i, :], AF.Square,
                                          accum_out=b[:, 1:2]))
                    ssS.wait(s)
                    ssS.mark(s.mul(b[:, 2:3], b[:, 0:1], 1.0 / D))
                    ssS.mark(s.mul(b[:, 3:4], b[:, 1:2], 1.0 / D))
                    ssS.wait(s)
                    ssS.mark(s.activation(b[:, 4:5], b[:, 2:3], AF.Square))
                    ssS.wait(s)
                    ssS.mark(s.mul(b[:, 4:5], b[:, 4:5], -1.0))
                    ssS.wait(s)
                    ssS.mark(s.add(b[:, 4:5], b[:, 3:4], b[:, 4:5]))
                    ssS.wait(s)
                    ssS.mark(s.activation(b[:, 5:6], b[:, 4:5], AF.Ln,
                                          bias=cst[:, 0:1], scale=1.0))
                    ssS.wait(s)
                    s.activation(b[:, 6:7], b[:, 5:6], AF.Exp, bias=0.0, scale=-0.5
                                 ).then_inc(cpS, 1)
                for rb in range(2):
                    s.wait_ge(mmA, 2 * q + rb + 1)
                    s.copy(xaf[:, rb, :], pxa[rb][:]).then_inc(evA, 1)
                for ocb in range(32):
                    k = q * 32 + ocb
                    s.wait_ge(mm1, k + 1)
                    if k >= 1:
                        s.wait_ge(tr1, k)
                    s.activation(f1tmp[:], pm1[k % 2][:], AF.Identity,
                                 bias=b1t[:, ocb:ocb + 1], scale=1.0
                                 ).then_inc(ev1, 1)

        @block.vector
        def _(v):
            v.memset(cst[:], LN_EPS)
            v.wait_ge(ldc, 80)
            ssV.mark(v.tensor_tensor(w16[:, 0:1], w16[:, 0:1], w16[:, 1:2],
                                     ALU.mult))
            ssV.wait(v)
            v.sem_inc(wcV, 1)
            for q in range(NQ):
                for ocb in range(32):
                    k = q * 32 + ocb
                    v.wait_ge(trO, k + 1)
                    v.wait_ge(ldxq, 32 * (q + 1))
                    for i in range(2):
                        vv = v.tensor_tensor(x2q[:, i, ocb * 128:(ocb + 1) * 128],
                                             x2q[:, i, ocb * 128:(ocb + 1) * 128],
                                             ptr[:, i * 128:(i + 1) * 128], ALU.add)
                    vv.then_inc(adV, 2)
                for i in range(2):
                    gt = q * 2 + i
                    v.wait_ge(adV, 64 * (q + 1))
                    b = stt[:, gt % 2, :]
                    ssV.mark(v.reduce_sum(b[:, 0:1], x2q[:, i, :], AX.X))
                    ssV.wait(v)
                    v.sem_inc(cpV1, 1)
                    v.wait_ge(cpS, gt + 1)
                    ssV.mark(v.tensor_tensor(b[:, 7:8], b[:, 2:3], b[:, 6:7],
                                             ALU.mult))
                    ssV.wait(v)
                    v.tensor_scalar(tokbf[:], x2q[:, i, :], b[:, 6:7], b[:, 7:8],
                                    ALU.mult, ALU.subtract).then_inc(cpV, 1)
                    for kk in range(16):
                        pk = gt * 16 + kk
                        v.wait_ge(trX, pk + 1)
                        ssV.mark(v.tensor_copy(
                            xl3T[:, kk * 2:(kk + 1) * 2, i * 128:(i + 1) * 128],
                            pb2[:].rearrange("p (a b) -> p a b", a=2)))
                        ssV.wait(v)
                        v.sem_inc(evX, 1)
                v.wait_ge(evA, 2 * (q + 1))
                v.wait_ge(wcD, 48)
                for rb in range(2):
                    ssV.mark(v.tensor_scalar(xab[:, rb, :], xaf[:, rb, :],
                                             wcol[:, rb:rb + 1], None, ALU.mult))
                    ssV.wait(v)
                v.sem_inc(xabV, 1)
                for ocb in range(32):
                    k = q * 32 + ocb
                    v.wait_ge(tr1, k + 1)
                    for i in range(2):
                        vv = v.tensor_tensor(x2q[:, i, ocb * 128:(ocb + 1) * 128],
                                             x2q[:, i, ocb * 128:(ocb + 1) * 128],
                                             ptr[:, i * 128:(i + 1) * 128], ALU.add)
                    vv.then_inc(ad1, 2)
    return nc


def _ident_b():
    return np.eye(128, dtype=BF)


def _ident_f():
    return np.eye(128, dtype=np.float32)


def kernel(**inputs):
    cores = list(range(C))

    r1 = run_bass_kernel_spmd(build_l1(), host_l1_inputs(inputs), core_ids=cores)
    l1_outs = [r["qkvT"] for r in r1.results]

    r2 = run_bass_kernel_spmd(build_l2(), host_l2_inputs(l1_outs), core_ids=cores)
    ctx = np.zeros((T, D), dtype=BF)
    for c in range(C):
        o = r2.results[c]["ctxq"]
        for j in range(4):
            ctx[:, (c * 4 + j) * HD:(c * 4 + j + 1) * HD] = o[j]

    r3a = run_bass_kernel_spmd(build_l3a(), host_l3a_inputs(inputs, ctx),
                               core_ids=cores)
    x1_full = np.concatenate([r["x1"] for r in r3a.results], 0)
    ln1_full = np.concatenate([r["xln1"] for r in r3a.results], 0)

    r3b = run_bass_kernel_spmd(build_l3b(), host_l3b_inputs(inputs, x1_full, ln1_full),
                               core_ids=cores)
    xb_full = np.concatenate([r["xb"] for r in r3b.results], 0)
    ln2T_full = np.concatenate([r["xln2T"] for r in r3b.results], 1)

    inw = inputs["L1_attn_in_proj_w"]
    maps4a = [{"xlh": np.ascontiguousarray(ln2T_full[:, c * TL:(c + 1) * TL]),
               "inw": inw} for c in range(C)]
    r4a = run_bass_kernel_spmd(build_l4a(), maps4a, core_ids=cores)

    cw = np.ascontiguousarray(
        inputs['L1_attn_conv_w'].reshape(64, 128, 4).transpose(1, 0, 2))
    cb = np.ascontiguousarray(inputs['L1_attn_conv_b'].reshape(64, 128).T)
    dbias = np.ascontiguousarray(inputs['L1_attn_dt_bias'].reshape(64, 128).T)
    lnw = np.ascontiguousarray(np.concatenate(
        [inputs['L1_attn_dt_ln_w'], inputs['L1_attn_B_ln_w'],
         inputs['L1_attn_C_ln_w']]).reshape(1, 288))
    xprot = np.ascontiguousarray(inputs['L1_attn_x_proj_w'].T)
    dtprot = np.ascontiguousarray(inputs['L1_attn_dt_proj_w'].T)
    def xp_halo(c):
        xp = np.empty((DIN, THL), np.float32)
        if c == 0:
            xp[:, 0:3] = 0.0
        else:
            xp[:, 0:3] = r4a.results[c - 1]["xpart"][:, TL - 3:TL]
        xp[:, 3:] = r4a.results[c]["xpart"]
        return xp

    maps4b = [{"xpart": xp_halo(c), "convw_t": cw, "convb_t": cb,
               "xprojw_t": xprot, "lnw_row": lnw, "dtprojw_t": dtprot,
               "dtbias_t": dbias, "ident": _ident_b(), "identF": _ident_f()}
              for c in range(C)]
    r4b = run_bass_kernel_spmd(build_l4b(), maps4b, core_ids=cores)

    xc_full = np.concatenate([r4b.results[c]["xc"] for c in range(C)], 1)
    dt_full = np.concatenate([r4b.results[c]["dt"] for c in range(C)], 1)
    sz_full = np.concatenate([r4a.results[c]["sz"] for c in range(C)], 1)
    B_full = np.concatenate(
        [r4b.results[c]["BC"][:, :, 0:16].transpose(1, 0, 2).reshape(TL, 16)
         for c in range(C)], 0)
    C_full = np.concatenate(
        [r4b.results[c]["BC"][:, :, 16:32].transpose(1, 0, 2).reshape(TL, 16)
         for c in range(C)], 0)
    B_T = np.ascontiguousarray(B_full.T)
    C_T = np.ascontiguousarray(C_full.T)
    Alog = inputs['L1_attn_A_log']
    Dp = inputs['L1_attn_D_param']
    maps5 = []
    for c in range(C):
        sl = slice(c * TL, (c + 1) * TL)
        maps5.append({
            "xc_d": np.ascontiguousarray(xc_full[sl]),
            "dt_d": np.ascontiguousarray(dt_full[sl]),
            "sz_d": np.ascontiguousarray(sz_full[sl]),
            "B_T": B_T, "C_T": C_T,
            "Alog_t": np.ascontiguousarray(
                Alog[sl].reshape(8, 128, 16).transpose(1, 0, 2)),
            "D_t": np.ascontiguousarray(Dp[sl].reshape(8, 128).T),
            "ident": _ident_b()})
    r5 = run_bass_kernel_spmd(build_l5(), maps5, core_ids=cores)

    yg_full = np.concatenate([r5.results[c]["yg"] for c in range(C)], 0)
    opw_q = np.ascontiguousarray(
        inputs['L1_attn_out_proj_w'].T.reshape(64, 128, 32, 128).transpose(2, 1, 0, 3))
    w1q = np.ascontiguousarray(
        inputs['L1_ffn_W'].T.reshape(32, 128, 32, 128).transpose(2, 1, 0, 3))
    b1t = np.ascontiguousarray(inputs['L1_ffn_bias'].reshape(32, 128).T)
    At = np.ascontiguousarray(inputs['L1_ffn_A'].reshape(256, D).T)
    Bft = np.ascontiguousarray(inputs['L1_ffn_Bf'].transpose(0, 2, 1).reshape(256, D))
    maps6 = []
    for c in range(C):
        sl = slice(c * TL, (c + 1) * TL)
        maps6.append({
            "ygT": np.ascontiguousarray(yg_full[:, sl]),
            "opw_q": opw_q, "xb": np.ascontiguousarray(xb_full[sl]),
            "w1q": w1q, "b1_t": b1t, "At": At, "Bft": Bft,
            "coeffs": np.ascontiguousarray(inputs['L1_ffn_coeffs'].reshape(16, 1)),
            "scales": np.ascontiguousarray(inputs['L1_ffn_scales'].reshape(16, 1)),
            "identF": _ident_f(), "ident": _ident_b()})
    r6 = run_bass_kernel_spmd(build_l6(), maps6, core_ids=cores)
    x3 = np.concatenate([r["x3"] for r in r6.results], 0)
    return np.ascontiguousarray(x3.reshape(1, T, D)).astype(np.float32)
